# version 20
# baseline (speedup 1.0000x reference)
import contextlib
import numpy as np
import concourse.bass as bass
import concourse.mybir as mybir
from concourse.bass_utils import run_bass_kernel_spmd

F32 = mybir.dt.float32
AF = mybir.ActivationFunctionType
ALU = mybir.AluOpType

LN2 = float(np.log(2.0))
INV_LN2 = float(1.0 / np.log(2.0))

T = 2048
D = 1024
NCH = 8
H = 16
HD = 64
DZ = 16384
EPS = 1e-5

ENGS = ("sync", "gpsimd", "vector", "scalar", "tensor")


class Plan:
    def __init__(self, nc, sems):
        self.nc = nc
        self.sems = sems
        self.cnt = {k: 0 for k in sems}
        self.q = {e: [] for e in ENGS}
        self.nbar = 0

    AUTO_SER = {"vector": "vv", "scalar": "aa", "gpsimd": "gg"}

    def op(self, eng, fn, wait=(), inc=None, n=1):
        ser = self.AUTO_SER.get(eng)
        if ser is not None:
            wait = tuple(wait) + ((ser, self.cnt[ser]),)
            if inc is None:
                inc = ser
                n = 1
        wait = tuple((s, v) for (s, v) in wait if v and v > 0)
        if inc is not None:
            self.cnt[inc] += n
        sems = self.sems

        def run(h, fn=fn, wait=wait, inc=inc, n=n):
            for s, v in wait:
                h.wait_ge(sems[s], v)
            inst = fn(h)
            if inc is not None:
                inst.then_inc(sems[inc], n)
        self.q[eng].append(run)
        return self.cnt[inc] if inc is not None else None

    def raw(self, eng, fn):
        self.q[eng].append(fn)

    def fence(self, eng, sem):
        v = self.cnt[sem]
        if v:
            sems = self.sems
            self.raw(eng, lambda h, v=v, s=sem: h.wait_ge(sems[s], v))

    def bar(self):
        self.nbar += 1
        tgt = self.nbar * 5
        ld_t, st_t = self.cnt["ld"], self.cnt["st"]
        ld3_t, ld4_t = self.cnt["ld3"], self.cnt["ld4"]
        ld2_t = self.cnt["ld2"]
        ld5_t = self.cnt["ld5"]
        st2_t = self.cnt["st2"]
        st3_t = self.cnt["st3"]
        st4_t = self.cnt["st4"]
        sems = self.sems
        for e in ENGS:
            def run(h, e=e, tgt=tgt, ld_t=ld_t, st_t=st_t,
                    ld3_t=ld3_t, ld4_t=ld4_t, ld2_t=ld2_t, st2_t=st2_t,
                    st3_t=st3_t, st4_t=st4_t, ld5_t=ld5_t):
                if e == "sync":
                    if ld_t:
                        h.wait_ge(sems["ld"], ld_t)
                    if ld2_t:
                        h.wait_ge(sems["ld2"], ld2_t)
                    if ld5_t:
                        h.wait_ge(sems["ld5"], ld5_t)
                if e == "gpsimd":
                    if ld3_t:
                        h.wait_ge(sems["ld3"], ld3_t)
                    if ld4_t:
                        h.wait_ge(sems["ld4"], ld4_t)
                if e == "gpsimd":
                    if st_t:
                        h.wait_ge(sems["st"], st_t)
                    if st2_t:
                        h.wait_ge(sems["st2"], st2_t)
                    if st3_t:
                        h.wait_ge(sems["st3"], st3_t)
                    if st4_t:
                        h.wait_ge(sems["st4"], st4_t)
                h.nop().then_inc(sems["bar"], 1)
                h.wait_ge(sems["bar"], tgt)
            self.q[e].append(run)

    def load(self, dst, src, wait=()):
        return self.op("sync", lambda h, d=dst, s=src: h.dma_start(out=d, in_=s),
                       wait=wait, inc="ld", n=16)

    def store(self, dst, src, wait=(), sem="st"):
        return self.op("gpsimd", lambda h, d=dst, s=src: h.dma_start(out=d, in_=s),
                       wait=wait, inc=sem, n=16)

    def mm(self, out, lhsT, rhs, start, stop, wait=(), inc=None, tile_position=None):
        return self.op(
            "tensor",
            lambda h, o=out, l=lhsT, r=rhs, a=start, b=stop, tp=tile_position:
                h.matmul(o, l, r, start=a, stop=b, tile_position=tp),
            wait=wait, inc=inc)

    def trans(self, out, in_, ident, wait=(), inc=None):
        return self.op(
            "tensor",
            lambda h, o=out, i=in_, d=ident: h.transpose(o, i, d),
            wait=wait, inc=inc)

    def act(self, out, in_, func, bias=0.0, scale=1.0, wait=(), inc=None):
        return self.op(
            "scalar",
            lambda h, o=out, i=in_, f=func, b=bias, s=scale:
                h.activation(o, i, f, bias=b, scale=s),
            wait=wait, inc=inc)

    def v(self, fn, wait=(), inc=None):
        return self.op("vector", fn, wait=wait, inc=inc)


def build(nc=None, debug=(), upto=8):
    if nc is None:
        nc = bass.Bass("TRN2", target_bir_lowering=False,
                       dynamic_dma_scratch_size=16384)

    def dk(name):
        return "ExternalOutput" if name in debug else "Internal"

    inp = {}
    names = [("L0_attn_x", [T, D])]
    for L in (0, 1):
        for w in ("Wq", "Wk", "Wv", "Wo"):
            names.append((f"L{L}_attn_{w}", [D, D]))
        for b in ("bq", "bk", "bv", "bo"):
            names.append((f"L{L}_attn_{b}", [D]))
    names += [("L0_ffn_rail_maker", [4, 64, 16, 16]),
              ("L0_ffn_rail_up_proj", [4, 64, 64, 16]),
              ("L0_ffn_w_cproj", [D, DZ]),
              ("L1_ffn_weight_fp8", [D, D]),
              ("L1_ffn_bias", [D]),
              ("L1_ffn_input_scale", [1]),
              ("L1_ffn_weight_scale", [1]),
              ("c_ident", [128, 128]),
              ("c_alibi", [2, 17 * T]),
              ("c_misc", [128, 8])]
    for nm, shp in names:
        inp[nm] = nc.dram_tensor(nm, shp, F32, kind="ExternalInput").ap()

    out = nc.dram_tensor("out", [T, D], F32, kind="ExternalOutput").ap()

    xs_scr = [nc.dram_tensor(f"x{i}_scr", [NCH, 128, T], F32,
                             kind=dk(f"x{i}_scr")).ap() for i in range(4)]
    q_scr = nc.dram_tensor("q_scr", [NCH, 128, T], F32, kind=dk("q_scr")).ap()
    k_scr = nc.dram_tensor("k_scr", [NCH, 128, T], F32, kind=dk("k_scr")).ap()
    v_scr = nc.dram_tensor("v_scr", [H, 128, 16 * 66], F32, kind=dk("v_scr")).ap()
    v0_scr = nc.dram_tensor("v0_scr", [16, 128, D], F32, kind=dk("v0_scr")).ap()
    attn_scr = nc.dram_tensor("attn_scr", [NCH, 128, T], F32,
                              kind=dk("attn_scr")).ap()
    wcT_scr = nc.dram_tensor("wcT_scr", [128, NCH, 128, 128], F32,
                             kind=dk("wcT_scr")).ap()
    rails_scr = nc.dram_tensor("rails_scr", [128, 8192], F32,
                               kind=dk("rails_scr")).ap()
    wT_scr = nc.dram_tensor("wT_scr", [5, 128, 8192], F32, kind=dk("wT_scr")).ap()

    def sbuf(name, shape):
        return nc.alloc_sbuf_tensor(name, shape, F32).ap()

    AR_W = sbuf("AR_W", [128, 24576])
    AR_A = sbuf("AR_A", [128, 10240])
    AR_B = sbuf("AR_B", [128, 8192])
    ident = sbuf("ident", [128, 128])
    ones = sbuf("ones", [1, 128])
    ones128 = sbuf("ones128", [128, 1])
    VEC = sbuf("VEC", [128, 8256])
    Avec = VEC[0:1, 0:2048]
    Bvec = VEC[0:1, 2048:4096]
    Svec = VEC[0:1, 4096:6144]
    S2vec = VEC[0:1, 6144:8192]
    rsum = VEC[0:1, 4096:5120]
    scal1 = VEC[0:1, 8192:8196]
    bias_sb = sbuf("bias_sb", [128, 9 * 8])
    cmisc = sbuf("cmisc", [128, 8])
    invis = sbuf("invis", [128, 1])
    iscol = sbuf("iscol", [128, 1])
    wscol = sbuf("wscol", [128, 1])

    BIAS_IDX = {f"L{L}_attn_{b}": 8 * (i + 4 * L)
                for L in (0, 1) for i, b in enumerate(("bq", "bk", "bv", "bo"))}
    BIAS_IDX["L1_ffn_bias"] = 64

    es = contextlib.ExitStack()
    pall = es.enter_context(nc.psum_tensor("pall", [128, 4096], F32)).ap()
    sems = {s: es.enter_context(nc.semaphore(s))
            for s in ("ld", "ld2", "ld3", "ld4", "ld5", "st", "st2", "st3",
                      "st4", "mm", "vv", "aa", "gg", "bar")}
    P = Plan(nc, sems)

    def bank(i, ncols=512, row0=0, nrow=128):
        return pall[row0:row0 + nrow, 512 * i: 512 * i + ncols]

    P.load(ident[:], inp["c_ident"][:])
    P.load(cmisc[:], inp["c_misc"][:])
    P.load(scal1[0:1, 0:1], inp["L1_ffn_input_scale"][None, :])
    P.load(scal1[0:1, 1:2], inp["L1_ffn_weight_scale"][None, :])
    for nm, off in BIAS_IDX.items():
        P.op("sync", lambda h, d=bias_sb[:, off:off + 8],
             s=inp[nm].rearrange("(c p) -> p c", p=128):
             h.dma_start(out=d, in_=s, allow_slow_non_contiguous=True),
             inc="ld", n=16)
    ld0 = P.cnt["ld"]
    P.v(lambda h: h.memset(ones[:], 1.0))
    P.v(lambda h: h.memset(ones128[:], 1.0), inc="vv")
    P.v(lambda h: h.reciprocal(scal1[0:1, 2:3], scal1[0:1, 0:1]),
        wait=[("ld", ld0)], inc="vv")
    vv0 = P.cnt["vv"]
    P.mm(pall[0:128, 0:1], ones[:], scal1[0:1, 2:3], True, True,
         wait=[("vv", vv0)])
    P.mm(pall[0:128, 512:513], ones[:], scal1[0:1, 1:2], True, True)
    P.mm(pall[0:128, 1024:1025], ones[:], scal1[0:1, 0:1], True, True, inc="mm")
    mm0 = P.cnt["mm"]
    P.v(lambda h: h.tensor_copy(invis[:], pall[0:128, 0:1]), wait=[("mm", mm0)])
    P.v(lambda h: h.tensor_copy(wscol[:], pall[0:128, 512:513]))
    P.v(lambda h: h.tensor_copy(iscol[:], pall[0:128, 1024:1025]), inc="vv")

    rm_pack = AR_B[:, 0:4096]
    rup_packT = AR_B[:, 4096:8192]
    P.v(lambda h: h.memset(rm_pack, 0.0))
    P.v(lambda h: h.memset(rup_packT, 0.0), inc="vv")
    vz = P.cnt["vv"]
    rmf = inp["L0_ffn_rail_maker"].rearrange("h r i j -> (h r i) j")
    rupf = inp["L0_ffn_rail_up_proj"].rearrange("h r k j -> (h r) k j")
    rup_in = AR_A[:, 4096:4224]
    rup_stage = AR_A[:, 4224:4288]

    def emit_rails_group(g8):
        src = rupf[8 * g8:8 * g8 + 8]
        ldv = P.op("gpsimd", lambda h, s=src, d=rup_in:
                   h.dma_start(out=d[0:64, :].rearrange("k (u j) -> k u j", j=16),
                               in_=s.transpose([1, 0, 2])),
                   wait=[("mm", P.cnt["mm"]), ("vv", vz)], inc="ld3", n=16)
        mv = P.trans(pall[0:128, 2048:2112], rup_in[0:64, :], ident[0:64, 0:64],
                     wait=[("ld3", ldv), ("st", P.cnt["st"])], inc="mm")
        ev = P.v(lambda h: h.tensor_copy(rup_stage, pall[0:128, 2048:2112]),
                 wait=[("mm", mv), ("st", P.cnt["st"])], inc="vv")
        for u in range(8):
            cdst = 128 * g8 + 64 * (u % 2)
            P.store(rup_packT[16 * u:16 * u + 16, cdst:cdst + 64],
                    rup_stage[16 * u:16 * u + 16, :],
                    wait=[("vv", ev), ("ld3", P.cnt["ld3"])] if u == 0 else ())

    xin_t = AR_A[:, 4352:5376]
    fm_t = AR_A[:, 5376:6400]
    sq_t0 = AR_A[:, 6400:7424]
    p0_state = {"mm_sq_prev": 0, "st2_prev": 0, "scp": 0}

    def emit_p0_tt(tt):
        lv = P.op("sync", lambda h, d=xin_t,
                  s_=inp["L0_attn_x"][128 * tt:128 * tt + 128, :]:
                  h.dma_start(out=d, in_=s_),
                  wait=[("mm", P.cnt["mm"])], inc="ld4", n=16)
        for ch in range(NCH):
            mv = P.trans(pall[:, 2560:2688],
                         xin_t[:, 128 * ch:128 * ch + 128],
                         ident[:], wait=[("ld4", lv), ("vv", P.cnt["vv"])],
                         inc="mm")
            ev = P.v(lambda h, ch=ch: h.tensor_copy(
                fm_t[:, 128 * ch:128 * ch + 128], pall[:, 2560:2688]),
                wait=[("mm", mv), ("st3", p0_state["st2_prev"])]
                if ch == 0 else [("mm", mv)], inc="vv")
            sv = P.act(sq_t0[:, 128 * ch:128 * ch + 128],
                       fm_t[:, 128 * ch:128 * ch + 128], AF.Square,
                       wait=[("vv", ev), ("mm", p0_state["mm_sq_prev"])],
                       inc="aa")
            c7a = 3072 + 128 * (tt % 2)
            c7b = 3584 + 128 * (tt % 2)
            P.mm(pall[0:1, c7a:c7a + 128], ones128[:],
                 fm_t[:, 128 * ch:128 * ch + 128], ch == 0, ch == NCH - 1,
                 wait=[("vv", ev), ("vv", p0_state["scp"])] if ch == 0
                 else [("vv", ev)])
            P.mm(pall[0:1, c7b:c7b + 128], ones128[:],
                 sq_t0[:, 128 * ch:128 * ch + 128], ch == 0, ch == NCH - 1,
                 wait=[("aa", sv)], inc="mm" if ch == NCH - 1 else None)
        p0_state["st2_prev"] = P.store(
            xs_scr[0].transpose([1, 0, 2])[:, :, 128 * tt:128 * tt + 128],
            fm_t.rearrange("p (c t) -> p c t", t=128),
            wait=[("vv", P.cnt["vv"])], sem="st3")
        mvs = P.cnt["mm"]
        P.v(lambda h, tt=tt, c7a=3072 + 128 * (tt % 2): h.tensor_copy(
            Svec[0:1, 128 * tt:128 * tt + 128], pall[0:1, c7a:c7a + 128]),
            wait=[("mm", mvs)])
        P.v(lambda h, tt=tt, c7b=3584 + 128 * (tt % 2): h.tensor_copy(
            S2vec[0:1, 128 * tt:128 * tt + 128], pall[0:1, c7b:c7b + 128]),
            inc="vv")
        p0_state["mm_sq_prev"] = P.cnt["mm"]
        p0_state["scp"] = P.cnt["vv"]

    wc = inp["L0_ffn_w_cproj"]
    wc_stages = [AR_W[:, 0:8192], AR_W[:, 8192:16384], AR_W[:, 16384:24576]]
    wcT_tiles = [AR_A[:, 0:2048], AR_A[:, 2048:4096], AR_A[:, 7424:9472]]
    wcT_sch = ("st", "st2", "st4")
    st_cgrp = {}
    lv_h = {}
    mm_h = {}
    for oh in range(16):
        o, hf = oh // 2, oh % 2
        sem_o = ("ld", "ld2", "ld5")[oh % 3]
        if oh < 3:
            P.fence("sync", sem_o)
        stg = wc_stages[oh % 3]
        lv = 0
        for c4 in range(8):
            lv = P.op("sync", lambda h, d=stg[:, 1024 * c4:1024 * c4 + 1024],
                      s_=wc[128 * o:128 * o + 128,
                            8192 * hf + 1024 * c4:8192 * hf + 1024 * c4 + 1024]:
                      h.dma_start(out=d, in_=s_),
                      wait=[("mm", mm_h.get(oh - 3, 0))] if c4 == 0 else (),
                      inc=sem_o, n=16)
        lv_h[oh] = lv
        for cgrp in range(4):
            g = 4 * oh + cgrp
            wt = wcT_tiles[g % 3]
            sch = wcT_sch[g % 3]
            for q4 in range(4):
                mv = None
                for t4 in range(4):
                    t16 = 4 * q4 + t4
                    pb = (2 * (g % 2)) + (q4 % 2)
                    mv = P.trans(pall[:, 512 * pb + 128 * t4:
                                      512 * pb + 128 * t4 + 128],
                                 stg[:, 128 * (16 * cgrp + t16):
                                     128 * (16 * cgrp + t16) + 128], ident[:],
                                 wait=[(sem_o, lv), ("vv", max(0, P.cnt["vv"] - 1))]
                                 if t4 == 0 else (), inc="mm" if t4 == 3 else None)
                mv = P.cnt["mm"]
                P.v(lambda h, q4=q4, wt=wt, pb=(2 * (g % 2)) + (q4 % 2):
                    h.tensor_copy(wt[:, 512 * q4:512 * q4 + 512],
                                  pall[:, 512 * pb:512 * pb + 512]),
                    wait=[("mm", mv), (sch, st_cgrp.get(g - 3, 0))], inc="vv")
            sv = 0
            for e2 in range(2):
                sv = P.store(
                    wcT_scr[64 * hf + 16 * cgrp + 8 * e2:
                            64 * hf + 16 * cgrp + 8 * e2 + 8, o]
                    .transpose([1, 0, 2]),
                    wt[:, 1024 * e2:1024 * e2 + 1024]
                    .rearrange("p (x d) -> p x d", d=128),
                    wait=[("vv", P.cnt["vv"]),
                          (sch, st_cgrp.get(g - 3, 0))] if e2 == 0 else (),
                    sem=sch)
            st_cgrp[g] = sv
        mm_h[oh] = P.cnt["mm"]
        emit_rails_group(2 * oh)
        emit_rails_group(2 * oh + 1)
        emit_p0_tt(oh)
    for l in range(8):
        P.op("gpsimd", lambda h, l=l,
             d=rm_pack[16 * l:16 * l + 16, :]
             .rearrange("p (pk b) -> p pk b", b=128)[:, :, 16 * l:16 * l + 16],
             s_=rmf.rearrange("(pk l p) j -> l p pk j", pk=32, l=8, p=16)[l]:
             h.dma_start(out=d, in_=s_),
             wait=[("vv", vz)] if l == 0 else (),
             inc="ld3", n=16)
    P.fence("gpsimd", "ld3")
    P.store(rails_scr[:, 0:4096], rm_pack, wait=[("ld3", P.cnt["ld3"])])
    P.store(rails_scr[:, 4096:8192], rup_packT, wait=[("st", P.cnt["st"])])
    P.bar()

    WT_IDX = {"L1_attn_Wq": 0, "L1_attn_Wk": 1, "L1_attn_Wv": 2,
              "L1_attn_Wo": 3, "L1_ffn_weight_fp8": 4}
    wtr_tmp4 = [AR_B[:, 2048 * j:2048 * j + 2048] for j in range(4)]
    wtr_dst4 = [AR_W[:, 16384 + 2048 * j:16384 + 2048 * j + 2048]
                for j in range(4)]
    wtr_lch = ("ld", "ld2", "ld5", "ld4")
    wtr_sch = ("st", "st2", "st4", "st3")
    tr_done = {}
    stw_done = {}
    qn = 0
    for wname, wi in WT_IDX.items():
        for qq in range(4):
            par = qn % 4
            lch = wtr_lch[par]
            sch = wtr_sch[par]
            tmp = wtr_tmp4[par]
            dst = wtr_dst4[par]
            ss = inp[wname].rearrange("(r p) d -> p r d", p=128)[
                :, 2 * qq:2 * qq + 2]
            lv = P.op("sync",
                      lambda h, d=tmp.rearrange("p (r d) -> p r d", d=1024),
                      s_=ss: h.dma_start(out=d, in_=s_),
                      wait=[("mm", tr_done.get(qn - 4, 0))], inc=lch, n=16)
            ev = 0
            for rr in range(2):
                for i in range(8):
                    pb = 4 + ((rr * 8 + i) % 4)
                    mv = P.trans(pall[:, 512 * pb:512 * pb + 128],
                                 tmp[:, 1024 * rr + 128 * i:
                                     1024 * rr + 128 * i + 128],
                                 ident[:],
                                 wait=[(lch, lv),
                                       ("vv", max(0, P.cnt["vv"] - 3))] +
                                 ([(sch, stw_done.get(qn - 4, 0))]
                                  if (rr == 0 and i == 0) else []),
                                 inc="mm")
                    ev = P.v(lambda h, rr=rr, i=i, pb=pb, dst=dst:
                             h.tensor_copy(
                                 dst[:, 256 * i + 128 * rr:
                                     256 * i + 128 * rr + 128],
                                 pall[:, 512 * pb:512 * pb + 128]),
                             wait=[("mm", mv)], inc="vv")
            tr_done[qn] = P.cnt["mm"]
            stw_done[qn] = P.store(
                wT_scr[wi].rearrange("p (i b) -> p i b", b=1024)
                [:, :, 256 * qq:256 * qq + 256],
                dst.rearrange("p (i c) -> p i c", c=256),
                wait=[("vv", ev)], sem=sch)
            qn += 1
    P.bar()

    def ln_affine(ln_id=None):
        P.v(lambda h: h.tensor_single_scalar(Svec, Svec, 1.0 / D, ALU.mult))
        P.v(lambda h: h.tensor_single_scalar(S2vec, S2vec, 1.0 / D, ALU.mult))
        P.v(lambda h: h.tensor_mul(Bvec, Svec, Svec))
        P.v(lambda h: h.tensor_sub(S2vec, S2vec, Bvec))
        P.v(lambda h: h.tensor_single_scalar(S2vec, S2vec, EPS, ALU.add),
            inc="vv")
        va = P.cnt["vv"]
        sq = P.act(Avec, S2vec, AF.Sqrt, wait=[("vv", va)], inc="aa")
        P.v(lambda h: h.reciprocal(Avec, Avec), wait=[("aa", sq)])
        P.v(lambda h: h.tensor_mul(Bvec, Svec, Avec), inc="vv")
        return P.cnt["vv"]

    def replicate(vec_slice, pbank, dst, n):
        mv = None
        for u0 in range(0, n, 512):
            w = min(512, n - u0)
            mv = P.mm(pall[0:128, 512 * pbank + u0:512 * pbank + u0 + w],
                      ones[:], vec_slice[:, u0:u0 + w], True, True,
                      wait=[("vv", P.cnt["vv"])] if u0 == 0 else (), inc="mm")
        return P.v(lambda h, d=dst, b=pbank, n=n: h.tensor_copy(
            d, pall[0:128, 512 * b:512 * b + n]),
            wait=[("mm", mv)], inc="vv")

    def load_w_transposed(wname, w_dst):
        P.fence("sync", "ld")
        P.load(w_dst, wT_scr[WT_IDX[wname]],
               wait=[("mm", P.cnt["mm"]), ("vv", P.cnt["vv"])])
        return 0

    def phaseP1(L, x_in_scr, ln_id=None):
        WL = f"L{L}_attn_"
        w_q, w_k, w_v = AR_W[:, 0:8192], AR_W[:, 8192:16384], AR_W[:, 16384:24576]
        if L == 0:
            P.fence("sync", "ld")
            for w, n in ((w_q, "Wq"), (w_k, "Wk"), (w_v, "Wv")):
                P.load(w.rearrange("p (i d) -> p i d", d=1024),
                       inp[WL + n].rearrange("(i p) d -> p i d", p=128))
            w_ld, w_vv = P.cnt["ld"], 0
        else:
            for w, n in ((w_q, "Wq"), (w_k, "Wk"), (w_v, "Wv")):
                load_w_transposed(WL + n, w)
            w_ld, w_vv = P.cnt["ld"], P.cnt["vv"]
        ln_affine(ln_id)
        xs2 = [AR_A[:, 0:2048], AR_A[:, 2048:4096]]
        xln2 = [AR_A[:, 4096:6144], AR_B[:, 5184:7232]]
        qs = AR_A[:, 6144:8192]
        ks = AR_A[:, 8192:10240]
        vs_ = AR_B[:, 1024:3072]
        arep = AR_B[:, 0:256]
        brep = AR_B[:, 256:512]
        v_all = AR_B[:, 3072:5184]
        vt0 = AR_B[:, 3072:4096]
        lvs_s = {}
        xln_done = {}
        mm_done = {}
        st2_v = 0
        vprev = {"v": 0}

        def emit_loads(si):
            sem_s = "ld" if si % 2 == 0 else "ld2"
            if si < 2:
                P.fence("sync", sem_s)
            cols_i = slice(256 * si, 256 * si + 256)
            lv = 0
            for ch in range(NCH):
                lv = P.op("sync", lambda h,
                          d=xs2[si % 2][:, 256 * ch:256 * ch + 256],
                          s_=x_in_scr[ch][:, cols_i]:
                          h.dma_start(out=d, in_=s_),
                          wait=[("vv", xln_done.get(si - 2, 0))]
                          if ch == 0 else (), inc=sem_s, n=16)
            lvs_s[si] = lv

        def front(s_):
            cols = slice(256 * s_, 256 * s_ + 256)
            xs = xs2[s_ % 2]
            xln = xln2[s_ % 2]
            sem_s = "ld" if s_ % 2 == 0 else "ld2"
            emit_loads(s_)
            replicate(Avec[0:1, cols], 4, arep, 256)
            ev = replicate(Bvec[0:1, cols], 4, brep, 256)
            for ch in range(NCH):
                P.v(lambda h, ch=ch, xs=xs, xln=xln: h.tensor_mul(
                    xln[:, 256 * ch:256 * ch + 256],
                    xs[:, 256 * ch:256 * ch + 256], arep),
                    wait=[(sem_s, lvs_s[s_]), ("vv", ev),
                          ("mm", mm_done.get(s_ - 2, 0))]
                    if ch == 0 else ())
                P.v(lambda h, ch=ch, xs=xs, xln=xln: h.tensor_sub(
                    xln[:, 256 * ch:256 * ch + 256],
                    xln[:, 256 * ch:256 * ch + 256], brep),
                    inc="vv" if ch == NCH - 1 else None)
            xln_done[s_] = P.cnt["vv"]

        front(0)
        front(1)
        for s_ in range(8):
            cols = slice(256 * s_, 256 * s_ + 256)
            xs = xs2[s_ % 2]
            xln = xln2[s_ % 2]
            xlv = xln_done[s_]
            st_prev = P.cnt["st"]
            for w_sb, dst_t, bname in ((w_q, qs, "bq"), (w_k, ks, "bk"),
                                       (w_v, vs_, "bv")):
                for o in range(NCH):
                    pb = o % 2
                    for i in range(NCH):
                        w8 = []
                        if i == 0:
                            w8 = [("vv", max(xlv, w_vv)), ("ld", w_ld),
                                  ("aa", max(0, P.cnt["aa"] - 1))]
                        P.mm(bank(pb, 256),
                             w_sb[:, 1024 * i + 128 * o:1024 * i + 128 * o + 128],
                             xln[:, 256 * i:256 * i + 256],
                             i == 0, i == NCH - 1, wait=w8,
                             inc="mm" if i == NCH - 1 else None)
                    mv = P.cnt["mm"]
                    P.act(dst_t[:, 256 * o:256 * o + 256], bank(pb, 256),
                          AF.Identity,
                          bias=bias_sb[:, BIAS_IDX[WL + bname] + o:
                                       BIAS_IDX[WL + bname] + o + 1],
                          wait=[("mm", mv), ("st", st_prev),
                                ("vv", xlv)], inc="aa")
            mm_done[s_] = P.cnt["mm"]
            qkv_done = P.cnt["aa"]
            P.fence("gpsimd", "st")
            for ch in range(NCH):
                P.store(q_scr[ch][:, cols], qs[:, 256 * ch:256 * ch + 256],
                        wait=[("aa", qkv_done)] if ch == 0 else ())
                P.store(k_scr[ch][:, cols], ks[:, 256 * ch:256 * ch + 256])
            if L == 1:
                P.v(lambda h: h.memset(v_all[:], 0.0), wait=[("st2", st2_v)])
                P.v(lambda h: h.memset(
                    v_all.rearrange("p (x c) -> p x c", c=66)[:, :, 64:65], 1.0),
                    inc="vv")
                evs = {}
                for hh in range(H):
                    rows = slice((hh % 2) * 64, (hh % 2) * 64 + 64)
                    for jj in range(2):
                        r0 = rows.start
                        mv = P.trans(bank(2 + jj, 64),
                                     vs_[rows, 256 * (hh // 2) + 128 * jj:
                                         256 * (hh // 2) + 128 * jj + 128],
                                     ident[r0:r0 + 64, r0:r0 + 64],
                                     wait=[("aa", qkv_done),
                                           ("vv", evs.get(2 * hh + jj - 2,
                                                          vprev["v"]))],
                                     inc="mm")
                        P.v(lambda h, hh=hh, jj=jj: h.tensor_copy(
                            v_all[:, 132 * hh + 66 * jj:132 * hh + 66 * jj + 64],
                            bank(2 + jj, 64)),
                            wait=[("mm", mv)], inc="vv")
                        evs[2 * hh + jj] = P.cnt["vv"]
                vprev["v"] = P.cnt["vv"]
                st2_v = P.store(
                    v_scr.transpose([1, 0, 2])[:, :, 132 * s_:132 * s_ + 132],
                    v_all.rearrange("p (x c) -> p x c", c=132),
                    wait=[("vv", evs[2 * H - 1])], sem="st2")
            else:
                cops = {}
                for jj in range(2):
                    for dv in range(NCH):
                        mv = P.trans(bank(2 + (dv % 2), 128),
                                     vs_[:, 256 * dv + 128 * jj:
                                         256 * dv + 128 * jj + 128],
                                     ident[:],
                                     wait=[("aa", qkv_done),
                                           ("vv", cops.get(8 * jj + dv - 2,
                                                           vprev["v"]))],
                                     inc="mm")
                        P.v(lambda h, dv=dv: h.tensor_copy(
                            vt0[:, 128 * dv:128 * dv + 128], bank(2 + (dv % 2), 128)),
                            wait=[("mm", mv), ("st", P.cnt["st"])], inc="vv")
                        cops[8 * jj + dv] = P.cnt["vv"]
                    P.store(v0_scr[2 * s_ + jj], vt0[:],
                            wait=[("vv", cops[8 * jj + NCH - 1])])
                vprev["v"] = P.cnt["vv"]
            if s_ + 2 < 8:
                front(s_ + 2)
        P.bar()

    def phaseP2_L0():
        kfm = AR_W[:, 0:16384]
        qh_t = AR_W[:, 16384:24576]
        probs = AR_A[:, 0:8192]
        rep_sb = AR_A[:, 8192:8704]
        attn_sb2 = [AR_A[:, 8704:9216], AR_A[:, 9216:9728]]
        vAh = [AR_B[:, 0:4096], AR_B[:, 4096:8192]]
        vch = ("ld2", "ld4")
        P.fence("sync", "ld")
        P.fence("sync", "ld2")
        P.fence("sync", "ld4")
        kld = 0
        for c in range(NCH):
            kld = P.load(kfm[:, 2048 * c:2048 * c + 2048], k_scr[c][:])
        av_g = {}
        avj3_prev = {}
        ev_q = {}
        qk_last_prev = 0
        prev_store = {0: ("st", 0), 1: ("st2", 0)}
        vld = {}

        def emit_vload(qq, g):
            if g >= 2:
                gate = av_g.get((qq, g - 2), 0)
            else:
                gate = av_g.get((qq - 1, 2 + g), 0)
            vld[(qq, g)] = P.op(
                "sync",
                lambda h, d=vAh[g % 2].rearrange("p (j e) -> p j e", e=256),
                s_=v0_scr[:, :, 256 * g:256 * g + 256].transpose([1, 0, 2]):
                h.dma_start(out=d, in_=s_),
                wait=[("mm", gate)] if gate else (), inc=vch[g % 2], n=16)

        for qq in range(4):
            qcols = slice(512 * qq, 512 * qq + 512)
            qld = 0
            for c in range(NCH):
                qld = P.load(qh_t[:, 1024 * c:1024 * c + 512],
                             q_scr[c][:, qcols],
                             wait=[("mm", qk_last_prev)] if c == 0 else ())
            emit_vload(qq, 0)
            emit_vload(qq, 1)
            expc = {}

            def emit_qk0(j, qq=qq):
                w0 = [("aa", expc.get(j - 2, 0))]
                if j <= 1:
                    w0.append(("ld", max(kld, qld)))
                for c in range(NCH):
                    P.mm(pall[:, 512 * (j % 2):512 * (j % 2) + 512],
                         kfm[:, 2048 * c + 128 * j:2048 * c + 128 * j + 128],
                         qh_t[:, 1024 * c:1024 * c + 512],
                         c == 0, c == NCH - 1, wait=w0 if c == 0 else (),
                         inc="mm" if c == NCH - 1 else None)
                mv = P.cnt["mm"]
                we = [("mm", mv)]
                if avj3_prev.get(j):
                    we.append(("mm", avj3_prev[j]))
                P.act(probs[:, 512 * j:512 * j + 512],
                      pall[:, 512 * (j % 2):512 * (j % 2) + 512],
                      AF.Exp, scale=0.125, bias=cmisc[:, 3:4],
                      wait=we, inc="aa")
                expc[j] = P.cnt["aa"]

            def emit_sum_avA(j, qq=qq):
                P.mm(pall[0:1, 1024:1536], ones128[:],
                     probs[:, 512 * j:512 * j + 512], j == 0, j == 15,
                     wait=[("aa", expc[j])], inc="mm" if j == 15 else None)
                for g in range(2):
                    for d2 in range(2):
                        w1 = []
                        if j == 0 and d2 == 0:
                            w1 = [(vch[g % 2], vld[(qq, g)]),
                                  ("vv", ev_q.get((qq - 1, 4 + 2 * g + 1), 0))]
                        P.mm(pall[:, 2048 + 512 * (2 * g + d2):
                                  2048 + 512 * (2 * g + d2) + 512],
                             vAh[g % 2][:, 256 * j + 128 * d2:
                                        256 * j + 128 * d2 + 128],
                             probs[:, 512 * j:512 * j + 512],
                             j == 0, j == 15, wait=w1,
                             inc="mm" if (j == 15 and d2 == 1) else None)
                    if j == 15:
                        av_g[(qq, g)] = P.cnt["mm"]

            emit_qk0(0)
            for j in range(1, 16):
                emit_qk0(j)
                emit_sum_avA(j - 1)
            emit_sum_avA(15)
            qk_last_prev = P.cnt["mm"]
            sum_done = P.cnt["mm"]
            P.v(lambda h: h.reciprocal(rsum[0:1, 0:512], pall[0:1, 1024:1536]),
                wait=[("mm", sum_done)])
            rv = P.cnt["vv"]
            mv = P.mm(pall[0:128, 1536:2048], ones[:], rsum[0:1, 0:512],
                      True, True, wait=[("vv", rv)], inc="mm")
            P.v(lambda h: h.tensor_copy(rep_sb, pall[0:128, 1536:2048]),
                wait=[("mm", mv)], inc="vv")
            for dv in range(4):
                sem_s = "st" if dv % 2 == 0 else "st2"
                P.v(lambda h, dv=dv: h.tensor_mul(
                    attn_sb2[dv % 2],
                    pall[:, 2048 + 512 * dv:2048 + 512 * dv + 512], rep_sb),
                    wait=[("mm", av_g[(qq, dv // 2)]), prev_store[dv % 2]],
                    inc="vv")
                ev_q[(qq, dv)] = P.cnt["vv"]
                prev_store[dv % 2] = (sem_s, P.store(
                    attn_scr[dv][:, qcols], attn_sb2[dv % 2],
                    wait=[("vv", ev_q[(qq, dv)])], sem=sem_s))
            emit_vload(qq, 2)
            emit_vload(qq, 3)
            avj3 = {}
            for g in (2, 3):
                for j in range(16):
                    for d2 in range(2):
                        w1 = []
                        if j == 0 and d2 == 0:
                            w1 = [(vch[g % 2], vld[(qq, g)]),
                                  ("vv", ev_q[(qq, 2 * (g - 2) + 1)])]
                        P.mm(pall[:, 2048 + 512 * (2 * (g - 2) + d2):
                                  2048 + 512 * (2 * (g - 2) + d2) + 512],
                             vAh[g % 2][:, 256 * j + 128 * d2:
                                        256 * j + 128 * d2 + 128],
                             probs[:, 512 * j:512 * j + 512],
                             j == 0, j == 15,
                             wait=w1, inc="mm" if d2 == 1 else None)
                    if g == 3:
                        avj3[j] = P.cnt["mm"]
                av_g[(qq, g)] = P.cnt["mm"]
            avj3_prev = avj3
            for dv in range(4, 8):
                sem_s = "st" if dv % 2 == 0 else "st2"
                g = 2 + (dv - 4) // 2
                P.v(lambda h, dv=dv: h.tensor_mul(
                    attn_sb2[dv % 2],
                    pall[:, 2048 + 512 * (dv - 4):2048 + 512 * (dv - 4) + 512],
                    rep_sb),
                    wait=[("mm", av_g[(qq, g)]), prev_store[dv % 2]],
                    inc="vv")
                ev_q[(qq, dv)] = P.cnt["vv"]
                prev_store[dv % 2] = (sem_s, P.store(
                    attn_scr[dv][:, qcols], attn_sb2[dv % 2],
                    wait=[("vv", ev_q[(qq, dv)])], sem=sem_s))
        P.bar()

    def phaseP2(L):
        kaug2 = [AR_A[0:66, 0:2048], AR_A[0:66, 8192:10240]]
        qaug2 = [AR_A[0:66, 2048:4096], AR_B[0:66, 2080:4128]]
        vh2 = [AR_B[0:128, 0:1056], AR_B[0:128, 4128:5184]]
        probs = [AR_A[:, 4096:5120], AR_A[:, 5120:6144]]
        rep_sb = AR_A[0:64, 6144:7168]
        attn_sb = AR_B[0:64, 1056:2080]
        P.v(lambda h: h.memset(qaug2[0][64:66, :], 1.0))
        P.v(lambda h: h.memset(qaug2[1][64:66, :], 1.0), inc="vv")
        lV2 = {}
        mm_head = {}
        for hh in range(H):
            kaug, qaug, vh = kaug2[hh % 2], qaug2[hh % 2], vh2[hh % 2]
            sem_h = "ld" if hh % 2 == 0 else "ld2"
            if hh < 2:
                P.fence("sync", sem_h)
            slot = hh if L == 1 else 16
            P.op("sync", lambda h, d=kaug[0:64, :],
                 s_=k_scr[hh // 2][(hh % 2) * 64:(hh % 2) * 64 + 64, :]:
                 h.dma_start(out=d, in_=s_),
                 wait=[("mm", mm_head.get(hh - 2, 0))], inc=sem_h, n=16)
            P.op("sync", lambda h, d=kaug[64:66, :],
                 s_=inp["c_alibi"][:, T * slot:T * slot + T]:
                 h.dma_start(out=d, in_=s_), inc=sem_h, n=16)
            P.op("sync", lambda h, d=qaug[0:64, :],
                 s_=q_scr[hh // 2][(hh % 2) * 64:(hh % 2) * 64 + 64, :]:
                 h.dma_start(out=d, in_=s_), inc=sem_h, n=16)
            lV = P.op("sync", lambda h, d=vh, s_=v_scr[hh][:]:
                      h.dma_start(out=d, in_=s_), inc=sem_h, n=16)
            lV2[hh] = lV
            vv_attn = P.cnt["vv"]
            exp_hist = {"e": {}, "a": {}}
            for half in range(2):
                qc = 1024 * half
                expc = {}
                avc = {}
                expc_prev = exp_hist["e"]
                avc_prev = exp_hist["a"]

                def emit_qk(j, expc_prev=expc_prev, vv_attn=vv_attn):
                    w0 = []
                    if j <= 1:
                        if half == 0:
                            w0.append((sem_h, lV))
                        if expc_prev.get(14 + j):
                            w0.append(("aa", expc_prev[14 + j]))
                        else:
                            w0.append(("vv", vv_attn))
                    if j >= 2:
                        w0 = w0 + [("aa", expc[j - 2])]
                    for u in range(2):
                        P.mm(pall[:, 1024 * (j % 2) + 512 * u:
                                  1024 * (j % 2) + 512 * u + 512],
                             kaug[:, 128 * j:128 * j + 128],
                             qaug[:, qc + 512 * u:qc + 512 * u + 512],
                             True, True, wait=w0 if u == 0 else (),
                             inc="mm" if u == 1 else None)
                    mv = P.cnt["mm"]
                    we = [("mm", mv)]
                    if j >= 2:
                        we.append(("mm", avc[j - 2]))
                    elif avc_prev.get(14 + j):
                        we.append(("mm", avc_prev[14 + j]))
                    P.act(probs[j % 2], pall[:, 1024 * (j % 2):1024 * (j % 2) + 1024],
                          AF.Exp, scale=0.125, wait=we, inc="aa")
                    expc[j] = P.cnt["aa"]

                def emit_av(j, vv_attn=vv_attn):
                    for u in range(2):
                        w1 = [("aa", expc[j])]
                        if j == 0 and vv_attn:
                            w1.append(("vv", vv_attn))
                        P.mm(pall[0:66, 2048 + 512 * u:2048 + 512 * u + 512],
                             vh[:, 66 * j:66 * j + 66],
                             probs[j % 2][:, 512 * u:512 * u + 512],
                             j == 0, j == 15,
                             wait=w1 if u == 0 else (),
                             inc="mm")
                    avc[j] = P.cnt["mm"]

                emit_qk(0)
                for j in range(1, 16):
                    emit_qk(j)
                    emit_av(j - 1)
                emit_av(15)
                P.v(lambda h: h.reciprocal(rsum[:], pall[64:65, 2048:3072]),
                    wait=[("mm", avc[15])], inc="vv")
                rv = P.cnt["vv"]
                P.mm(pall[0:64, 3072:3584], ones[0:1, 0:64], rsum[0:1, 0:512],
                     True, True, wait=[("vv", rv)])
                mv = P.mm(pall[0:64, 3584:4096], ones[0:1, 0:64],
                          rsum[0:1, 512:1024], True, True, inc="mm")
                P.v(lambda h: h.tensor_copy(rep_sb, pall[0:64, 3072:4096]),
                    wait=[("mm", mv), ("vv", P.cnt["vv"])], inc="vv")
                P.v(lambda h: h.tensor_mul(attn_sb, pall[0:64, 2048:3072], rep_sb),
                    wait=[("st", P.cnt["st"])], inc="vv")
                vv_attn = P.cnt["vv"]
                P.store(attn_scr[hh // 2][(hh % 2) * 64:(hh % 2) * 64 + 64,
                                          qc:qc + 1024],
                        attn_sb, wait=[("vv", vv_attn)])
                exp_hist["e"] = expc
                exp_hist["a"] = avc
            mm_head[hh] = P.cnt["mm"]
        P.bar()

    def phaseP3(L, x_prev, x_new, ln_next=None):
        WL = f"L{L}_attn_"
        wo = AR_W[:, 0:8192]
        if L == 0:
            P.load(wo.rearrange("p (i d) -> p i d", d=1024),
                   inp[WL + "Wo"].rearrange("(i p) d -> p i d", p=128))
            wo_vv = 0
        else:
            wo_vv = load_w_transposed(WL + "Wo", wo)
        wo_ld = P.cnt["ld"]
        attn_t = AR_A[:, 0:8192]
        y_t = AR_A[:, 8192:9216]
        xo2 = [AR_A[:, 9216:10240], AR_B[:, 2048:3072]]
        xn_t = AR_B[:, 0:1024]
        xq_t = AR_B[:, 1024:2048]
        add_done = {}
        for th in range(2):
            tc = slice(1024 * th, 1024 * th + 1024)
            mm_now = P.cnt["mm"]
            s2prev = 0
            P.fence("sync", "ld")
            P.fence("sync", "ld4")
            P.fence("sync", "ld5")
            lva = 0
            for c in range(NCH):
                lva = P.load(attn_t[:, 1024 * c:1024 * c + 1024],
                             attn_scr[c][:, tc],
                             wait=[("mm", mm_now)] if c == 0 else ())
            lxo2 = {}

            def emit_lxo(o, th=th):
                xch = "ld4" if o % 2 == 0 else "ld5"
                gate = (add_done.get((th, o - 2), 0) if o >= 2
                        else add_done.get((th - 1, 6 + o), 0))
                lxo2[o] = P.op("sync", lambda h, d=xo2[o % 2],
                               s_=x_prev[o][:, tc]: h.dma_start(out=d, in_=s_),
                               wait=[("vv", gate)], inc=xch, n=16)

            emit_lxo(0)
            emit_lxo(1)
            adds = []
            for o in range(NCH):
                pb = (o % 2) * 2
                xch = "ld4" if o % 2 == 0 else "ld5"
                for c in range(NCH):
                    w8 = [("ld", max(lva, wo_ld)), ("vv", wo_vv),
                          ("aa", max(0, P.cnt["aa"] - 1))] if c == 0 else []
                    for u in range(2):
                        P.mm(pall[:, 512 * pb + 512 * u:512 * pb + 512 * u + 512],
                             wo[:, 1024 * c + 128 * o:1024 * c + 128 * o + 128],
                             attn_t[:, 1024 * c + 512 * u:1024 * c + 512 * u + 512],
                             c == 0, c == NCH - 1,
                             wait=w8 if u == 0 else (),
                             inc="mm" if (c == NCH - 1 and u == 1) else None)
                mv = P.cnt["mm"]
                av = P.act(y_t, pall[:, 512 * pb:512 * pb + 1024], AF.Identity,
                           bias=bias_sb[:, BIAS_IDX[WL + "bo"] + o:
                                        BIAS_IDX[WL + "bo"] + o + 1],
                           wait=[("mm", mv), ("vv", P.cnt["vv"])], inc="aa")
                P.v(lambda h, o=o: h.tensor_add(xn_t, y_t, xo2[o % 2]),
                    wait=[("aa", av), (xch, lxo2[o]), ("st", P.cnt["st"])],
                    inc="vv")
                add_done[(th, o)] = P.cnt["vv"]
                if o + 2 < NCH:
                    emit_lxo(o + 2)
                xnv = P.cnt["vv"]
                P.fence("gpsimd", "st")
                P.store(x_new[o][:, tc], xn_t, wait=[("vv", xnv)])
                sv = P.act(xq_t, xn_t, AF.Square,
                           wait=[("vv", xnv), ("mm", s2prev)], inc="aa")
                for u in range(2):
                    P.mm(pall[0:1, 2048 + 512 * u:2048 + 512 * u + 512],
                         ones128[:], xn_t[:, 512 * u:512 * u + 512],
                         o == 0, o == NCH - 1, wait=[("vv", xnv)] if u == 0 else ())
                    P.mm(pall[0:1, 3072 + 512 * u:3072 + 512 * u + 512],
                         ones128[:], xq_t[:, 512 * u:512 * u + 512],
                         o == 0, o == NCH - 1,
                         wait=[("aa", sv)] if u == 0 else (),
                         inc="mm" if u == 1 else None)
                s2prev = P.cnt["mm"]
            mvs = P.cnt["mm"]
            P.v(lambda h, th=th: h.tensor_copy(
                Svec[0:1, 1024 * th:1024 * th + 1024],
                pall[0:1, 2048:3072]), wait=[("mm", mvs)])
            P.v(lambda h, th=th: h.tensor_copy(
                S2vec[0:1, 1024 * th:1024 * th + 1024],
                pall[0:1, 3072:4096]), inc="vv")
        P.bar()

    def phaseP4(x_prev, x_new, ln_id=None, ln_next=None):
        ln_affine(ln_id)
        rm4 = AR_B[:, 0:4096]
        rup4 = AR_B[:, 4096:8192]
        P.fence("sync", "ld")
        l1 = P.load(rm4, rails_scr[:, 0:4096])
        l2 = P.load(rup4, rails_scr[:, 4096:8192])
        xs = AR_A[:, 2048:4096]
        xln = AR_A[:, 4096:6144]
        arep = AR_A[:, 6144:6400]
        brep = AR_A[:, 6400:6656]
        sq_t = AR_A[:, 6656:6912]
        part0 = AR_A[:, 8192:10240]
        wct = [AR_A[:, 0:1024], AR_A[:, 1024:2048], AR_A[:, 6912:7936]]
        wct_sems = ("ld", "ld2", "ld4")
        y_sb = AR_W[:, 16384:24576]
        h_half = AR_W[:, 0:16384]
        for s in range(8):
            cols = slice(256 * s, 256 * s + 256)
            P.fence("sync", "ld")
            lvs = 0
            for ch in range(NCH):
                lvs = P.load(xs[:, 256 * ch:256 * ch + 256], x_prev[ch][:, cols],
                             wait=[("vv", P.cnt["vv"])] if ch == 0 else ())
            replicate(Avec[0:1, cols], 6, arep, 256)
            ev = replicate(Bvec[0:1, cols], 6, brep, 256)
            mm_prev = P.cnt["mm"]
            for ch in range(NCH):
                P.v(lambda h, ch=ch: h.tensor_mul(
                    xln[:, 256 * ch:256 * ch + 256],
                    xs[:, 256 * ch:256 * ch + 256], arep),
                    wait=[("ld", lvs), ("vv", ev), ("mm", mm_prev)]
                    if ch == 0 else ())
                P.v(lambda h, ch=ch: h.tensor_sub(
                    xln[:, 256 * ch:256 * ch + 256],
                    xln[:, 256 * ch:256 * ch + 256], brep),
                    inc="vv" if ch == NCH - 1 else None)
            xlv = P.cnt["vv"]
            for pk in range(32):
                w0 = [("ld", max(l1, l2)), ("vv", xlv)] if pk == 0 else \
                     [("vv", P.cnt["vv"] - 1)]
                P.mm(bank(4 + (pk % 2), 256), rm4[:, 128 * pk:128 * pk + 128],
                     xln[:, 256 * (pk % 8):256 * (pk % 8) + 256], True, True,
                     wait=w0, inc="mm")
                mv = P.cnt["mm"]
                P.v(lambda h, pk=pk: h.tensor_copy(
                    y_sb[:, 256 * pk:256 * pk + 256], bank(4 + (pk % 2), 256)),
                    wait=[("mm", mv)], inc="vv")
            yv = P.cnt["vv"]
            prev_done = yv
            for half in range(2):
                for ci in range(64):
                    c = 64 * half + ci
                    hh, m = c // 32, c % 32
                    g = m // 4
                    rowb = 32 * (m % 4)
                    w0 = [("vv", prev_done)] if ci == 0 else \
                         [("aa", P.cnt["aa"] - 1)]
                    P.mm(bank(4 + (ci % 2), 256, rowb if False else 0, 128),
                         rup4[rowb:rowb + 32, 128 * (c // 4):128 * (c // 4) + 128],
                         y_sb[rowb:rowb + 32,
                              256 * (8 * hh + g):256 * (8 * hh + g) + 256],
                         True, True, wait=w0, inc="mm",
                         tile_position=(rowb, 0))
                    mv = P.cnt["mm"]
                    P.act(h_half[:, 256 * ci:256 * ci + 256],
                          bank(4 + (ci % 2), 256), AF.Gelu,
                          wait=[("mm", mv)], inc="aa")
                hv = P.cnt["aa"]
                ldw = {}
                mmc = {}
                for ci in range(64):
                    c = 64 * half + ci
                    sem_c = wct_sems[ci % 3]
                    wl = [("aa", hv), ("vv", P.cnt["vv"]),
                          ("mm", P.cnt["mm"])] if ci == 0 else \
                         ([("mm", mmc[ci - 3])] if ci >= 3 else
                          [("mm", P.cnt["mm"] - NCH)])
                    if ci < 3:
                        P.fence("sync", sem_c)
                    ldw[ci] = P.op(
                        "sync",
                        lambda h, d=wct[ci % 3].rearrange("p (o d) -> p o d", d=128),
                        s_=wcT_scr[c].rearrange("o p d -> p o d"):
                        h.dma_start(out=d, in_=s_),
                        wait=wl, inc=sem_c, n=16)
                    for o in range(NCH):
                        P.mm(pall[:, 512 * o:512 * o + 256],
                             wct[ci % 3][:, 128 * o:128 * o + 128],
                             h_half[:, 256 * ci:256 * ci + 256],
                             ci == 0, ci == 63,
                             wait=[(sem_c, ldw[ci]), ("aa", hv)] if o == 0 else (),
                             inc="mm" if o == NCH - 1 else None)
                    mmc[ci] = P.cnt["mm"]
                mv = mmc[63]
                if half == 0:
                    for o in range(NCH):
                        P.v(lambda h, o=o: h.tensor_copy(
                            part0[:, 256 * o:256 * o + 256],
                            pall[:, 512 * o:512 * o + 256]),
                            wait=[("mm", mv), ("st", P.cnt["st"])] if o == 0 else (),
                            inc="vv" if o == NCH - 1 else None)
                    prev_done = P.cnt["vv"]
                else:
                    st_prev = P.cnt["st"]
                    for o in range(NCH):
                        P.v(lambda h, o=o: h.tensor_add(
                            part0[:, 256 * o:256 * o + 256],
                            pall[:, 512 * o:512 * o + 256],
                            part0[:, 256 * o:256 * o + 256]),
                            wait=[("mm", mv)] if o == 0 else ())
                        P.v(lambda h, o=o: h.tensor_add(
                            part0[:, 256 * o:256 * o + 256],
                            part0[:, 256 * o:256 * o + 256],
                            xs[:, 256 * o:256 * o + 256]),
                            inc="vv")
                    addv = P.cnt["vv"]
                    P.fence("gpsimd", "st")
                    for o in range(NCH):
                        P.store(x_new[o][:, cols], part0[:, 256 * o:256 * o + 256],
                                wait=[("vv", addv)] if o == 0 else ())
                    s2p4 = 0
                    for o in range(NCH):
                        sv = P.act(sq_t, part0[:, 256 * o:256 * o + 256],
                                   AF.Square, wait=[("vv", addv), ("mm", s2p4)],
                                   inc="aa")
                        P.mm(pall[0:1, 2048:2304], ones128[:],
                             part0[:, 256 * o:256 * o + 256],
                             o == 0, o == NCH - 1,
                             wait=[("vv", addv)] if o == 0 else ())
                        P.mm(pall[0:1, 3072:3328], ones128[:], sq_t,
                             o == 0, o == NCH - 1, wait=[("aa", sv)],
                             inc="mm")
                        s2p4 = P.cnt["mm"]
                    mvs = P.cnt["mm"]
                    P.v(lambda h, s=s: h.tensor_copy(
                        Svec[0:1, 256 * s:256 * s + 256],
                        pall[0:1, 2048:2304]), wait=[("mm", mvs)])
                    P.v(lambda h, s=s: h.tensor_copy(
                        S2vec[0:1, 256 * s:256 * s + 256],
                        pall[0:1, 3072:3328]), inc="vv")
        P.bar()

    def phaseP6(x_prev, ln_id=None):
        wT = AR_W[:, 0:8192]
        wt_vv = load_w_transposed("L1_ffn_weight_fp8", wT)
        ln_affine(ln_id)
        xs2 = [AR_W[:, 8192:12288], AR_W[:, 12288:16384]]
        xq2 = [AR_B[:, 0:4096], AR_B[:, 4096:8192]]
        xf = AR_W[:, 16384:20480]
        arep = AR_A[:, 2048:2560]
        brep = AR_A[:, 2560:3072]
        t_a = AR_A[:, 4096:5120]
        t_g = AR_A[:, 5120:6144]
        t_c = AR_A[:, 6144:7168]
        t_f = AR_A[:, 7168:8192]
        t_r = AR_A[:, 8192:9216]
        t_s = AR_A[:, 9216:10240]
        yo2 = [AR_A[:, 0:512], AR_A[:, 512:1024]]
        tm2 = [AR_A[:, 1024:2048], AR_A[:, 3072:4096]]
        mm_strip = {}
        qv_strip = {}
        avx_pb = {-1: {0: 0, 1: 0}}
        add_pb = {-1: {0: 0, 1: 0}}
        trans_last = {-1: P.cnt["mm"]}
        tmev_bank = {2: 0, 3: 0}
        st_ch = {"st": {-1: P.cnt["st"]}, "st2": {-1: P.cnt["st2"]}}

        def front(k):
            cols = slice(512 * k, 512 * k + 512)
            xs = xs2[k % 2]
            xq = xq2[k % 2]
            sem_k = "ld" if k % 2 == 0 else "ld2"
            if k < 2:
                P.fence("sync", sem_k)
            lvs = 0
            for ch in range(NCH):
                lvs = P.op("sync", lambda h, d=xs[:, 512 * ch:512 * ch + 512],
                           s_=x_prev[ch][:, cols]: h.dma_start(out=d, in_=s_),
                           wait=[("mm", mm_strip.get(k - 2, 0))]
                           if ch == 0 else (), inc=sem_k, n=16)
            ev = replicate(Avec[0:1, cols], 6, arep, 512)
            ev = replicate(Bvec[0:1, cols], 6, brep, 512)
            for ch in range(NCH):
                x1 = xq[:, 512 * ch:512 * ch + 512]
                xsch = xs[:, 512 * ch:512 * ch + 512]
                P.v(lambda h, a=x1, b=xsch: h.tensor_mul(a, b, arep),
                    wait=[(sem_k, lvs), ("vv", max(ev, wt_vv)),
                          ("mm", mm_strip.get(k - 2, 0))] if ch == 0 else ())
                P.v(lambda h, a=x1: h.tensor_sub(a, a, brep),
                    inc="vv" if ch == NCH - 1 else None)
            for t4 in range(4):
                x1 = xq[:, 1024 * t4:1024 * t4 + 1024]
                P.v(lambda h, a=x1: h.tensor_scalar(a, a, invis[:], 448.0,
                                                    ALU.mult, ALU.min))
                P.v(lambda h, a=x1: h.tensor_single_scalar(a, a, -448.0, ALU.max),
                    inc="vv")
                cv = P.cnt["vv"]
                P.act(t_a, x1, AF.Abs, wait=[("vv", cv)])
                sg = P.act(t_g, x1, AF.Sign, inc="aa")
                P.v(lambda h: h.tensor_single_scalar(t_f, t_a, 2.0 ** -9, ALU.max),
                    wait=[("aa", sg)], inc="vv")
                av2 = P.cnt["vv"]
                lnv = P.act(t_f, t_f, AF.Ln, wait=[("vv", av2)], inc="aa")
                P.v(lambda h: h.tensor_scalar(t_f, t_f, INV_LN2, 9.0, ALU.mult,
                                              ALU.add), wait=[("aa", lnv)])
                P.v(lambda h: h.tensor_copy(t_c, t_f))
                for b in (8.0, 4.0, 2.0, 1.0):
                    P.v(lambda h, b=b: h.tensor_single_scalar(t_r, t_f, b,
                                                              ALU.is_ge))
                    P.v(lambda h, b=b: h.scalar_tensor_tensor(t_f, t_r, -b, t_f,
                                                              ALU.mult, ALU.add))
                P.v(lambda h: h.tensor_sub(t_c, t_c, t_f))
                P.v(lambda h: h.tensor_scalar(t_c, t_c, 3.0, -12.0,
                                              ALU.max, ALU.add), inc="vv")
                ecv = P.cnt["vv"]
                P.act(t_r, t_c, AF.Exp, scale=-LN2, wait=[("vv", ecv)])
                sv2 = P.act(t_s, t_c, AF.Exp, scale=LN2, inc="aa")
                P.v(lambda h: h.tensor_mul(t_f, t_a, t_r), wait=[("aa", sv2)])
                P.v(lambda h: h.tensor_single_scalar(t_f, t_f, 0.5, ALU.add))
                P.v(lambda h: h.tensor_copy(t_c, t_f))
                for b in (16.0, 8.0, 4.0, 2.0, 1.0):
                    P.v(lambda h, b=b: h.tensor_single_scalar(t_r, t_f, b,
                                                              ALU.is_ge))
                    P.v(lambda h, b=b: h.scalar_tensor_tensor(t_f, t_r, -b, t_f,
                                                              ALU.mult, ALU.add))
                P.v(lambda h: h.tensor_sub(t_c, t_c, t_f))
                P.v(lambda h: h.scalar_tensor_tensor(t_c, t_c, iscol[:], t_s,
                                                     ALU.mult, ALU.mult))
                P.v(lambda h, a=x1: h.tensor_mul(a, t_c, t_g), inc="vv")
            qv_strip[k] = P.cnt["vv"]

        def back(k):
            xs = xs2[k % 2]
            xq = xq2[k % 2]
            avx_pb[k] = dict(avx_pb[k - 1])
            add_pb[k] = dict(add_pb[k - 1])
            for o in range(NCH):
                pb = o % 2
                for i in range(NCH):
                    P.mm(pall[:, 512 * pb:512 * pb + 512],
                         wT[:, 1024 * i + 128 * o:1024 * i + 128 * o + 128],
                         xq[:, 512 * i:512 * i + 512],
                         i == 0, i == NCH - 1,
                         wait=[("vv", qv_strip[k]),
                               ("aa", avx_pb[k][pb])] if i == 0 else (),
                         inc="mm" if i == NCH - 1 else None)
                mv = P.cnt["mm"]
                yo = yo2[pb]
                avx = P.act(yo, pall[:, 512 * pb:512 * pb + 512], AF.Identity,
                            scale=wscol[:],
                            bias=bias_sb[:, BIAS_IDX["L1_ffn_bias"] + o:
                                         BIAS_IDX["L1_ffn_bias"] + o + 1],
                            wait=[("mm", mv), ("gg", add_pb[k][pb])], inc="aa")
                avx_pb[k][pb] = avx
                P.op("gpsimd", lambda h, o=o, xs=xs, yo=yo: h.tensor_add(
                    xf[:, 512 * o:512 * o + 512], yo,
                    xs[:, 512 * o:512 * o + 512]),
                    wait=[("aa", avx)] + ([("mm", trans_last[k - 1])]
                                          if o == 0 else []))
                add_pb[k][pb] = P.cnt["gg"]
            adds_done = P.cnt["gg"]
            for tb in range(4):
                tmb = tm2[tb % 2]
                sch = "st" if tb % 2 == 0 else "st2"
                stw = st_ch[sch].get((k, tb - 2), st_ch[sch][-1])
                evs = []
                for ch in range(NCH):
                    bnk = 2 + ch % 2
                    mvt = P.trans(bank(bnk, 128),
                                  xf[:, 512 * ch + 128 * tb:
                                     512 * ch + 128 * tb + 128],
                                  ident[:],
                                  wait=[("gg", adds_done), (sch, stw),
                                        ("aa", tmev_bank[bnk])],
                                  inc="mm")
                    evv = P.act(tmb[:, 128 * ch:128 * ch + 128],
                                bank(bnk, 128), AF.Identity,
                                wait=[("mm", mvt), (sch, stw)], inc="aa")
                    evs.append(evv)
                    tmev_bank[bnk] = evv
                st_ch[sch][(k, tb)] = P.store(
                    out[512 * k + 128 * tb:512 * k + 128 * tb + 128, :],
                    tmb[:], wait=[("aa", evs[-1])], sem=sch)
            st_ch["st"][-1] = st_ch["st"][(k, 2)]
            st_ch["st2"][-1] = st_ch["st2"][(k, 3)]
            trans_last[k] = P.cnt["mm"]
            mm_strip[k] = P.cnt["mm"]

        for k in range(5):
            if k < 4:
                front(k)
            if k >= 1:
                back(k - 1)
        P.bar()

    chain = [
        lambda: phaseP1(0, xs_scr[0], 0),
        lambda: phaseP2_L0(),
        lambda: phaseP3(0, xs_scr[0], xs_scr[1], 1),
        lambda: phaseP4(xs_scr[1], xs_scr[2], 1, 2),
        lambda: phaseP1(1, xs_scr[2], 2),
        lambda: phaseP2(1),
        lambda: phaseP3(1, xs_scr[2], xs_scr[3], 3),
        lambda: phaseP6(xs_scr[3], 3),
    ]
    for i, ph in enumerate(chain):
        if i < upto:
            ph()

    P.raw("gpsimd", lambda h, v=P.cnt["st"]: h.wait_ge(sems["st"], v))
    P.raw("sync", lambda h, v=P.cnt["ld"]: h.wait_ge(sems["ld"], v))

    with es:
        block = es.enter_context(nc.Block())

        @block.sync
        def _(h):
            for f in P.q["sync"]:
                f(h)

        @block.gpsimd
        def _(h):
            for f in P.q["gpsimd"]:
                f(h)

        @block.vector
        def _(h):
            for f in P.q["vector"]:
                f(h)

        @block.scalar
        def _(h):
            for f in P.q["scalar"]:
                f(h)

        @block.tensor
        def _(h):
            for f in P.q["tensor"]:
                f(h)
    return nc


def make_consts():
    ident = np.eye(128, dtype=np.float32)
    alibi = np.zeros((2, 17 * T), dtype=np.float32)
    slopes = np.power(2.0, np.linspace(-4.0, -0.3, 16).astype(np.float64))
    k = np.arange(T, dtype=np.float64)
    for h in range(16):
        t = -8.0 * slopes[h] * k - 160.0
        hi = t.astype(np.float32)
        lo = (t - hi.astype(np.float64)).astype(np.float32)
        alibi[0, h * T:(h + 1) * T] = hi
        alibi[1, h * T:(h + 1) * T] = lo
    alibi[0, 16 * T:17 * T] = -160.0
    cmisc = np.zeros((128, 8), dtype=np.float32)
    cmisc[:, 0] = EPS
    cmisc[:, 1] = 12.0 * LN2
    cmisc[:, 2] = -12.0 * LN2
    cmisc[:, 3] = -20.0
    return {"c_ident": ident, "c_alibi": alibi, "c_misc": cmisc}


def kernel(**inputs):
    consts = make_consts()
    nc = build()
    x = np.ascontiguousarray(np.asarray(inputs["L0_attn_x"], dtype=np.float32))
    in_maps = []
    for c in range(8):
        m = {k: np.ascontiguousarray(np.asarray(v, dtype=np.float32))
             for k, v in inputs.items() if k != "L0_attn_x"}
        m["L0_attn_x"] = x[c]
        m.update(consts)
        in_maps.append(m)
    res = run_bass_kernel_spmd(nc, in_maps, core_ids=list(range(8)))
    return np.stack([np.asarray(r["out"]) for r in res.results], axis=0)


# version 21
# speedup vs baseline: 1.0395x; 1.0395x over previous
import contextlib
import numpy as np
import concourse.bass as bass
import concourse.mybir as mybir
from concourse.bass_utils import run_bass_kernel_spmd

F32 = mybir.dt.float32
AF = mybir.ActivationFunctionType
ALU = mybir.AluOpType

LN2 = float(np.log(2.0))
INV_LN2 = float(1.0 / np.log(2.0))

T = 2048
D = 1024
NCH = 8
H = 16
HD = 64
DZ = 16384
EPS = 1e-5

ENGS = ("sync", "gpsimd", "vector", "scalar", "tensor")


class Plan:
    def __init__(self, nc, sems):
        self.nc = nc
        self.sems = sems
        self.cnt = {k: 0 for k in sems}
        self.q = {e: [] for e in ENGS}
        self.nbar = 0

    AUTO_SER = {"vector": "vv", "scalar": "aa", "gpsimd": "gg"}

    def op(self, eng, fn, wait=(), inc=None, n=1):
        ser = self.AUTO_SER.get(eng)
        if ser is not None:
            wait = tuple(wait) + ((ser, self.cnt[ser]),)
            if inc is None:
                inc = ser
                n = 1
        wait = tuple((s, v) for (s, v) in wait if v and v > 0)
        if inc is not None:
            self.cnt[inc] += n
        sems = self.sems

        def run(h, fn=fn, wait=wait, inc=inc, n=n):
            for s, v in wait:
                h.wait_ge(sems[s], v)
            inst = fn(h)
            if inc is not None:
                inst.then_inc(sems[inc], n)
        self.q[eng].append(run)
        return self.cnt[inc] if inc is not None else None

    def raw(self, eng, fn):
        self.q[eng].append(fn)

    def fence(self, eng, sem):
        v = self.cnt[sem]
        if v:
            sems = self.sems
            self.raw(eng, lambda h, v=v, s=sem: h.wait_ge(sems[s], v))

    def bar(self):
        self.nbar += 1
        tgt = self.nbar * 5
        ld_t, st_t = self.cnt["ld"], self.cnt["st"]
        ld3_t, ld4_t = self.cnt["ld3"], self.cnt["ld4"]
        ld2_t = self.cnt["ld2"]
        ld5_t = self.cnt["ld5"]
        st2_t = self.cnt["st2"]
        st3_t = self.cnt["st3"]
        st4_t = self.cnt["st4"]
        sems = self.sems
        for e in ENGS:
            def run(h, e=e, tgt=tgt, ld_t=ld_t, st_t=st_t,
                    ld3_t=ld3_t, ld4_t=ld4_t, ld2_t=ld2_t, st2_t=st2_t,
                    st3_t=st3_t, st4_t=st4_t, ld5_t=ld5_t):
                if e == "sync":
                    if ld_t:
                        h.wait_ge(sems["ld"], ld_t)
                    if ld2_t:
                        h.wait_ge(sems["ld2"], ld2_t)
                    if ld5_t:
                        h.wait_ge(sems["ld5"], ld5_t)
                if e == "gpsimd":
                    if ld3_t:
                        h.wait_ge(sems["ld3"], ld3_t)
                    if ld4_t:
                        h.wait_ge(sems["ld4"], ld4_t)
                if e == "gpsimd":
                    if st_t:
                        h.wait_ge(sems["st"], st_t)
                    if st2_t:
                        h.wait_ge(sems["st2"], st2_t)
                    if st3_t:
                        h.wait_ge(sems["st3"], st3_t)
                    if st4_t:
                        h.wait_ge(sems["st4"], st4_t)
                h.nop().then_inc(sems["bar"], 1)
                h.wait_ge(sems["bar"], tgt)
            self.q[e].append(run)

    def load(self, dst, src, wait=()):
        return self.op("sync", lambda h, d=dst, s=src: h.dma_start(out=d, in_=s),
                       wait=wait, inc="ld", n=16)

    def store(self, dst, src, wait=(), sem="st"):
        return self.op("gpsimd", lambda h, d=dst, s=src: h.dma_start(out=d, in_=s),
                       wait=wait, inc=sem, n=16)

    def mm(self, out, lhsT, rhs, start, stop, wait=(), inc=None, tile_position=None):
        return self.op(
            "tensor",
            lambda h, o=out, l=lhsT, r=rhs, a=start, b=stop, tp=tile_position:
                h.matmul(o, l, r, start=a, stop=b, tile_position=tp),
            wait=wait, inc=inc)

    def trans(self, out, in_, ident, wait=(), inc=None):
        return self.op(
            "tensor",
            lambda h, o=out, i=in_, d=ident: h.transpose(o, i, d),
            wait=wait, inc=inc)

    def act(self, out, in_, func, bias=0.0, scale=1.0, wait=(), inc=None):
        return self.op(
            "scalar",
            lambda h, o=out, i=in_, f=func, b=bias, s=scale:
                h.activation(o, i, f, bias=b, scale=s),
            wait=wait, inc=inc)

    def v(self, fn, wait=(), inc=None):
        return self.op("vector", fn, wait=wait, inc=inc)


def build(nc=None, debug=(), upto=8):
    if nc is None:
        nc = bass.Bass("TRN2", target_bir_lowering=False,
                       dynamic_dma_scratch_size=16384)

    def dk(name):
        return "ExternalOutput" if name in debug else "Internal"

    inp = {}
    names = [("L0_attn_x", [T, D])]
    for L in (0, 1):
        for w in ("Wq", "Wk", "Wv", "Wo"):
            names.append((f"L{L}_attn_{w}", [D, D]))
        for b in ("bq", "bk", "bv", "bo"):
            names.append((f"L{L}_attn_{b}", [D]))
    names += [("L0_ffn_rail_maker", [4, 64, 16, 16]),
              ("L0_ffn_rail_up_proj", [4, 64, 64, 16]),
              ("L0_ffn_w_cproj", [D, DZ]),
              ("L1_ffn_weight_fp8", [D, D]),
              ("L1_ffn_bias", [D]),
              ("L1_ffn_input_scale", [1]),
              ("L1_ffn_weight_scale", [1]),
              ("c_ident", [128, 128]),
              ("c_alibi", [2, 17 * T]),
              ("c_misc", [128, 8])]
    for nm, shp in names:
        inp[nm] = nc.dram_tensor(nm, shp, F32, kind="ExternalInput").ap()

    out = nc.dram_tensor("out", [T, D], F32, kind="ExternalOutput").ap()

    xs_scr = [nc.dram_tensor(f"x{i}_scr", [NCH, 128, T], F32,
                             kind=dk(f"x{i}_scr")).ap() for i in range(4)]
    q_scr = nc.dram_tensor("q_scr", [NCH, 128, T], F32, kind=dk("q_scr")).ap()
    k_scr = nc.dram_tensor("k_scr", [NCH, 128, T], F32, kind=dk("k_scr")).ap()
    v_scr = nc.dram_tensor("v_scr", [H, 128, 16 * 66], F32, kind=dk("v_scr")).ap()
    v0_scr = nc.dram_tensor("v0_scr", [16, 128, D], F32, kind=dk("v0_scr")).ap()
    attn_scr = nc.dram_tensor("attn_scr", [NCH, 128, T], F32,
                              kind=dk("attn_scr")).ap()
    wcT_scr = nc.dram_tensor("wcT_scr", [128, NCH, 128, 128], F32,
                             kind=dk("wcT_scr")).ap()
    rails_scr = nc.dram_tensor("rails_scr", [128, 8192], F32,
                               kind=dk("rails_scr")).ap()
    wT_scr = nc.dram_tensor("wT_scr", [5, 128, 8192], F32, kind=dk("wT_scr")).ap()

    def sbuf(name, shape):
        return nc.alloc_sbuf_tensor(name, shape, F32).ap()

    AR_W = sbuf("AR_W", [128, 24576])
    AR_A = sbuf("AR_A", [128, 10240])
    AR_B = sbuf("AR_B", [128, 8192])
    ident = sbuf("ident", [128, 128])
    ones = sbuf("ones", [1, 128])
    ones128 = sbuf("ones128", [128, 1])
    VEC = sbuf("VEC", [128, 8256])
    Avec = VEC[0:1, 0:2048]
    Bvec = VEC[0:1, 2048:4096]
    Svec = VEC[0:1, 4096:6144]
    S2vec = VEC[0:1, 6144:8192]
    rsum = VEC[0:1, 4096:5120]
    scal1 = VEC[0:1, 8192:8196]
    bias_sb = sbuf("bias_sb", [128, 9 * 8])
    cmisc = sbuf("cmisc", [128, 8])
    invis = sbuf("invis", [128, 1])
    iscol = sbuf("iscol", [128, 1])
    wscol = sbuf("wscol", [128, 1])

    BIAS_IDX = {f"L{L}_attn_{b}": 8 * (i + 4 * L)
                for L in (0, 1) for i, b in enumerate(("bq", "bk", "bv", "bo"))}
    BIAS_IDX["L1_ffn_bias"] = 64

    es = contextlib.ExitStack()
    pall = es.enter_context(nc.psum_tensor("pall", [128, 4096], F32)).ap()
    sems = {s: es.enter_context(nc.semaphore(s))
            for s in ("ld", "ld2", "ld3", "ld4", "ld5", "st", "st2", "st3",
                      "st4", "mm", "vv", "aa", "gg", "bar")}
    P = Plan(nc, sems)

    def bank(i, ncols=512, row0=0, nrow=128):
        return pall[row0:row0 + nrow, 512 * i: 512 * i + ncols]

    P.load(ident[:], inp["c_ident"][:])
    P.load(cmisc[:], inp["c_misc"][:])
    P.load(scal1[0:1, 0:1], inp["L1_ffn_input_scale"][None, :])
    P.load(scal1[0:1, 1:2], inp["L1_ffn_weight_scale"][None, :])
    for nm, off in BIAS_IDX.items():
        P.op("sync", lambda h, d=bias_sb[:, off:off + 8],
             s=inp[nm].rearrange("(c p) -> p c", p=128):
             h.dma_start(out=d, in_=s, allow_slow_non_contiguous=True),
             inc="ld", n=16)
    ld0 = P.cnt["ld"]
    P.v(lambda h: h.memset(ones[:], 1.0))
    P.v(lambda h: h.memset(ones128[:], 1.0), inc="vv")
    P.v(lambda h: h.reciprocal(scal1[0:1, 2:3], scal1[0:1, 0:1]),
        wait=[("ld", ld0)], inc="vv")
    vv0 = P.cnt["vv"]
    P.mm(pall[0:128, 0:1], ones[:], scal1[0:1, 2:3], True, True,
         wait=[("vv", vv0)])
    P.mm(pall[0:128, 512:513], ones[:], scal1[0:1, 1:2], True, True)
    P.mm(pall[0:128, 1024:1025], ones[:], scal1[0:1, 0:1], True, True, inc="mm")
    mm0 = P.cnt["mm"]
    P.v(lambda h: h.tensor_copy(invis[:], pall[0:128, 0:1]), wait=[("mm", mm0)])
    P.v(lambda h: h.tensor_copy(wscol[:], pall[0:128, 512:513]))
    P.v(lambda h: h.tensor_copy(iscol[:], pall[0:128, 1024:1025]), inc="vv")

    rm_pack = AR_B[:, 0:4096]
    rup_packT = AR_B[:, 4096:8192]
    P.v(lambda h: h.memset(rm_pack, 0.0))
    P.v(lambda h: h.memset(rup_packT, 0.0), inc="vv")
    vz = P.cnt["vv"]
    rmf = inp["L0_ffn_rail_maker"].rearrange("h r i j -> (h r i) j")
    rupf = inp["L0_ffn_rail_up_proj"].rearrange("h r k j -> (h r) k j")
    rup_in = AR_A[:, 4096:4224]
    rup_stage = AR_A[:, 4224:4288]

    def emit_rails_group(g8):
        src = rupf[8 * g8:8 * g8 + 8]
        ldv = P.op("gpsimd", lambda h, s=src, d=rup_in:
                   h.dma_start(out=d[0:64, :].rearrange("k (u j) -> k u j", j=16),
                               in_=s.transpose([1, 0, 2])),
                   wait=[("mm", P.cnt["mm"]), ("vv", vz)], inc="ld3", n=16)
        mv = P.trans(pall[0:128, 2048:2112], rup_in[0:64, :], ident[0:64, 0:64],
                     wait=[("ld3", ldv), ("st", P.cnt["st"])], inc="mm")
        ev = P.v(lambda h: h.tensor_copy(rup_stage, pall[0:128, 2048:2112]),
                 wait=[("mm", mv), ("st", P.cnt["st"])], inc="vv")
        for u in range(8):
            cdst = 128 * g8 + 64 * (u % 2)
            P.store(rup_packT[16 * u:16 * u + 16, cdst:cdst + 64],
                    rup_stage[16 * u:16 * u + 16, :],
                    wait=[("vv", ev), ("ld3", P.cnt["ld3"])] if u == 0 else ())

    xin_t = AR_A[:, 4352:5376]
    fm_t = AR_A[:, 5376:6400]
    sq_t0 = AR_A[:, 6400:7424]
    p0_state = {"mm_sq_prev": 0, "st2_prev": 0, "scp": 0}

    def emit_p0_tt(tt):
        lv = P.op("sync", lambda h, d=xin_t,
                  s_=inp["L0_attn_x"][128 * tt:128 * tt + 128, :]:
                  h.dma_start(out=d, in_=s_),
                  wait=[("mm", P.cnt["mm"])], inc="ld4", n=16)
        for ch in range(NCH):
            mv = P.trans(pall[:, 2560:2688],
                         xin_t[:, 128 * ch:128 * ch + 128],
                         ident[:], wait=[("ld4", lv), ("vv", P.cnt["vv"])],
                         inc="mm")
            ev = P.v(lambda h, ch=ch: h.tensor_copy(
                fm_t[:, 128 * ch:128 * ch + 128], pall[:, 2560:2688]),
                wait=[("mm", mv), ("st3", p0_state["st2_prev"])]
                if ch == 0 else [("mm", mv)], inc="vv")
            sv = P.act(sq_t0[:, 128 * ch:128 * ch + 128],
                       fm_t[:, 128 * ch:128 * ch + 128], AF.Square,
                       wait=[("vv", ev), ("mm", p0_state["mm_sq_prev"])],
                       inc="aa")
            c7a = 3072 + 128 * (tt % 2)
            c7b = 3584 + 128 * (tt % 2)
            P.mm(pall[0:1, c7a:c7a + 128], ones128[:],
                 fm_t[:, 128 * ch:128 * ch + 128], ch == 0, ch == NCH - 1,
                 wait=[("vv", ev), ("vv", p0_state["scp"])] if ch == 0
                 else [("vv", ev)])
            P.mm(pall[0:1, c7b:c7b + 128], ones128[:],
                 sq_t0[:, 128 * ch:128 * ch + 128], ch == 0, ch == NCH - 1,
                 wait=[("aa", sv)], inc="mm" if ch == NCH - 1 else None)
        p0_state["st2_prev"] = P.store(
            xs_scr[0].transpose([1, 0, 2])[:, :, 128 * tt:128 * tt + 128],
            fm_t.rearrange("p (c t) -> p c t", t=128),
            wait=[("vv", P.cnt["vv"])], sem="st3")
        mvs = P.cnt["mm"]
        P.v(lambda h, tt=tt, c7a=3072 + 128 * (tt % 2): h.tensor_copy(
            Svec[0:1, 128 * tt:128 * tt + 128], pall[0:1, c7a:c7a + 128]),
            wait=[("mm", mvs)])
        P.v(lambda h, tt=tt, c7b=3584 + 128 * (tt % 2): h.tensor_copy(
            S2vec[0:1, 128 * tt:128 * tt + 128], pall[0:1, c7b:c7b + 128]),
            inc="vv")
        p0_state["mm_sq_prev"] = P.cnt["mm"]
        p0_state["scp"] = P.cnt["vv"]

    wc = inp["L0_ffn_w_cproj"]
    wc_stages = [AR_W[:, 0:8192], AR_W[:, 8192:16384], AR_W[:, 16384:24576]]
    wcT_tiles = [AR_A[:, 0:2048], AR_A[:, 2048:4096], AR_A[:, 7424:9472]]
    wcT_sch = ("st", "st2", "st4")
    st_cgrp = {}
    lv_h = {}
    mm_h = {}
    for oh in range(16):
        o, hf = oh // 2, oh % 2
        sem_o = ("ld", "ld2", "ld5")[oh % 3]
        if oh < 3:
            P.fence("sync", sem_o)
        stg = wc_stages[oh % 3]
        lv = 0
        for c4 in range(4):
            lv = P.op("sync", lambda h, d=stg[:, 2048 * c4:2048 * c4 + 2048],
                      s_=wc[128 * o:128 * o + 128,
                            8192 * hf + 2048 * c4:8192 * hf + 2048 * c4 + 2048]:
                      h.dma_start(out=d, in_=s_),
                      wait=[("mm", mm_h.get(oh - 3, 0))] if c4 == 0 else (),
                      inc=sem_o, n=16)
        lv_h[oh] = lv
        for cgrp in range(4):
            g = 4 * oh + cgrp
            wt = wcT_tiles[g % 3]
            sch = wcT_sch[g % 3]
            for q4 in range(4):
                mv = None
                for t4 in range(4):
                    t16 = 4 * q4 + t4
                    pb = (2 * (g % 2)) + (q4 % 2)
                    mv = P.trans(pall[:, 512 * pb + 128 * t4:
                                      512 * pb + 128 * t4 + 128],
                                 stg[:, 128 * (16 * cgrp + t16):
                                     128 * (16 * cgrp + t16) + 128], ident[:],
                                 wait=[(sem_o, lv), ("vv", max(0, P.cnt["vv"] - 1))]
                                 if t4 == 0 else (), inc="mm" if t4 == 3 else None)
                mv = P.cnt["mm"]
                P.v(lambda h, q4=q4, wt=wt, pb=(2 * (g % 2)) + (q4 % 2):
                    h.tensor_copy(wt[:, 512 * q4:512 * q4 + 512],
                                  pall[:, 512 * pb:512 * pb + 512]),
                    wait=[("mm", mv), (sch, st_cgrp.get(g - 3, 0))], inc="vv")
            sv = 0
            for e2 in range(2):
                sv = P.store(
                    wcT_scr[64 * hf + 16 * cgrp + 8 * e2:
                            64 * hf + 16 * cgrp + 8 * e2 + 8, o]
                    .transpose([1, 0, 2]),
                    wt[:, 1024 * e2:1024 * e2 + 1024]
                    .rearrange("p (x d) -> p x d", d=128),
                    wait=[("vv", P.cnt["vv"]),
                          (sch, st_cgrp.get(g - 3, 0))] if e2 == 0 else (),
                    sem=sch)
            st_cgrp[g] = sv
        mm_h[oh] = P.cnt["mm"]
        emit_rails_group(2 * oh)
        emit_rails_group(2 * oh + 1)
        emit_p0_tt(oh)
    for l in range(8):
        P.op("gpsimd", lambda h, l=l,
             d=rm_pack[16 * l:16 * l + 16, :]
             .rearrange("p (pk b) -> p pk b", b=128)[:, :, 16 * l:16 * l + 16],
             s_=rmf.rearrange("(pk l p) j -> l p pk j", pk=32, l=8, p=16)[l]:
             h.dma_start(out=d, in_=s_),
             wait=[("vv", vz)] if l == 0 else (),
             inc="ld3", n=16)
    P.fence("gpsimd", "ld3")
    P.store(rails_scr[:, 0:4096], rm_pack, wait=[("ld3", P.cnt["ld3"])])
    P.store(rails_scr[:, 4096:8192], rup_packT, wait=[("st", P.cnt["st"])])
    P.bar()

    WT_IDX = {"L1_attn_Wq": 0, "L1_attn_Wk": 1, "L1_attn_Wv": 2,
              "L1_attn_Wo": 3, "L1_ffn_weight_fp8": 4}
    wtr_tmp4 = [AR_B[:, 2048 * j:2048 * j + 2048] for j in range(4)]
    wtr_dst4 = [AR_W[:, 16384 + 2048 * j:16384 + 2048 * j + 2048]
                for j in range(4)]
    wtr_lch = ("ld", "ld2", "ld5", "ld4")
    wtr_sch = ("st", "st2", "st4", "st3")
    tr_done = {}
    stw_done = {}
    qn = 0
    for wname, wi in WT_IDX.items():
        for qq in range(4):
            par = qn % 4
            lch = wtr_lch[par]
            sch = wtr_sch[par]
            tmp = wtr_tmp4[par]
            dst = wtr_dst4[par]
            ss = inp[wname].rearrange("(r p) d -> p r d", p=128)[
                :, 2 * qq:2 * qq + 2]
            lv = P.op("sync",
                      lambda h, d=tmp.rearrange("p (r d) -> p r d", d=1024),
                      s_=ss: h.dma_start(out=d, in_=s_),
                      wait=[("mm", tr_done.get(qn - 4, 0))], inc=lch, n=16)
            ev = 0
            for rr in range(2):
                for i in range(8):
                    pb = 4 + ((rr * 8 + i) % 4)
                    mv = P.trans(pall[:, 512 * pb:512 * pb + 128],
                                 tmp[:, 1024 * rr + 128 * i:
                                     1024 * rr + 128 * i + 128],
                                 ident[:],
                                 wait=[(lch, lv),
                                       ("vv", max(0, P.cnt["vv"] - 3))] +
                                 ([(sch, stw_done.get(qn - 4, 0))]
                                  if (rr == 0 and i == 0) else []),
                                 inc="mm")
                    ev = P.v(lambda h, rr=rr, i=i, pb=pb, dst=dst:
                             h.tensor_copy(
                                 dst[:, 256 * i + 128 * rr:
                                     256 * i + 128 * rr + 128],
                                 pall[:, 512 * pb:512 * pb + 128]),
                             wait=[("mm", mv)], inc="vv")
            tr_done[qn] = P.cnt["mm"]
            stw_done[qn] = P.store(
                wT_scr[wi].rearrange("p (i b) -> p i b", b=1024)
                [:, :, 256 * qq:256 * qq + 256],
                dst.rearrange("p (i c) -> p i c", c=256),
                wait=[("vv", ev)], sem=sch)
            qn += 1
    P.bar()

    def ln_affine(ln_id=None):
        P.v(lambda h: h.tensor_single_scalar(Svec, Svec, 1.0 / D, ALU.mult))
        P.v(lambda h: h.tensor_single_scalar(S2vec, S2vec, 1.0 / D, ALU.mult))
        P.v(lambda h: h.tensor_mul(Bvec, Svec, Svec))
        P.v(lambda h: h.tensor_sub(S2vec, S2vec, Bvec))
        P.v(lambda h: h.tensor_single_scalar(S2vec, S2vec, EPS, ALU.add),
            inc="vv")
        va = P.cnt["vv"]
        sq = P.act(Avec, S2vec, AF.Sqrt, wait=[("vv", va)], inc="aa")
        P.v(lambda h: h.reciprocal(Avec, Avec), wait=[("aa", sq)])
        P.v(lambda h: h.tensor_mul(Bvec, Svec, Avec), inc="vv")
        return P.cnt["vv"]

    def replicate(vec_slice, pbank, dst, n):
        mv = None
        for u0 in range(0, n, 512):
            w = min(512, n - u0)
            mv = P.mm(pall[0:128, 512 * pbank + u0:512 * pbank + u0 + w],
                      ones[:], vec_slice[:, u0:u0 + w], True, True,
                      wait=[("vv", P.cnt["vv"])] if u0 == 0 else (), inc="mm")
        return P.v(lambda h, d=dst, b=pbank, n=n: h.tensor_copy(
            d, pall[0:128, 512 * b:512 * b + n]),
            wait=[("mm", mv)], inc="vv")

    def load_w_transposed(wname, w_dst):
        P.fence("sync", "ld")
        P.load(w_dst, wT_scr[WT_IDX[wname]],
               wait=[("mm", P.cnt["mm"]), ("vv", P.cnt["vv"])])
        return 0

    def phaseP1(L, x_in_scr, ln_id=None):
        WL = f"L{L}_attn_"
        w_q, w_k, w_v = AR_W[:, 0:8192], AR_W[:, 8192:16384], AR_W[:, 16384:24576]
        if L == 0:
            P.fence("sync", "ld")
            for w, n in ((w_q, "Wq"), (w_k, "Wk"), (w_v, "Wv")):
                P.load(w.rearrange("p (i d) -> p i d", d=1024),
                       inp[WL + n].rearrange("(i p) d -> p i d", p=128))
            w_ld, w_vv = P.cnt["ld"], 0
        else:
            for w, n in ((w_q, "Wq"), (w_k, "Wk"), (w_v, "Wv")):
                load_w_transposed(WL + n, w)
            w_ld, w_vv = P.cnt["ld"], P.cnt["vv"]
        ln_affine(ln_id)
        xs2 = [AR_A[:, 0:2048], AR_A[:, 2048:4096]]
        xln2 = [AR_A[:, 4096:6144], AR_B[:, 5184:7232]]
        qs = AR_A[:, 6144:8192]
        ks = AR_A[:, 8192:10240]
        vs_ = AR_B[:, 1024:3072]
        arep = AR_B[:, 0:256]
        brep = AR_B[:, 256:512]
        v_all = AR_B[:, 3072:5184]
        vt0 = AR_B[:, 3072:4096]
        lvs_s = {}
        xln_done = {}
        mm_done = {}
        st2_v = 0
        vprev = {"v": 0}

        def emit_loads(si):
            sem_s = "ld" if si % 2 == 0 else "ld2"
            if si < 2:
                P.fence("sync", sem_s)
            cols_i = slice(256 * si, 256 * si + 256)
            lv = 0
            for ch in range(NCH):
                lv = P.op("sync", lambda h,
                          d=xs2[si % 2][:, 256 * ch:256 * ch + 256],
                          s_=x_in_scr[ch][:, cols_i]:
                          h.dma_start(out=d, in_=s_),
                          wait=[("vv", xln_done.get(si - 2, 0))]
                          if ch == 0 else (), inc=sem_s, n=16)
            lvs_s[si] = lv

        def front(s_):
            cols = slice(256 * s_, 256 * s_ + 256)
            xs = xs2[s_ % 2]
            xln = xln2[s_ % 2]
            sem_s = "ld" if s_ % 2 == 0 else "ld2"
            emit_loads(s_)
            replicate(Avec[0:1, cols], 4, arep, 256)
            ev = replicate(Bvec[0:1, cols], 4, brep, 256)
            for ch in range(NCH):
                P.v(lambda h, ch=ch, xs=xs, xln=xln: h.tensor_mul(
                    xln[:, 256 * ch:256 * ch + 256],
                    xs[:, 256 * ch:256 * ch + 256], arep),
                    wait=[(sem_s, lvs_s[s_]), ("vv", ev),
                          ("mm", mm_done.get(s_ - 2, 0))]
                    if ch == 0 else ())
                P.v(lambda h, ch=ch, xs=xs, xln=xln: h.tensor_sub(
                    xln[:, 256 * ch:256 * ch + 256],
                    xln[:, 256 * ch:256 * ch + 256], brep),
                    inc="vv" if ch == NCH - 1 else None)
            xln_done[s_] = P.cnt["vv"]

        front(0)
        front(1)
        for s_ in range(8):
            cols = slice(256 * s_, 256 * s_ + 256)
            xs = xs2[s_ % 2]
            xln = xln2[s_ % 2]
            xlv = xln_done[s_]
            st_prev = P.cnt["st"]
            if L == 1:
                P.v(lambda h: h.memset(v_all[:], 0.0), wait=[("st2", st2_v)])
                P.v(lambda h: h.memset(
                    v_all.rearrange("p (x c) -> p x c", c=66)[:, :, 64:65], 1.0),
                    inc="vv")
            evs = {}
            for w_sb, dst_t, bname in ((w_q, qs, "bq"), (w_k, ks, "bk"),
                                       (w_v, vs_, "bv")):
                for o in range(NCH):
                    pb = o % 2
                    for i in range(NCH):
                        w8 = []
                        if i == 0:
                            w8 = [("vv", max(xlv, w_vv)), ("ld", w_ld),
                                  ("aa", max(0, P.cnt["aa"] - 1))]
                        P.mm(bank(pb, 256),
                             w_sb[:, 1024 * i + 128 * o:1024 * i + 128 * o + 128],
                             xln[:, 256 * i:256 * i + 256],
                             i == 0, i == NCH - 1, wait=w8,
                             inc="mm" if i == NCH - 1 else None)
                    mv = P.cnt["mm"]
                    P.act(dst_t[:, 256 * o:256 * o + 256], bank(pb, 256),
                          AF.Identity,
                          bias=bias_sb[:, BIAS_IDX[WL + bname] + o:
                                       BIAS_IDX[WL + bname] + o + 1],
                          wait=[("mm", mv), ("st", st_prev),
                                ("vv", xlv)], inc="aa")
                    if L == 1 and bname == "bv":
                        evv = P.cnt["aa"]
                        for hh in (2 * o, 2 * o + 1):
                            r0 = (hh % 2) * 64
                            for jj in range(2):
                                mv = P.trans(bank(2 + jj, 64),
                                             vs_[r0:r0 + 64,
                                                 256 * o + 128 * jj:
                                                 256 * o + 128 * jj + 128],
                                             ident[r0:r0 + 64, r0:r0 + 64],
                                             wait=[("aa", evv),
                                                   ("vv", evs.get(2 * hh + jj - 2,
                                                                  vprev["v"]))],
                                             inc="mm")
                                P.v(lambda h, hh=hh, jj=jj: h.tensor_copy(
                                    v_all[:, 132 * hh + 66 * jj:
                                          132 * hh + 66 * jj + 64],
                                    bank(2 + jj, 64)),
                                    wait=[("mm", mv)], inc="vv")
                                evs[2 * hh + jj] = P.cnt["vv"]
            mm_done[s_] = P.cnt["mm"]
            qkv_done = P.cnt["aa"]
            P.fence("gpsimd", "st")
            for ch in range(NCH):
                P.store(q_scr[ch][:, cols], qs[:, 256 * ch:256 * ch + 256],
                        wait=[("aa", qkv_done)] if ch == 0 else ())
                P.store(k_scr[ch][:, cols], ks[:, 256 * ch:256 * ch + 256])
            if L == 1:
                vprev["v"] = P.cnt["vv"]
                st2_v = P.store(
                    v_scr.transpose([1, 0, 2])[:, :, 132 * s_:132 * s_ + 132],
                    v_all.rearrange("p (x c) -> p x c", c=132),
                    wait=[("vv", evs[2 * H - 1])], sem="st2")
            else:
                cops = {}
                for jj in range(2):
                    for dv in range(NCH):
                        mv = P.trans(bank(2 + (dv % 2), 128),
                                     vs_[:, 256 * dv + 128 * jj:
                                         256 * dv + 128 * jj + 128],
                                     ident[:],
                                     wait=[("aa", qkv_done),
                                           ("vv", cops.get(8 * jj + dv - 2,
                                                           vprev["v"]))],
                                     inc="mm")
                        P.v(lambda h, dv=dv: h.tensor_copy(
                            vt0[:, 128 * dv:128 * dv + 128], bank(2 + (dv % 2), 128)),
                            wait=[("mm", mv), ("st", P.cnt["st"])], inc="vv")
                        cops[8 * jj + dv] = P.cnt["vv"]
                    P.store(v0_scr[2 * s_ + jj], vt0[:],
                            wait=[("vv", cops[8 * jj + NCH - 1])])
                vprev["v"] = P.cnt["vv"]
            if s_ + 2 < 8:
                front(s_ + 2)
        P.bar()

    def phaseP2_L0():
        kfm = AR_W[:, 0:16384]
        qh_t = AR_W[:, 16384:24576]
        probs = AR_A[:, 0:8192]
        rep_sb = AR_A[:, 8192:8704]
        attn_sb2 = [AR_A[:, 8704:9216], AR_A[:, 9216:9728]]
        vAh = [AR_B[:, 0:4096], AR_B[:, 4096:8192]]
        vch = ("ld2", "ld4")
        P.fence("sync", "ld")
        P.fence("sync", "ld2")
        P.fence("sync", "ld4")
        kld = 0
        for c in range(NCH):
            kld = P.load(kfm[:, 2048 * c:2048 * c + 2048], k_scr[c][:])
        av_g = {}
        avj3_prev = {}
        ev_q = {}
        qk_last_prev = 0
        prev_store = {0: ("st", 0), 1: ("st2", 0)}
        vld = {}

        def emit_vload(qq, g):
            if g >= 2:
                gate = av_g.get((qq, g - 2), 0)
            else:
                gate = av_g.get((qq - 1, 2 + g), 0)
            vld[(qq, g)] = P.op(
                "sync",
                lambda h, d=vAh[g % 2].rearrange("p (j e) -> p j e", e=256),
                s_=v0_scr[:, :, 256 * g:256 * g + 256].transpose([1, 0, 2]):
                h.dma_start(out=d, in_=s_),
                wait=[("mm", gate)] if gate else (), inc=vch[g % 2], n=16)

        for qq in range(4):
            qcols = slice(512 * qq, 512 * qq + 512)
            qld = 0
            for c in range(NCH):
                qld = P.load(qh_t[:, 1024 * c:1024 * c + 512],
                             q_scr[c][:, qcols],
                             wait=[("mm", qk_last_prev)] if c == 0 else ())
            emit_vload(qq, 0)
            emit_vload(qq, 1)
            expc = {}

            def emit_qk0(j, qq=qq):
                w0 = [("aa", expc.get(j - 2, 0))]
                if j <= 1:
                    w0.append(("ld", max(kld, qld)))
                for c in range(NCH):
                    P.mm(pall[:, 512 * (j % 2):512 * (j % 2) + 512],
                         kfm[:, 2048 * c + 128 * j:2048 * c + 128 * j + 128],
                         qh_t[:, 1024 * c:1024 * c + 512],
                         c == 0, c == NCH - 1, wait=w0 if c == 0 else (),
                         inc="mm" if c == NCH - 1 else None)
                mv = P.cnt["mm"]
                we = [("mm", mv)]
                if avj3_prev.get(j):
                    we.append(("mm", avj3_prev[j]))
                P.act(probs[:, 512 * j:512 * j + 512],
                      pall[:, 512 * (j % 2):512 * (j % 2) + 512],
                      AF.Exp, scale=0.125, bias=cmisc[:, 3:4],
                      wait=we, inc="aa")
                expc[j] = P.cnt["aa"]

            def emit_sum_avA(j, qq=qq):
                P.mm(pall[0:1, 1024:1536], ones128[:],
                     probs[:, 512 * j:512 * j + 512], j == 0, j == 15,
                     wait=[("aa", expc[j])], inc="mm" if j == 15 else None)
                for g in range(2):
                    for d2 in range(2):
                        w1 = []
                        if j == 0 and d2 == 0:
                            w1 = [(vch[g % 2], vld[(qq, g)]),
                                  ("vv", ev_q.get((qq - 1, 4 + 2 * g + 1), 0))]
                        P.mm(pall[:, 2048 + 512 * (2 * g + d2):
                                  2048 + 512 * (2 * g + d2) + 512],
                             vAh[g % 2][:, 256 * j + 128 * d2:
                                        256 * j + 128 * d2 + 128],
                             probs[:, 512 * j:512 * j + 512],
                             j == 0, j == 15, wait=w1,
                             inc="mm" if (j == 15 and d2 == 1) else None)
                    if j == 15:
                        av_g[(qq, g)] = P.cnt["mm"]

            emit_qk0(0)
            for j in range(1, 16):
                emit_qk0(j)
                emit_sum_avA(j - 1)
            emit_sum_avA(15)
            qk_last_prev = P.cnt["mm"]
            sum_done = P.cnt["mm"]
            P.v(lambda h: h.reciprocal(rsum[0:1, 0:512], pall[0:1, 1024:1536]),
                wait=[("mm", sum_done)])
            rv = P.cnt["vv"]
            mv = P.mm(pall[0:128, 1536:2048], ones[:], rsum[0:1, 0:512],
                      True, True, wait=[("vv", rv)], inc="mm")
            P.v(lambda h: h.tensor_copy(rep_sb, pall[0:128, 1536:2048]),
                wait=[("mm", mv)], inc="vv")
            for dv in range(4):
                sem_s = "st" if dv % 2 == 0 else "st2"
                P.v(lambda h, dv=dv: h.tensor_mul(
                    attn_sb2[dv % 2],
                    pall[:, 2048 + 512 * dv:2048 + 512 * dv + 512], rep_sb),
                    wait=[("mm", av_g[(qq, dv // 2)]), prev_store[dv % 2]],
                    inc="vv")
                ev_q[(qq, dv)] = P.cnt["vv"]
                prev_store[dv % 2] = (sem_s, P.store(
                    attn_scr[dv][:, qcols], attn_sb2[dv % 2],
                    wait=[("vv", ev_q[(qq, dv)])], sem=sem_s))
            emit_vload(qq, 2)
            emit_vload(qq, 3)
            avj3 = {}
            for g in (2, 3):
                for j in range(16):
                    for d2 in range(2):
                        w1 = []
                        if j == 0 and d2 == 0:
                            w1 = [(vch[g % 2], vld[(qq, g)]),
                                  ("vv", ev_q[(qq, 2 * (g - 2) + 1)])]
                        P.mm(pall[:, 2048 + 512 * (2 * (g - 2) + d2):
                                  2048 + 512 * (2 * (g - 2) + d2) + 512],
                             vAh[g % 2][:, 256 * j + 128 * d2:
                                        256 * j + 128 * d2 + 128],
                             probs[:, 512 * j:512 * j + 512],
                             j == 0, j == 15,
                             wait=w1, inc="mm" if d2 == 1 else None)
                    if g == 3:
                        avj3[j] = P.cnt["mm"]
                av_g[(qq, g)] = P.cnt["mm"]
            avj3_prev = avj3
            for dv in range(4, 8):
                sem_s = "st" if dv % 2 == 0 else "st2"
                g = 2 + (dv - 4) // 2
                P.v(lambda h, dv=dv: h.tensor_mul(
                    attn_sb2[dv % 2],
                    pall[:, 2048 + 512 * (dv - 4):2048 + 512 * (dv - 4) + 512],
                    rep_sb),
                    wait=[("mm", av_g[(qq, g)]), prev_store[dv % 2]],
                    inc="vv")
                ev_q[(qq, dv)] = P.cnt["vv"]
                prev_store[dv % 2] = (sem_s, P.store(
                    attn_scr[dv][:, qcols], attn_sb2[dv % 2],
                    wait=[("vv", ev_q[(qq, dv)])], sem=sem_s))
        P.bar()

    def phaseP2(L):
        kaug2 = [AR_A[0:66, 0:2048], AR_A[0:66, 8192:10240]]
        qaug2 = [AR_A[0:66, 2048:4096], AR_B[0:66, 2080:4128]]
        vh2 = [AR_B[0:128, 0:1056], AR_B[0:128, 4128:5184]]
        probs = [AR_A[:, 4096:5120], AR_A[:, 5120:6144]]
        rep_sb = AR_A[0:64, 6144:7168]
        attn_sb = AR_B[0:64, 1056:2080]
        P.v(lambda h: h.memset(qaug2[0][64:66, :], 1.0))
        P.v(lambda h: h.memset(qaug2[1][64:66, :], 1.0), inc="vv")
        lV2 = {}
        mm_head = {}
        for hh in range(H):
            kaug, qaug, vh = kaug2[hh % 2], qaug2[hh % 2], vh2[hh % 2]
            sem_h = "ld" if hh % 2 == 0 else "ld2"
            if hh < 2:
                P.fence("sync", sem_h)
            slot = hh if L == 1 else 16
            P.op("sync", lambda h, d=kaug[0:64, :],
                 s_=k_scr[hh // 2][(hh % 2) * 64:(hh % 2) * 64 + 64, :]:
                 h.dma_start(out=d, in_=s_),
                 wait=[("mm", mm_head.get(hh - 2, 0))], inc=sem_h, n=16)
            P.op("sync", lambda h, d=kaug[64:66, :],
                 s_=inp["c_alibi"][:, T * slot:T * slot + T]:
                 h.dma_start(out=d, in_=s_), inc=sem_h, n=16)
            P.op("sync", lambda h, d=qaug[0:64, :],
                 s_=q_scr[hh // 2][(hh % 2) * 64:(hh % 2) * 64 + 64, :]:
                 h.dma_start(out=d, in_=s_), inc=sem_h, n=16)
            lV = P.op("sync", lambda h, d=vh, s_=v_scr[hh][:]:
                      h.dma_start(out=d, in_=s_), inc=sem_h, n=16)
            lV2[hh] = lV
            vv_attn = P.cnt["vv"]
            exp_hist = {"e": {}, "a": {}}
            for half in range(2):
                qc = 1024 * half
                expc = {}
                avc = {}
                expc_prev = exp_hist["e"]
                avc_prev = exp_hist["a"]

                def emit_qk(j, expc_prev=expc_prev, vv_attn=vv_attn):
                    w0 = []
                    if j <= 1:
                        if half == 0:
                            w0.append((sem_h, lV))
                        if expc_prev.get(14 + j):
                            w0.append(("aa", expc_prev[14 + j]))
                        else:
                            w0.append(("vv", vv_attn))
                    if j >= 2:
                        w0 = w0 + [("aa", expc[j - 2])]
                    for u in range(2):
                        P.mm(pall[:, 1024 * (j % 2) + 512 * u:
                                  1024 * (j % 2) + 512 * u + 512],
                             kaug[:, 128 * j:128 * j + 128],
                             qaug[:, qc + 512 * u:qc + 512 * u + 512],
                             True, True, wait=w0 if u == 0 else (),
                             inc="mm" if u == 1 else None)
                    mv = P.cnt["mm"]
                    we = [("mm", mv)]
                    if j >= 2:
                        we.append(("mm", avc[j - 2]))
                    elif avc_prev.get(14 + j):
                        we.append(("mm", avc_prev[14 + j]))
                    P.act(probs[j % 2], pall[:, 1024 * (j % 2):1024 * (j % 2) + 1024],
                          AF.Exp, scale=0.125, wait=we, inc="aa")
                    expc[j] = P.cnt["aa"]

                def emit_av(j, vv_attn=vv_attn):
                    for u in range(2):
                        w1 = [("aa", expc[j])]
                        if j == 0 and vv_attn:
                            w1.append(("vv", vv_attn))
                        P.mm(pall[0:66, 2048 + 512 * u:2048 + 512 * u + 512],
                             vh[:, 66 * j:66 * j + 66],
                             probs[j % 2][:, 512 * u:512 * u + 512],
                             j == 0, j == 15,
                             wait=w1 if u == 0 else (),
                             inc="mm")
                    avc[j] = P.cnt["mm"]

                emit_qk(0)
                for j in range(1, 16):
                    emit_qk(j)
                    emit_av(j - 1)
                emit_av(15)
                P.v(lambda h: h.reciprocal(rsum[:], pall[64:65, 2048:3072]),
                    wait=[("mm", avc[15])], inc="vv")
                rv = P.cnt["vv"]
                P.mm(pall[0:64, 3072:3584], ones[0:1, 0:64], rsum[0:1, 0:512],
                     True, True, wait=[("vv", rv)])
                mv = P.mm(pall[0:64, 3584:4096], ones[0:1, 0:64],
                          rsum[0:1, 512:1024], True, True, inc="mm")
                P.v(lambda h: h.tensor_copy(rep_sb, pall[0:64, 3072:4096]),
                    wait=[("mm", mv), ("vv", P.cnt["vv"])], inc="vv")
                P.v(lambda h: h.tensor_mul(attn_sb, pall[0:64, 2048:3072], rep_sb),
                    wait=[("st", P.cnt["st"])], inc="vv")
                vv_attn = P.cnt["vv"]
                P.store(attn_scr[hh // 2][(hh % 2) * 64:(hh % 2) * 64 + 64,
                                          qc:qc + 1024],
                        attn_sb, wait=[("vv", vv_attn)])
                exp_hist["e"] = expc
                exp_hist["a"] = avc
            mm_head[hh] = P.cnt["mm"]
        P.bar()

    def phaseP3(L, x_prev, x_new, ln_next=None):
        WL = f"L{L}_attn_"
        wo = AR_W[:, 0:8192]
        if L == 0:
            P.load(wo.rearrange("p (i d) -> p i d", d=1024),
                   inp[WL + "Wo"].rearrange("(i p) d -> p i d", p=128))
            wo_vv = 0
        else:
            wo_vv = load_w_transposed(WL + "Wo", wo)
        wo_ld = P.cnt["ld"]
        attn_t = AR_A[:, 0:8192]
        y_t = AR_A[:, 8192:9216]
        xo2 = [AR_A[:, 9216:10240], AR_B[:, 2048:3072]]
        xn_t = AR_B[:, 0:1024]
        xq_t = AR_B[:, 1024:2048]
        add_done = {}
        for th in range(2):
            tc = slice(1024 * th, 1024 * th + 1024)
            mm_now = P.cnt["mm"]
            s2prev = 0
            P.fence("sync", "ld")
            P.fence("sync", "ld4")
            P.fence("sync", "ld5")
            lva = 0
            for c in range(NCH):
                lva = P.load(attn_t[:, 1024 * c:1024 * c + 1024],
                             attn_scr[c][:, tc],
                             wait=[("mm", mm_now)] if c == 0 else ())
            lxo2 = {}

            def emit_lxo(o, th=th):
                xch = "ld4" if o % 2 == 0 else "ld5"
                gate = (add_done.get((th, o - 2), 0) if o >= 2
                        else add_done.get((th - 1, 6 + o), 0))
                lxo2[o] = P.op("sync", lambda h, d=xo2[o % 2],
                               s_=x_prev[o][:, tc]: h.dma_start(out=d, in_=s_),
                               wait=[("vv", gate)], inc=xch, n=16)

            emit_lxo(0)
            emit_lxo(1)
            adds = []
            for o in range(NCH):
                pb = (o % 2) * 2
                xch = "ld4" if o % 2 == 0 else "ld5"
                for c in range(NCH):
                    w8 = [("ld", max(lva, wo_ld)), ("vv", wo_vv),
                          ("aa", max(0, P.cnt["aa"] - 1))] if c == 0 else []
                    for u in range(2):
                        P.mm(pall[:, 512 * pb + 512 * u:512 * pb + 512 * u + 512],
                             wo[:, 1024 * c + 128 * o:1024 * c + 128 * o + 128],
                             attn_t[:, 1024 * c + 512 * u:1024 * c + 512 * u + 512],
                             c == 0, c == NCH - 1,
                             wait=w8 if u == 0 else (),
                             inc="mm" if (c == NCH - 1 and u == 1) else None)
                mv = P.cnt["mm"]
                av = P.act(y_t, pall[:, 512 * pb:512 * pb + 1024], AF.Identity,
                           bias=bias_sb[:, BIAS_IDX[WL + "bo"] + o:
                                        BIAS_IDX[WL + "bo"] + o + 1],
                           wait=[("mm", mv), ("vv", P.cnt["vv"])], inc="aa")
                P.v(lambda h, o=o: h.tensor_add(xn_t, y_t, xo2[o % 2]),
                    wait=[("aa", av), (xch, lxo2[o]), ("st", P.cnt["st"])],
                    inc="vv")
                add_done[(th, o)] = P.cnt["vv"]
                if o + 2 < NCH:
                    emit_lxo(o + 2)
                xnv = P.cnt["vv"]
                P.fence("gpsimd", "st")
                P.store(x_new[o][:, tc], xn_t, wait=[("vv", xnv)])
                sv = P.act(xq_t, xn_t, AF.Square,
                           wait=[("vv", xnv), ("mm", s2prev)], inc="aa")
                for u in range(2):
                    P.mm(pall[0:1, 2048 + 512 * u:2048 + 512 * u + 512],
                         ones128[:], xn_t[:, 512 * u:512 * u + 512],
                         o == 0, o == NCH - 1, wait=[("vv", xnv)] if u == 0 else ())
                    P.mm(pall[0:1, 3072 + 512 * u:3072 + 512 * u + 512],
                         ones128[:], xq_t[:, 512 * u:512 * u + 512],
                         o == 0, o == NCH - 1,
                         wait=[("aa", sv)] if u == 0 else (),
                         inc="mm" if u == 1 else None)
                s2prev = P.cnt["mm"]
            mvs = P.cnt["mm"]
            P.v(lambda h, th=th: h.tensor_copy(
                Svec[0:1, 1024 * th:1024 * th + 1024],
                pall[0:1, 2048:3072]), wait=[("mm", mvs)])
            P.v(lambda h, th=th: h.tensor_copy(
                S2vec[0:1, 1024 * th:1024 * th + 1024],
                pall[0:1, 3072:4096]), inc="vv")
        P.bar()

    def phaseP4(x_prev, x_new, ln_id=None, ln_next=None):
        ln_affine(ln_id)
        rm4 = AR_B[:, 0:4096]
        rup4 = AR_B[:, 4096:8192]
        P.fence("sync", "ld")
        l1 = P.load(rm4, rails_scr[:, 0:4096])
        l2 = P.load(rup4, rails_scr[:, 4096:8192])
        xs = AR_A[:, 2048:4096]
        xln = AR_A[:, 4096:6144]
        arep = AR_A[:, 6144:6400]
        brep = AR_A[:, 6400:6656]
        sq_t = AR_A[:, 6656:6912]
        part0 = AR_A[:, 8192:10240]
        wct = [AR_A[:, 0:1024], AR_A[:, 1024:2048], AR_A[:, 6912:7936]]
        wct_sems = ("ld", "ld2", "ld4")
        y_sb = AR_W[:, 16384:24576]
        h_half = AR_W[:, 0:16384]
        for s in range(8):
            cols = slice(256 * s, 256 * s + 256)
            P.fence("sync", "ld")
            lvs = 0
            for ch in range(NCH):
                lvs = P.load(xs[:, 256 * ch:256 * ch + 256], x_prev[ch][:, cols],
                             wait=[("vv", P.cnt["vv"])] if ch == 0 else ())
            replicate(Avec[0:1, cols], 6, arep, 256)
            ev = replicate(Bvec[0:1, cols], 6, brep, 256)
            mm_prev = P.cnt["mm"]
            for ch in range(NCH):
                P.v(lambda h, ch=ch: h.tensor_mul(
                    xln[:, 256 * ch:256 * ch + 256],
                    xs[:, 256 * ch:256 * ch + 256], arep),
                    wait=[("ld", lvs), ("vv", ev), ("mm", mm_prev)]
                    if ch == 0 else ())
                P.v(lambda h, ch=ch: h.tensor_sub(
                    xln[:, 256 * ch:256 * ch + 256],
                    xln[:, 256 * ch:256 * ch + 256], brep),
                    inc="vv" if ch == NCH - 1 else None)
            xlv = P.cnt["vv"]
            for pk in range(32):
                w0 = [("ld", max(l1, l2)), ("vv", xlv)] if pk == 0 else \
                     [("vv", P.cnt["vv"] - 1)]
                P.mm(bank(4 + (pk % 2), 256), rm4[:, 128 * pk:128 * pk + 128],
                     xln[:, 256 * (pk % 8):256 * (pk % 8) + 256], True, True,
                     wait=w0, inc="mm")
                mv = P.cnt["mm"]
                P.v(lambda h, pk=pk: h.tensor_copy(
                    y_sb[:, 256 * pk:256 * pk + 256], bank(4 + (pk % 2), 256)),
                    wait=[("mm", mv)], inc="vv")
            yv = P.cnt["vv"]
            prev_done = yv
            for half in range(2):
                for ci in range(64):
                    c = 64 * half + ci
                    hh, m = c // 32, c % 32
                    g = m // 4
                    rowb = 32 * (m % 4)
                    w0 = [("vv", prev_done)] if ci == 0 else \
                         [("aa", P.cnt["aa"] - 1)]
                    P.mm(bank(4 + (ci % 2), 256, rowb if False else 0, 128),
                         rup4[rowb:rowb + 32, 128 * (c // 4):128 * (c // 4) + 128],
                         y_sb[rowb:rowb + 32,
                              256 * (8 * hh + g):256 * (8 * hh + g) + 256],
                         True, True, wait=w0, inc="mm",
                         tile_position=(rowb, 0))
                    mv = P.cnt["mm"]
                    P.act(h_half[:, 256 * ci:256 * ci + 256],
                          bank(4 + (ci % 2), 256), AF.Gelu,
                          wait=[("mm", mv)], inc="aa")
                hv = P.cnt["aa"]
                ldw = {}
                mmc = {}
                for ci in range(64):
                    c = 64 * half + ci
                    sem_c = wct_sems[ci % 3]
                    wl = [("aa", hv), ("vv", P.cnt["vv"]),
                          ("mm", P.cnt["mm"])] if ci == 0 else \
                         ([("mm", mmc[ci - 3])] if ci >= 3 else
                          [("mm", P.cnt["mm"] - NCH)])
                    if ci < 3:
                        P.fence("sync", sem_c)
                    ldw[ci] = P.op(
                        "sync",
                        lambda h, d=wct[ci % 3].rearrange("p (o d) -> p o d", d=128),
                        s_=wcT_scr[c].rearrange("o p d -> p o d"):
                        h.dma_start(out=d, in_=s_),
                        wait=wl, inc=sem_c, n=16)
                    for o in range(NCH):
                        P.mm(pall[:, 512 * o:512 * o + 256],
                             wct[ci % 3][:, 128 * o:128 * o + 128],
                             h_half[:, 256 * ci:256 * ci + 256],
                             ci == 0, ci == 63,
                             wait=[(sem_c, ldw[ci]), ("aa", hv)] if o == 0 else (),
                             inc="mm" if o == NCH - 1 else None)
                    mmc[ci] = P.cnt["mm"]
                mv = mmc[63]
                if half == 0:
                    for o in range(NCH):
                        P.v(lambda h, o=o: h.tensor_copy(
                            part0[:, 256 * o:256 * o + 256],
                            pall[:, 512 * o:512 * o + 256]),
                            wait=[("mm", mv), ("st", P.cnt["st"])] if o == 0 else (),
                            inc="vv" if o == NCH - 1 else None)
                    prev_done = P.cnt["vv"]
                else:
                    st_prev = P.cnt["st"]
                    for o in range(NCH):
                        P.v(lambda h, o=o: h.tensor_add(
                            part0[:, 256 * o:256 * o + 256],
                            pall[:, 512 * o:512 * o + 256],
                            part0[:, 256 * o:256 * o + 256]),
                            wait=[("mm", mv)] if o == 0 else ())
                        P.v(lambda h, o=o: h.tensor_add(
                            part0[:, 256 * o:256 * o + 256],
                            part0[:, 256 * o:256 * o + 256],
                            xs[:, 256 * o:256 * o + 256]),
                            inc="vv")
                    addv = P.cnt["vv"]
                    P.fence("gpsimd", "st")
                    for o in range(NCH):
                        P.store(x_new[o][:, cols], part0[:, 256 * o:256 * o + 256],
                                wait=[("vv", addv)] if o == 0 else ())
                    s2p4 = 0
                    for o in range(NCH):
                        sv = P.act(sq_t, part0[:, 256 * o:256 * o + 256],
                                   AF.Square, wait=[("vv", addv), ("mm", s2p4)],
                                   inc="aa")
                        P.mm(pall[0:1, 2048:2304], ones128[:],
                             part0[:, 256 * o:256 * o + 256],
                             o == 0, o == NCH - 1,
                             wait=[("vv", addv)] if o == 0 else ())
                        P.mm(pall[0:1, 3072:3328], ones128[:], sq_t,
                             o == 0, o == NCH - 1, wait=[("aa", sv)],
                             inc="mm")
                        s2p4 = P.cnt["mm"]
                    mvs = P.cnt["mm"]
                    P.v(lambda h, s=s: h.tensor_copy(
                        Svec[0:1, 256 * s:256 * s + 256],
                        pall[0:1, 2048:2304]), wait=[("mm", mvs)])
                    P.v(lambda h, s=s: h.tensor_copy(
                        S2vec[0:1, 256 * s:256 * s + 256],
                        pall[0:1, 3072:3328]), inc="vv")
        P.bar()

    def phaseP6(x_prev, ln_id=None):
        wT = AR_W[:, 0:8192]
        wt_vv = load_w_transposed("L1_ffn_weight_fp8", wT)
        ln_affine(ln_id)
        xs2 = [AR_W[:, 8192:12288], AR_W[:, 12288:16384]]
        xq2 = [AR_B[:, 0:4096], AR_B[:, 4096:8192]]
        xf = AR_W[:, 16384:20480]
        arep = AR_A[:, 2048:2560]
        brep = AR_A[:, 2560:3072]
        t_a = AR_A[:, 4096:5120]
        t_g = AR_A[:, 5120:6144]
        t_c = AR_A[:, 6144:7168]
        t_f = AR_A[:, 7168:8192]
        t_r = AR_A[:, 8192:9216]
        t_s = AR_A[:, 9216:10240]
        yo2 = [AR_A[:, 0:512], AR_A[:, 512:1024]]
        tm2 = [AR_A[:, 1024:2048], AR_A[:, 3072:4096]]
        mm_strip = {}
        qv_strip = {}
        avx_pb = {-1: {0: 0, 1: 0}}
        add_pb = {-1: {0: 0, 1: 0}}
        trans_last = {-1: P.cnt["mm"]}
        tmev_bank = {2: 0, 3: 0}
        st_ch = {"st": {-1: P.cnt["st"]}, "st2": {-1: P.cnt["st2"]}}

        def front(k):
            cols = slice(512 * k, 512 * k + 512)
            xs = xs2[k % 2]
            xq = xq2[k % 2]
            sem_k = "ld" if k % 2 == 0 else "ld2"
            if k < 2:
                P.fence("sync", sem_k)
            lvs = 0
            for ch in range(NCH):
                lvs = P.op("sync", lambda h, d=xs[:, 512 * ch:512 * ch + 512],
                           s_=x_prev[ch][:, cols]: h.dma_start(out=d, in_=s_),
                           wait=[("mm", mm_strip.get(k - 2, 0))]
                           if ch == 0 else (), inc=sem_k, n=16)
            ev = replicate(Avec[0:1, cols], 6, arep, 512)
            ev = replicate(Bvec[0:1, cols], 6, brep, 512)
            for ch in range(NCH):
                x1 = xq[:, 512 * ch:512 * ch + 512]
                xsch = xs[:, 512 * ch:512 * ch + 512]
                P.v(lambda h, a=x1, b=xsch: h.tensor_mul(a, b, arep),
                    wait=[(sem_k, lvs), ("vv", max(ev, wt_vv)),
                          ("mm", mm_strip.get(k - 2, 0))] if ch == 0 else ())
                P.v(lambda h, a=x1: h.tensor_sub(a, a, brep),
                    inc="vv" if ch == NCH - 1 else None)
            for t4 in range(4):
                x1 = xq[:, 1024 * t4:1024 * t4 + 1024]
                P.v(lambda h, a=x1: h.tensor_scalar(a, a, invis[:], 448.0,
                                                    ALU.mult, ALU.min))
                P.v(lambda h, a=x1: h.tensor_single_scalar(a, a, -448.0, ALU.max),
                    inc="vv")
                cv = P.cnt["vv"]
                P.act(t_a, x1, AF.Abs, wait=[("vv", cv)])
                sg = P.act(t_g, x1, AF.Sign, inc="aa")
                P.v(lambda h: h.tensor_single_scalar(t_f, t_a, 2.0 ** -9, ALU.max),
                    wait=[("aa", sg)], inc="vv")
                av2 = P.cnt["vv"]
                lnv = P.act(t_f, t_f, AF.Ln, wait=[("vv", av2)], inc="aa")
                P.v(lambda h: h.tensor_scalar(t_f, t_f, INV_LN2, 9.0, ALU.mult,
                                              ALU.add), wait=[("aa", lnv)])
                P.v(lambda h: h.tensor_copy(t_c, t_f))
                for b in (8.0, 4.0, 2.0, 1.0):
                    P.v(lambda h, b=b: h.tensor_single_scalar(t_r, t_f, b,
                                                              ALU.is_ge))
                    P.v(lambda h, b=b: h.scalar_tensor_tensor(t_f, t_r, -b, t_f,
                                                              ALU.mult, ALU.add))
                P.v(lambda h: h.tensor_sub(t_c, t_c, t_f))
                P.v(lambda h: h.tensor_scalar(t_c, t_c, 3.0, -12.0,
                                              ALU.max, ALU.add), inc="vv")
                ecv = P.cnt["vv"]
                P.act(t_r, t_c, AF.Exp, scale=-LN2, wait=[("vv", ecv)])
                sv2 = P.act(t_s, t_c, AF.Exp, scale=LN2, inc="aa")
                P.v(lambda h: h.tensor_mul(t_f, t_a, t_r), wait=[("aa", sv2)])
                P.v(lambda h: h.tensor_single_scalar(t_f, t_f, 0.5, ALU.add))
                P.v(lambda h: h.tensor_copy(t_c, t_f))
                for b in (16.0, 8.0, 4.0, 2.0, 1.0):
                    P.v(lambda h, b=b: h.tensor_single_scalar(t_r, t_f, b,
                                                              ALU.is_ge))
                    P.v(lambda h, b=b: h.scalar_tensor_tensor(t_f, t_r, -b, t_f,
                                                              ALU.mult, ALU.add))
                P.v(lambda h: h.tensor_sub(t_c, t_c, t_f))
                P.v(lambda h: h.scalar_tensor_tensor(t_c, t_c, iscol[:], t_s,
                                                     ALU.mult, ALU.mult))
                P.v(lambda h, a=x1: h.tensor_mul(a, t_c, t_g), inc="vv")
            qv_strip[k] = P.cnt["vv"]

        def back(k):
            xs = xs2[k % 2]
            xq = xq2[k % 2]
            avx_pb[k] = dict(avx_pb[k - 1])
            add_pb[k] = dict(add_pb[k - 1])
            for o in range(NCH):
                pb = o % 2
                for i in range(NCH):
                    P.mm(pall[:, 512 * pb:512 * pb + 512],
                         wT[:, 1024 * i + 128 * o:1024 * i + 128 * o + 128],
                         xq[:, 512 * i:512 * i + 512],
                         i == 0, i == NCH - 1,
                         wait=[("vv", qv_strip[k]),
                               ("aa", avx_pb[k][pb])] if i == 0 else (),
                         inc="mm" if i == NCH - 1 else None)
                mv = P.cnt["mm"]
                yo = yo2[pb]
                avx = P.act(yo, pall[:, 512 * pb:512 * pb + 512], AF.Identity,
                            scale=wscol[:],
                            bias=bias_sb[:, BIAS_IDX["L1_ffn_bias"] + o:
                                         BIAS_IDX["L1_ffn_bias"] + o + 1],
                            wait=[("mm", mv), ("gg", add_pb[k][pb])], inc="aa")
                avx_pb[k][pb] = avx
                P.op("gpsimd", lambda h, o=o, xs=xs, yo=yo: h.tensor_add(
                    xf[:, 512 * o:512 * o + 512], yo,
                    xs[:, 512 * o:512 * o + 512]),
                    wait=[("aa", avx)] + ([("mm", trans_last[k - 1])]
                                          if o == 0 else []))
                add_pb[k][pb] = P.cnt["gg"]
            adds_done = P.cnt["gg"]
            for tb in range(4):
                tmb = tm2[tb % 2]
                sch = "st" if tb % 2 == 0 else "st2"
                stw = st_ch[sch].get((k, tb - 2), st_ch[sch][-1])
                evs = []
                for ch in range(NCH):
                    bnk = 2 + ch % 2
                    mvt = P.trans(bank(bnk, 128),
                                  xf[:, 512 * ch + 128 * tb:
                                     512 * ch + 128 * tb + 128],
                                  ident[:],
                                  wait=[("gg", adds_done), (sch, stw),
                                        ("aa", tmev_bank[bnk])],
                                  inc="mm")
                    evv = P.act(tmb[:, 128 * ch:128 * ch + 128],
                                bank(bnk, 128), AF.Identity,
                                wait=[("mm", mvt), (sch, stw)], inc="aa")
                    evs.append(evv)
                    tmev_bank[bnk] = evv
                st_ch[sch][(k, tb)] = P.store(
                    out[512 * k + 128 * tb:512 * k + 128 * tb + 128, :],
                    tmb[:], wait=[("aa", evs[-1])], sem=sch)
            st_ch["st"][-1] = st_ch["st"][(k, 2)]
            st_ch["st2"][-1] = st_ch["st2"][(k, 3)]
            trans_last[k] = P.cnt["mm"]
            mm_strip[k] = P.cnt["mm"]

        for k in range(5):
            if k < 4:
                front(k)
            if k >= 1:
                back(k - 1)
        P.bar()

    chain = [
        lambda: phaseP1(0, xs_scr[0], 0),
        lambda: phaseP2_L0(),
        lambda: phaseP3(0, xs_scr[0], xs_scr[1], 1),
        lambda: phaseP4(xs_scr[1], xs_scr[2], 1, 2),
        lambda: phaseP1(1, xs_scr[2], 2),
        lambda: phaseP2(1),
        lambda: phaseP3(1, xs_scr[2], xs_scr[3], 3),
        lambda: phaseP6(xs_scr[3], 3),
    ]
    for i, ph in enumerate(chain):
        if i < upto:
            ph()

    P.raw("gpsimd", lambda h, v=P.cnt["st"]: h.wait_ge(sems["st"], v))
    P.raw("sync", lambda h, v=P.cnt["ld"]: h.wait_ge(sems["ld"], v))

    with es:
        block = es.enter_context(nc.Block())

        @block.sync
        def _(h):
            for f in P.q["sync"]:
                f(h)

        @block.gpsimd
        def _(h):
            for f in P.q["gpsimd"]:
                f(h)

        @block.vector
        def _(h):
            for f in P.q["vector"]:
                f(h)

        @block.scalar
        def _(h):
            for f in P.q["scalar"]:
                f(h)

        @block.tensor
        def _(h):
            for f in P.q["tensor"]:
                f(h)
    return nc


def make_consts():
    ident = np.eye(128, dtype=np.float32)
    alibi = np.zeros((2, 17 * T), dtype=np.float32)
    slopes = np.power(2.0, np.linspace(-4.0, -0.3, 16).astype(np.float64))
    k = np.arange(T, dtype=np.float64)
    for h in range(16):
        t = -8.0 * slopes[h] * k - 160.0
        hi = t.astype(np.float32)
        lo = (t - hi.astype(np.float64)).astype(np.float32)
        alibi[0, h * T:(h + 1) * T] = hi
        alibi[1, h * T:(h + 1) * T] = lo
    alibi[0, 16 * T:17 * T] = -160.0
    cmisc = np.zeros((128, 8), dtype=np.float32)
    cmisc[:, 0] = EPS
    cmisc[:, 1] = 12.0 * LN2
    cmisc[:, 2] = -12.0 * LN2
    cmisc[:, 3] = -20.0
    return {"c_ident": ident, "c_alibi": alibi, "c_misc": cmisc}


def kernel(**inputs):
    consts = make_consts()
    nc = build()
    x = np.ascontiguousarray(np.asarray(inputs["L0_attn_x"], dtype=np.float32))
    in_maps = []
    for c in range(8):
        m = {k: np.ascontiguousarray(np.asarray(v, dtype=np.float32))
             for k, v in inputs.items() if k != "L0_attn_x"}
        m["L0_attn_x"] = x[c]
        m.update(consts)
        in_maps.append(m)
    res = run_bass_kernel_spmd(nc, in_maps, core_ids=list(range(8)))
    return np.stack([np.asarray(r["out"]) for r in res.results], axis=0)


# version 22
# speedup vs baseline: 1.0706x; 1.0300x over previous
import contextlib
import numpy as np
import concourse.bass as bass
import concourse.mybir as mybir
from concourse.bass_utils import run_bass_kernel_spmd

F32 = mybir.dt.float32
AF = mybir.ActivationFunctionType
ALU = mybir.AluOpType

LN2 = float(np.log(2.0))
INV_LN2 = float(1.0 / np.log(2.0))

T = 2048
D = 1024
NCH = 8
H = 16
HD = 64
DZ = 16384
EPS = 1e-5

ENGS = ("sync", "gpsimd", "vector", "scalar", "tensor")


class Plan:
    def __init__(self, nc, sems):
        self.nc = nc
        self.sems = sems
        self.cnt = {k: 0 for k in sems}
        self.q = {e: [] for e in ENGS}
        self.nbar = 0

    AUTO_SER = {"vector": "vv", "scalar": "aa", "gpsimd": "gg"}

    def op(self, eng, fn, wait=(), inc=None, n=1):
        ser = self.AUTO_SER.get(eng)
        if ser is not None:
            wait = tuple(wait) + ((ser, self.cnt[ser]),)
            if inc is None:
                inc = ser
                n = 1
        wait = tuple((s, v) for (s, v) in wait if v and v > 0)
        if inc is not None:
            self.cnt[inc] += n
        sems = self.sems

        def run(h, fn=fn, wait=wait, inc=inc, n=n):
            for s, v in wait:
                h.wait_ge(sems[s], v)
            inst = fn(h)
            if inc is not None:
                inst.then_inc(sems[inc], n)
        self.q[eng].append(run)
        return self.cnt[inc] if inc is not None else None

    def raw(self, eng, fn):
        self.q[eng].append(fn)

    def fence(self, eng, sem):
        v = self.cnt[sem]
        if v:
            sems = self.sems
            self.raw(eng, lambda h, v=v, s=sem: h.wait_ge(sems[s], v))

    def bar(self):
        self.nbar += 1
        tgt = self.nbar * 5
        ld_t, st_t = self.cnt["ld"], self.cnt["st"]
        ld3_t, ld4_t = self.cnt["ld3"], self.cnt["ld4"]
        ld2_t = self.cnt["ld2"]
        ld5_t = self.cnt["ld5"]
        st2_t = self.cnt["st2"]
        st3_t = self.cnt["st3"]
        st4_t = self.cnt["st4"]
        sems = self.sems
        for e in ENGS:
            def run(h, e=e, tgt=tgt, ld_t=ld_t, st_t=st_t,
                    ld3_t=ld3_t, ld4_t=ld4_t, ld2_t=ld2_t, st2_t=st2_t,
                    st3_t=st3_t, st4_t=st4_t, ld5_t=ld5_t):
                if e == "sync":
                    if ld_t:
                        h.wait_ge(sems["ld"], ld_t)
                    if ld2_t:
                        h.wait_ge(sems["ld2"], ld2_t)
                    if ld5_t:
                        h.wait_ge(sems["ld5"], ld5_t)
                if e == "gpsimd":
                    if ld3_t:
                        h.wait_ge(sems["ld3"], ld3_t)
                    if ld4_t:
                        h.wait_ge(sems["ld4"], ld4_t)
                if e == "gpsimd":
                    if st_t:
                        h.wait_ge(sems["st"], st_t)
                    if st2_t:
                        h.wait_ge(sems["st2"], st2_t)
                    if st3_t:
                        h.wait_ge(sems["st3"], st3_t)
                    if st4_t:
                        h.wait_ge(sems["st4"], st4_t)
                h.nop().then_inc(sems["bar"], 1)
                h.wait_ge(sems["bar"], tgt)
            self.q[e].append(run)

    def load(self, dst, src, wait=()):
        return self.op("sync", lambda h, d=dst, s=src: h.dma_start(out=d, in_=s),
                       wait=wait, inc="ld", n=16)

    def store(self, dst, src, wait=(), sem="st"):
        return self.op("gpsimd", lambda h, d=dst, s=src: h.dma_start(out=d, in_=s),
                       wait=wait, inc=sem, n=16)

    def mm(self, out, lhsT, rhs, start, stop, wait=(), inc=None, tile_position=None):
        return self.op(
            "tensor",
            lambda h, o=out, l=lhsT, r=rhs, a=start, b=stop, tp=tile_position:
                h.matmul(o, l, r, start=a, stop=b, tile_position=tp),
            wait=wait, inc=inc)

    def trans(self, out, in_, ident, wait=(), inc=None):
        return self.op(
            "tensor",
            lambda h, o=out, i=in_, d=ident: h.transpose(o, i, d),
            wait=wait, inc=inc)

    def act(self, out, in_, func, bias=0.0, scale=1.0, wait=(), inc=None):
        return self.op(
            "scalar",
            lambda h, o=out, i=in_, f=func, b=bias, s=scale:
                h.activation(o, i, f, bias=b, scale=s),
            wait=wait, inc=inc)

    def v(self, fn, wait=(), inc=None):
        return self.op("vector", fn, wait=wait, inc=inc)


def build(nc=None, debug=(), upto=8):
    if nc is None:
        nc = bass.Bass("TRN2", target_bir_lowering=False,
                       dynamic_dma_scratch_size=16384)

    def dk(name):
        return "ExternalOutput" if name in debug else "Internal"

    inp = {}
    names = [("L0_attn_x", [T, D])]
    for L in (0, 1):
        for w in ("Wq", "Wk", "Wv", "Wo"):
            names.append((f"L{L}_attn_{w}", [D, D]))
        for b in ("bq", "bk", "bv", "bo"):
            names.append((f"L{L}_attn_{b}", [D]))
    names += [("L0_ffn_rail_maker", [4, 64, 16, 16]),
              ("L0_ffn_rail_up_proj", [4, 64, 64, 16]),
              ("L0_ffn_w_cproj", [D, DZ]),
              ("L1_ffn_weight_fp8", [D, D]),
              ("L1_ffn_bias", [D]),
              ("L1_ffn_input_scale", [1]),
              ("L1_ffn_weight_scale", [1]),
              ("c_ident", [128, 128]),
              ("c_alibi", [2, 17 * T]),
              ("c_misc", [128, 8])]
    for nm, shp in names:
        inp[nm] = nc.dram_tensor(nm, shp, F32, kind="ExternalInput").ap()

    out = nc.dram_tensor("out", [T, D], F32, kind="ExternalOutput").ap()

    xs_scr = [nc.dram_tensor(f"x{i}_scr", [NCH, 128, T], F32,
                             kind=dk(f"x{i}_scr")).ap() for i in range(4)]
    q_scr = nc.dram_tensor("q_scr", [NCH, 128, T], F32, kind=dk("q_scr")).ap()
    k_scr = nc.dram_tensor("k_scr", [NCH, 128, T], F32, kind=dk("k_scr")).ap()
    v_scr = nc.dram_tensor("v_scr", [H, 128, 16 * 66], F32, kind=dk("v_scr")).ap()
    v0_scr = nc.dram_tensor("v0_scr", [16, 128, D], F32, kind=dk("v0_scr")).ap()
    attn_scr = nc.dram_tensor("attn_scr", [NCH, 128, T], F32,
                              kind=dk("attn_scr")).ap()
    wcT_scr = nc.dram_tensor("wcT_scr", [128, 128, NCH * 128], F32,
                             kind=dk("wcT_scr")).ap()
    rails_scr = nc.dram_tensor("rails_scr", [128, 8192], F32,
                               kind=dk("rails_scr")).ap()
    wT_scr = nc.dram_tensor("wT_scr", [5, 128, 8192], F32, kind=dk("wT_scr")).ap()

    def sbuf(name, shape):
        return nc.alloc_sbuf_tensor(name, shape, F32).ap()

    AR_W = sbuf("AR_W", [128, 24576])
    AR_A = sbuf("AR_A", [128, 10240])
    AR_B = sbuf("AR_B", [128, 8192])
    ident = sbuf("ident", [128, 128])
    ones = sbuf("ones", [1, 128])
    ones128 = sbuf("ones128", [128, 1])
    VEC = sbuf("VEC", [128, 8256])
    Avec = VEC[0:1, 0:2048]
    Bvec = VEC[0:1, 2048:4096]
    Svec = VEC[0:1, 4096:6144]
    S2vec = VEC[0:1, 6144:8192]
    rsum = VEC[0:1, 4096:5120]
    scal1 = VEC[0:1, 8192:8196]
    bias_sb = sbuf("bias_sb", [128, 9 * 8])
    cmisc = sbuf("cmisc", [128, 8])
    invis = sbuf("invis", [128, 1])
    iscol = sbuf("iscol", [128, 1])
    wscol = sbuf("wscol", [128, 1])

    BIAS_IDX = {f"L{L}_attn_{b}": 8 * (i + 4 * L)
                for L in (0, 1) for i, b in enumerate(("bq", "bk", "bv", "bo"))}
    BIAS_IDX["L1_ffn_bias"] = 64

    es = contextlib.ExitStack()
    pall = es.enter_context(nc.psum_tensor("pall", [128, 4096], F32)).ap()
    sems = {s: es.enter_context(nc.semaphore(s))
            for s in ("ld", "ld2", "ld3", "ld4", "ld5", "st", "st2", "st3",
                      "st4", "mm", "vv", "aa", "gg", "bar")}
    P = Plan(nc, sems)

    def bank(i, ncols=512, row0=0, nrow=128):
        return pall[row0:row0 + nrow, 512 * i: 512 * i + ncols]

    P.load(ident[:], inp["c_ident"][:])
    P.load(cmisc[:], inp["c_misc"][:])
    P.load(scal1[0:1, 0:1], inp["L1_ffn_input_scale"][None, :])
    P.load(scal1[0:1, 1:2], inp["L1_ffn_weight_scale"][None, :])
    for nm, off in BIAS_IDX.items():
        P.op("sync", lambda h, d=bias_sb[:, off:off + 8],
             s=inp[nm].rearrange("(c p) -> p c", p=128):
             h.dma_start(out=d, in_=s, allow_slow_non_contiguous=True),
             inc="ld", n=16)
    ld0 = P.cnt["ld"]
    P.v(lambda h: h.memset(ones[:], 1.0))
    P.v(lambda h: h.memset(ones128[:], 1.0), inc="vv")
    P.v(lambda h: h.reciprocal(scal1[0:1, 2:3], scal1[0:1, 0:1]),
        wait=[("ld", ld0)], inc="vv")
    vv0 = P.cnt["vv"]
    P.mm(pall[0:128, 0:1], ones[:], scal1[0:1, 2:3], True, True,
         wait=[("vv", vv0)])
    P.mm(pall[0:128, 512:513], ones[:], scal1[0:1, 1:2], True, True)
    P.mm(pall[0:128, 1024:1025], ones[:], scal1[0:1, 0:1], True, True, inc="mm")
    mm0 = P.cnt["mm"]
    P.v(lambda h: h.tensor_copy(invis[:], pall[0:128, 0:1]), wait=[("mm", mm0)])
    P.v(lambda h: h.tensor_copy(wscol[:], pall[0:128, 512:513]))
    P.v(lambda h: h.tensor_copy(iscol[:], pall[0:128, 1024:1025]), inc="vv")

    rm_pack = AR_B[:, 0:4096]
    rup_packT = AR_B[:, 4096:8192]
    P.v(lambda h: h.memset(rm_pack, 0.0))
    P.v(lambda h: h.memset(rup_packT, 0.0), inc="vv")
    vz = P.cnt["vv"]
    rmf = inp["L0_ffn_rail_maker"].rearrange("h r i j -> (h r i) j")
    rupf = inp["L0_ffn_rail_up_proj"].rearrange("h r k j -> (h r) k j")
    rup_in = AR_A[:, 4096:4224]
    rup_stage = AR_A[:, 4224:4288]

    def emit_rails_group(g8):
        src = rupf[8 * g8:8 * g8 + 8]
        ldv = P.op("gpsimd", lambda h, s=src, d=rup_in:
                   h.dma_start(out=d[0:64, :].rearrange("k (u j) -> k u j", j=16),
                               in_=s.transpose([1, 0, 2])),
                   wait=[("mm", P.cnt["mm"]), ("vv", vz)], inc="ld3", n=16)
        mv = P.trans(pall[0:128, 2048:2112], rup_in[0:64, :], ident[0:64, 0:64],
                     wait=[("ld3", ldv), ("st", P.cnt["st"])], inc="mm")
        ev = P.v(lambda h: h.tensor_copy(rup_stage, pall[0:128, 2048:2112]),
                 wait=[("mm", mv), ("st", P.cnt["st"])], inc="vv")
        for u in range(8):
            cdst = 128 * g8 + 64 * (u % 2)
            P.store(rup_packT[16 * u:16 * u + 16, cdst:cdst + 64],
                    rup_stage[16 * u:16 * u + 16, :],
                    wait=[("vv", ev), ("ld3", P.cnt["ld3"])] if u == 0 else ())

    xin_t = AR_A[:, 4352:5376]
    fm_t = AR_A[:, 5376:6400]
    sq_t0 = AR_A[:, 6400:7424]
    p0_state = {"mm_sq_prev": 0, "st2_prev": 0, "scp": 0}

    def emit_p0_tt(tt):
        lv = P.op("sync", lambda h, d=xin_t,
                  s_=inp["L0_attn_x"][128 * tt:128 * tt + 128, :]:
                  h.dma_start(out=d, in_=s_),
                  wait=[("mm", P.cnt["mm"])], inc="ld4", n=16)
        for ch in range(NCH):
            mv = P.trans(pall[:, 2560:2688],
                         xin_t[:, 128 * ch:128 * ch + 128],
                         ident[:], wait=[("ld4", lv), ("vv", P.cnt["vv"])],
                         inc="mm")
            ev = P.v(lambda h, ch=ch: h.tensor_copy(
                fm_t[:, 128 * ch:128 * ch + 128], pall[:, 2560:2688]),
                wait=[("mm", mv), ("st3", p0_state["st2_prev"])]
                if ch == 0 else [("mm", mv)], inc="vv")
            sv = P.act(sq_t0[:, 128 * ch:128 * ch + 128],
                       fm_t[:, 128 * ch:128 * ch + 128], AF.Square,
                       wait=[("vv", ev), ("mm", p0_state["mm_sq_prev"])],
                       inc="aa")
            c7a = 3072 + 128 * (tt % 2)
            c7b = 3584 + 128 * (tt % 2)
            P.mm(pall[0:1, c7a:c7a + 128], ones128[:],
                 fm_t[:, 128 * ch:128 * ch + 128], ch == 0, ch == NCH - 1,
                 wait=[("vv", ev), ("vv", p0_state["scp"])] if ch == 0
                 else [("vv", ev)])
            P.mm(pall[0:1, c7b:c7b + 128], ones128[:],
                 sq_t0[:, 128 * ch:128 * ch + 128], ch == 0, ch == NCH - 1,
                 wait=[("aa", sv)], inc="mm" if ch == NCH - 1 else None)
        p0_state["st2_prev"] = P.store(
            xs_scr[0].transpose([1, 0, 2])[:, :, 128 * tt:128 * tt + 128],
            fm_t.rearrange("p (c t) -> p c t", t=128),
            wait=[("vv", P.cnt["vv"])], sem="st3")
        mvs = P.cnt["mm"]
        P.v(lambda h, tt=tt, c7a=3072 + 128 * (tt % 2): h.tensor_copy(
            Svec[0:1, 128 * tt:128 * tt + 128], pall[0:1, c7a:c7a + 128]),
            wait=[("mm", mvs)])
        P.v(lambda h, tt=tt, c7b=3584 + 128 * (tt % 2): h.tensor_copy(
            S2vec[0:1, 128 * tt:128 * tt + 128], pall[0:1, c7b:c7b + 128]),
            inc="vv")
        p0_state["mm_sq_prev"] = P.cnt["mm"]
        p0_state["scp"] = P.cnt["vv"]

    wc = inp["L0_ffn_w_cproj"]
    wc_stages = [AR_W[:, 0:8192], AR_W[:, 8192:16384], AR_W[:, 16384:24576]]
    wcT_tiles = [AR_A[:, 0:2048], AR_A[:, 2048:4096], AR_A[:, 7424:9472]]
    wcT_sch = ("st", "st2", "st4")
    st_cgrp = {}
    lv_h = {}
    mm_h = {}
    for oh in range(16):
        o, hf = oh // 2, oh % 2
        sem_o = ("ld", "ld2", "ld5")[oh % 3]
        if oh < 3:
            P.fence("sync", sem_o)
        stg = wc_stages[oh % 3]
        lv = 0
        for c4 in range(4):
            lv = P.op("sync", lambda h, d=stg[:, 2048 * c4:2048 * c4 + 2048],
                      s_=wc[128 * o:128 * o + 128,
                            8192 * hf + 2048 * c4:8192 * hf + 2048 * c4 + 2048]:
                      h.dma_start(out=d, in_=s_),
                      wait=[("mm", mm_h.get(oh - 3, 0))] if c4 == 0 else (),
                      inc=sem_o, n=16)
        lv_h[oh] = lv
        for cgrp in range(4):
            g = 4 * oh + cgrp
            wt = wcT_tiles[g % 3]
            sch = wcT_sch[g % 3]
            for q4 in range(4):
                mv = None
                for t4 in range(4):
                    t16 = 4 * q4 + t4
                    pb = (2 * (g % 2)) + (q4 % 2)
                    mv = P.trans(pall[:, 512 * pb + 128 * t4:
                                      512 * pb + 128 * t4 + 128],
                                 stg[:, 128 * (16 * cgrp + t16):
                                     128 * (16 * cgrp + t16) + 128], ident[:],
                                 wait=[(sem_o, lv), ("vv", max(0, P.cnt["vv"] - 1))]
                                 if t4 == 0 else (), inc="mm" if t4 == 3 else None)
                mv = P.cnt["mm"]
                P.v(lambda h, q4=q4, wt=wt, pb=(2 * (g % 2)) + (q4 % 2):
                    h.tensor_copy(wt[:, 512 * q4:512 * q4 + 512],
                                  pall[:, 512 * pb:512 * pb + 512]),
                    wait=[("mm", mv), (sch, st_cgrp.get(g - 3, 0))], inc="vv")
            sv = 0
            for e2 in range(2):
                sv = P.store(
                    wcT_scr[64 * hf + 16 * cgrp + 8 * e2:
                            64 * hf + 16 * cgrp + 8 * e2 + 8, :,
                            128 * o:128 * o + 128]
                    .transpose([1, 0, 2]),
                    wt[:, 1024 * e2:1024 * e2 + 1024]
                    .rearrange("p (x d) -> p x d", d=128),
                    wait=[("vv", P.cnt["vv"]),
                          (sch, st_cgrp.get(g - 3, 0))] if e2 == 0 else (),
                    sem=sch)
            st_cgrp[g] = sv
        mm_h[oh] = P.cnt["mm"]
        emit_rails_group(2 * oh)
        emit_rails_group(2 * oh + 1)
        emit_p0_tt(oh)
    for l in range(8):
        P.op("gpsimd", lambda h, l=l,
             d=rm_pack[16 * l:16 * l + 16, :]
             .rearrange("p (pk b) -> p pk b", b=128)[:, :, 16 * l:16 * l + 16],
             s_=rmf.rearrange("(pk l p) j -> l p pk j", pk=32, l=8, p=16)[l]:
             h.dma_start(out=d, in_=s_),
             wait=[("vv", vz)] if l == 0 else (),
             inc="ld3", n=16)
    P.fence("gpsimd", "ld3")
    P.store(rails_scr[:, 0:4096], rm_pack, wait=[("ld3", P.cnt["ld3"])])
    P.store(rails_scr[:, 4096:8192], rup_packT, wait=[("st", P.cnt["st"])])
    P.bar()

    WT_IDX = {"L1_attn_Wq": 0, "L1_attn_Wk": 1, "L1_attn_Wv": 2,
              "L1_attn_Wo": 3, "L1_ffn_weight_fp8": 4}
    wtr_tmp4 = [AR_B[:, 2048 * j:2048 * j + 2048] for j in range(4)]
    wtr_dst4 = [AR_W[:, 16384 + 2048 * j:16384 + 2048 * j + 2048]
                for j in range(4)]
    wtr_lch = ("ld", "ld2", "ld5", "ld4")
    wtr_sch = ("st", "st2", "st4", "st3")
    tr_done = {}
    stw_done = {}
    qn = 0
    for wname, wi in WT_IDX.items():
        for qq in range(4):
            par = qn % 4
            lch = wtr_lch[par]
            sch = wtr_sch[par]
            tmp = wtr_tmp4[par]
            dst = wtr_dst4[par]
            ss = inp[wname].rearrange("(r p) d -> p r d", p=128)[
                :, 2 * qq:2 * qq + 2]
            lv = P.op("sync",
                      lambda h, d=tmp.rearrange("p (r d) -> p r d", d=1024),
                      s_=ss: h.dma_start(out=d, in_=s_),
                      wait=[("mm", tr_done.get(qn - 4, 0))], inc=lch, n=16)
            ev = 0
            for rr in range(2):
                for i in range(8):
                    pb = 4 + ((rr * 8 + i) % 4)
                    mv = P.trans(pall[:, 512 * pb:512 * pb + 128],
                                 tmp[:, 1024 * rr + 128 * i:
                                     1024 * rr + 128 * i + 128],
                                 ident[:],
                                 wait=[(lch, lv),
                                       ("vv", max(0, P.cnt["vv"] - 3))] +
                                 ([(sch, stw_done.get(qn - 4, 0))]
                                  if (rr == 0 and i == 0) else []),
                                 inc="mm")
                    ev = P.v(lambda h, rr=rr, i=i, pb=pb, dst=dst:
                             h.tensor_copy(
                                 dst[:, 256 * i + 128 * rr:
                                     256 * i + 128 * rr + 128],
                                 pall[:, 512 * pb:512 * pb + 128]),
                             wait=[("mm", mv)], inc="vv")
            tr_done[qn] = P.cnt["mm"]
            stw_done[qn] = P.store(
                wT_scr[wi].rearrange("p (i b) -> p i b", b=1024)
                [:, :, 256 * qq:256 * qq + 256],
                dst.rearrange("p (i c) -> p i c", c=256),
                wait=[("vv", ev)], sem=sch)
            qn += 1
    P.bar()

    def ln_affine(ln_id=None):
        P.v(lambda h: h.tensor_single_scalar(Svec, Svec, 1.0 / D, ALU.mult))
        P.v(lambda h: h.tensor_single_scalar(S2vec, S2vec, 1.0 / D, ALU.mult))
        P.v(lambda h: h.tensor_mul(Bvec, Svec, Svec))
        P.v(lambda h: h.tensor_sub(S2vec, S2vec, Bvec))
        P.v(lambda h: h.tensor_single_scalar(S2vec, S2vec, EPS, ALU.add),
            inc="vv")
        va = P.cnt["vv"]
        sq = P.act(Avec, S2vec, AF.Sqrt, wait=[("vv", va)], inc="aa")
        P.v(lambda h: h.reciprocal(Avec, Avec), wait=[("aa", sq)])
        P.v(lambda h: h.tensor_mul(Bvec, Svec, Avec), inc="vv")
        return P.cnt["vv"]

    def replicate(vec_slice, pbank, dst, n):
        mv = None
        for u0 in range(0, n, 512):
            w = min(512, n - u0)
            mv = P.mm(pall[0:128, 512 * pbank + u0:512 * pbank + u0 + w],
                      ones[:], vec_slice[:, u0:u0 + w], True, True,
                      wait=[("vv", P.cnt["vv"])] if u0 == 0 else (), inc="mm")
        return P.v(lambda h, d=dst, b=pbank, n=n: h.tensor_copy(
            d, pall[0:128, 512 * b:512 * b + n]),
            wait=[("mm", mv)], inc="vv")

    def load_w_transposed(wname, w_dst):
        P.fence("sync", "ld")
        P.load(w_dst, wT_scr[WT_IDX[wname]],
               wait=[("mm", P.cnt["mm"]), ("vv", P.cnt["vv"])])
        return 0

    def phaseP1(L, x_in_scr, ln_id=None):
        WL = f"L{L}_attn_"
        w_q, w_k, w_v = AR_W[:, 0:8192], AR_W[:, 8192:16384], AR_W[:, 16384:24576]
        if L == 0:
            P.fence("sync", "ld")
            for w, n in ((w_q, "Wq"), (w_k, "Wk"), (w_v, "Wv")):
                P.load(w.rearrange("p (i d) -> p i d", d=1024),
                       inp[WL + n].rearrange("(i p) d -> p i d", p=128))
            w_ld, w_vv = P.cnt["ld"], 0
        else:
            for w, n in ((w_q, "Wq"), (w_k, "Wk"), (w_v, "Wv")):
                load_w_transposed(WL + n, w)
            w_ld, w_vv = P.cnt["ld"], P.cnt["vv"]
        ln_affine(ln_id)
        xs2 = [AR_A[:, 0:2048], AR_A[:, 2048:4096]]
        xln2 = [AR_A[:, 4096:6144], AR_B[:, 5184:7232]]
        qs = AR_A[:, 6144:8192]
        ks = AR_A[:, 8192:10240]
        vs_ = AR_B[:, 1024:3072]
        arep = AR_B[:, 0:256]
        brep = AR_B[:, 256:512]
        v_all = AR_B[:, 3072:5184]
        vt0 = AR_B[:, 3072:4096]
        lvs_s = {}
        xln_done = {}
        mm_done = {}
        st2_v = 0
        vprev = {"v": 0}

        def emit_loads(si):
            sem_s = "ld" if si % 2 == 0 else "ld2"
            if si < 2:
                P.fence("sync", sem_s)
            cols_i = slice(256 * si, 256 * si + 256)
            lv = 0
            for ch in range(NCH):
                lv = P.op("sync", lambda h,
                          d=xs2[si % 2][:, 256 * ch:256 * ch + 256],
                          s_=x_in_scr[ch][:, cols_i]:
                          h.dma_start(out=d, in_=s_),
                          wait=[("vv", xln_done.get(si - 2, 0))]
                          if ch == 0 else (), inc=sem_s, n=16)
            lvs_s[si] = lv

        def front(s_):
            cols = slice(256 * s_, 256 * s_ + 256)
            xs = xs2[s_ % 2]
            xln = xln2[s_ % 2]
            sem_s = "ld" if s_ % 2 == 0 else "ld2"
            emit_loads(s_)
            replicate(Avec[0:1, cols], 4, arep, 256)
            ev = replicate(Bvec[0:1, cols], 4, brep, 256)
            for ch in range(NCH):
                P.v(lambda h, ch=ch, xs=xs, xln=xln: h.tensor_mul(
                    xln[:, 256 * ch:256 * ch + 256],
                    xs[:, 256 * ch:256 * ch + 256], arep),
                    wait=[(sem_s, lvs_s[s_]), ("vv", ev),
                          ("mm", mm_done.get(s_ - 2, 0))]
                    if ch == 0 else ())
                P.v(lambda h, ch=ch, xs=xs, xln=xln: h.tensor_sub(
                    xln[:, 256 * ch:256 * ch + 256],
                    xln[:, 256 * ch:256 * ch + 256], brep),
                    inc="vv" if ch == NCH - 1 else None)
            xln_done[s_] = P.cnt["vv"]

        front(0)
        front(1)
        for s_ in range(8):
            cols = slice(256 * s_, 256 * s_ + 256)
            xs = xs2[s_ % 2]
            xln = xln2[s_ % 2]
            xlv = xln_done[s_]
            st_prev = P.cnt["st"]
            if L == 1:
                P.v(lambda h: h.memset(v_all[:], 0.0), wait=[("st2", st2_v)])
                P.v(lambda h: h.memset(
                    v_all.rearrange("p (x c) -> p x c", c=66)[:, :, 64:65], 1.0),
                    inc="vv")
            evs = {}
            for w_sb, dst_t, bname in ((w_q, qs, "bq"), (w_k, ks, "bk"),
                                       (w_v, vs_, "bv")):
                for o in range(NCH):
                    pb = o % 2
                    for i in range(NCH):
                        w8 = []
                        if i == 0:
                            w8 = [("vv", max(xlv, w_vv)), ("ld", w_ld),
                                  ("aa", max(0, P.cnt["aa"] - 1))]
                        P.mm(bank(pb, 256),
                             w_sb[:, 1024 * i + 128 * o:1024 * i + 128 * o + 128],
                             xln[:, 256 * i:256 * i + 256],
                             i == 0, i == NCH - 1, wait=w8,
                             inc="mm" if i == NCH - 1 else None)
                    mv = P.cnt["mm"]
                    P.act(dst_t[:, 256 * o:256 * o + 256], bank(pb, 256),
                          AF.Identity,
                          bias=bias_sb[:, BIAS_IDX[WL + bname] + o:
                                       BIAS_IDX[WL + bname] + o + 1],
                          wait=[("mm", mv), ("st", st_prev),
                                ("vv", xlv)], inc="aa")
                    if L == 1 and bname == "bv":
                        evv = P.cnt["aa"]
                        for hh in (2 * o, 2 * o + 1):
                            r0 = (hh % 2) * 64
                            for jj in range(2):
                                mv = P.trans(bank(2 + jj, 64),
                                             vs_[r0:r0 + 64,
                                                 256 * o + 128 * jj:
                                                 256 * o + 128 * jj + 128],
                                             ident[r0:r0 + 64, r0:r0 + 64],
                                             wait=[("aa", evv),
                                                   ("vv", evs.get(2 * hh + jj - 2,
                                                                  vprev["v"]))],
                                             inc="mm")
                                P.v(lambda h, hh=hh, jj=jj: h.tensor_copy(
                                    v_all[:, 132 * hh + 66 * jj:
                                          132 * hh + 66 * jj + 64],
                                    bank(2 + jj, 64)),
                                    wait=[("mm", mv)], inc="vv")
                                evs[2 * hh + jj] = P.cnt["vv"]
            mm_done[s_] = P.cnt["mm"]
            qkv_done = P.cnt["aa"]
            P.fence("gpsimd", "st")
            for ch in range(NCH):
                P.store(q_scr[ch][:, cols], qs[:, 256 * ch:256 * ch + 256],
                        wait=[("aa", qkv_done)] if ch == 0 else ())
                P.store(k_scr[ch][:, cols], ks[:, 256 * ch:256 * ch + 256])
            if L == 1:
                vprev["v"] = P.cnt["vv"]
                st2_v = P.store(
                    v_scr.transpose([1, 0, 2])[:, :, 132 * s_:132 * s_ + 132],
                    v_all.rearrange("p (x c) -> p x c", c=132),
                    wait=[("vv", evs[2 * H - 1])], sem="st2")
            else:
                cops = {}
                for jj in range(2):
                    for dv in range(NCH):
                        mv = P.trans(bank(2 + (dv % 2), 128),
                                     vs_[:, 256 * dv + 128 * jj:
                                         256 * dv + 128 * jj + 128],
                                     ident[:],
                                     wait=[("aa", qkv_done),
                                           ("vv", cops.get(8 * jj + dv - 2,
                                                           vprev["v"]))],
                                     inc="mm")
                        P.v(lambda h, dv=dv: h.tensor_copy(
                            vt0[:, 128 * dv:128 * dv + 128], bank(2 + (dv % 2), 128)),
                            wait=[("mm", mv), ("st", P.cnt["st"])], inc="vv")
                        cops[8 * jj + dv] = P.cnt["vv"]
                    P.store(v0_scr[2 * s_ + jj], vt0[:],
                            wait=[("vv", cops[8 * jj + NCH - 1])])
                vprev["v"] = P.cnt["vv"]
            if s_ + 2 < 8:
                front(s_ + 2)
        P.bar()

    def phaseP2_L0():
        kfm = AR_W[:, 0:16384]
        qh_t = AR_W[:, 16384:24576]
        probs = AR_A[:, 0:8192]
        rep_sb = AR_A[:, 8192:8704]
        attn_sb2 = [AR_A[:, 8704:9216], AR_A[:, 9216:9728]]
        vAh = [AR_B[:, 0:4096], AR_B[:, 4096:8192]]
        vch = ("ld2", "ld4")
        P.fence("sync", "ld")
        P.fence("sync", "ld2")
        P.fence("sync", "ld4")
        kld = 0
        for c in range(NCH):
            kld = P.load(kfm[:, 2048 * c:2048 * c + 2048], k_scr[c][:])
        av_g = {}
        avj3_prev = {}
        ev_q = {}
        qk_last_prev = 0
        prev_store = {0: ("st", 0), 1: ("st2", 0)}
        vld = {}

        def emit_vload(qq, g):
            if g >= 2:
                gate = av_g.get((qq, g - 2), 0)
            else:
                gate = av_g.get((qq - 1, 2 + g), 0)
            vld[(qq, g)] = P.op(
                "sync",
                lambda h, d=vAh[g % 2].rearrange("p (j e) -> p j e", e=256),
                s_=v0_scr[:, :, 256 * g:256 * g + 256].transpose([1, 0, 2]):
                h.dma_start(out=d, in_=s_),
                wait=[("mm", gate)] if gate else (), inc=vch[g % 2], n=16)

        for qq in range(4):
            qcols = slice(512 * qq, 512 * qq + 512)
            qld = 0
            for c in range(NCH):
                qld = P.load(qh_t[:, 1024 * c:1024 * c + 512],
                             q_scr[c][:, qcols],
                             wait=[("mm", qk_last_prev)] if c == 0 else ())
            emit_vload(qq, 0)
            emit_vload(qq, 1)
            expc = {}

            def emit_qk0(j, qq=qq):
                w0 = [("aa", expc.get(j - 2, 0))]
                if j <= 1:
                    w0.append(("ld", max(kld, qld)))
                for c in range(NCH):
                    P.mm(pall[:, 512 * (j % 2):512 * (j % 2) + 512],
                         kfm[:, 2048 * c + 128 * j:2048 * c + 128 * j + 128],
                         qh_t[:, 1024 * c:1024 * c + 512],
                         c == 0, c == NCH - 1, wait=w0 if c == 0 else (),
                         inc="mm" if c == NCH - 1 else None)
                mv = P.cnt["mm"]
                we = [("mm", mv)]
                if avj3_prev.get(j):
                    we.append(("mm", avj3_prev[j]))
                P.act(probs[:, 512 * j:512 * j + 512],
                      pall[:, 512 * (j % 2):512 * (j % 2) + 512],
                      AF.Exp, scale=0.125, bias=cmisc[:, 3:4],
                      wait=we, inc="aa")
                expc[j] = P.cnt["aa"]

            def emit_sum_avA(j, qq=qq):
                P.mm(pall[0:1, 1024:1536], ones128[:],
                     probs[:, 512 * j:512 * j + 512], j == 0, j == 15,
                     wait=[("aa", expc[j])], inc="mm" if j == 15 else None)
                for g in range(2):
                    for d2 in range(2):
                        w1 = []
                        if j == 0 and d2 == 0:
                            w1 = [(vch[g % 2], vld[(qq, g)]),
                                  ("vv", ev_q.get((qq - 1, 4 + 2 * g + 1), 0))]
                        P.mm(pall[:, 2048 + 512 * (2 * g + d2):
                                  2048 + 512 * (2 * g + d2) + 512],
                             vAh[g % 2][:, 256 * j + 128 * d2:
                                        256 * j + 128 * d2 + 128],
                             probs[:, 512 * j:512 * j + 512],
                             j == 0, j == 15, wait=w1,
                             inc="mm" if (j == 15 and d2 == 1) else None)
                    if j == 15:
                        av_g[(qq, g)] = P.cnt["mm"]

            emit_qk0(0)
            for j in range(1, 16):
                emit_qk0(j)
                emit_sum_avA(j - 1)
            emit_sum_avA(15)
            qk_last_prev = P.cnt["mm"]
            sum_done = P.cnt["mm"]
            P.v(lambda h: h.reciprocal(rsum[0:1, 0:512], pall[0:1, 1024:1536]),
                wait=[("mm", sum_done)])
            rv = P.cnt["vv"]
            mv = P.mm(pall[0:128, 1536:2048], ones[:], rsum[0:1, 0:512],
                      True, True, wait=[("vv", rv)], inc="mm")
            P.v(lambda h: h.tensor_copy(rep_sb, pall[0:128, 1536:2048]),
                wait=[("mm", mv)], inc="vv")
            for dv in range(4):
                sem_s = "st" if dv % 2 == 0 else "st2"
                P.v(lambda h, dv=dv: h.tensor_mul(
                    attn_sb2[dv % 2],
                    pall[:, 2048 + 512 * dv:2048 + 512 * dv + 512], rep_sb),
                    wait=[("mm", av_g[(qq, dv // 2)]), prev_store[dv % 2]],
                    inc="vv")
                ev_q[(qq, dv)] = P.cnt["vv"]
                prev_store[dv % 2] = (sem_s, P.store(
                    attn_scr[dv][:, qcols], attn_sb2[dv % 2],
                    wait=[("vv", ev_q[(qq, dv)])], sem=sem_s))
            emit_vload(qq, 2)
            emit_vload(qq, 3)
            avj3 = {}
            for g in (2, 3):
                for j in range(16):
                    for d2 in range(2):
                        w1 = []
                        if j == 0 and d2 == 0:
                            w1 = [(vch[g % 2], vld[(qq, g)]),
                                  ("vv", ev_q[(qq, 2 * (g - 2) + 1)])]
                        P.mm(pall[:, 2048 + 512 * (2 * (g - 2) + d2):
                                  2048 + 512 * (2 * (g - 2) + d2) + 512],
                             vAh[g % 2][:, 256 * j + 128 * d2:
                                        256 * j + 128 * d2 + 128],
                             probs[:, 512 * j:512 * j + 512],
                             j == 0, j == 15,
                             wait=w1, inc="mm" if d2 == 1 else None)
                    if g == 3:
                        avj3[j] = P.cnt["mm"]
                av_g[(qq, g)] = P.cnt["mm"]
            avj3_prev = avj3
            for dv in range(4, 8):
                sem_s = "st" if dv % 2 == 0 else "st2"
                g = 2 + (dv - 4) // 2
                P.v(lambda h, dv=dv: h.tensor_mul(
                    attn_sb2[dv % 2],
                    pall[:, 2048 + 512 * (dv - 4):2048 + 512 * (dv - 4) + 512],
                    rep_sb),
                    wait=[("mm", av_g[(qq, g)]), prev_store[dv % 2]],
                    inc="vv")
                ev_q[(qq, dv)] = P.cnt["vv"]
                prev_store[dv % 2] = (sem_s, P.store(
                    attn_scr[dv][:, qcols], attn_sb2[dv % 2],
                    wait=[("vv", ev_q[(qq, dv)])], sem=sem_s))
        P.bar()

    def phaseP2(L):
        kaug2 = [AR_A[0:66, 0:2048], AR_A[0:66, 8192:10240]]
        qaug2 = [AR_A[0:66, 2048:4096], AR_B[0:66, 2080:4128]]
        vh2 = [AR_B[0:128, 0:1056], AR_B[0:128, 4128:5184]]
        probs = [AR_A[:, 4096:5120], AR_A[:, 5120:6144]]
        rep_sb = AR_A[0:64, 6144:7168]
        attn_sb = AR_B[0:64, 1056:2080]
        P.v(lambda h: h.memset(qaug2[0][64:66, :], 1.0))
        P.v(lambda h: h.memset(qaug2[1][64:66, :], 1.0), inc="vv")
        lV2 = {}
        mm_head = {}
        for hh in range(H):
            kaug, qaug, vh = kaug2[hh % 2], qaug2[hh % 2], vh2[hh % 2]
            sem_h = "ld" if hh % 2 == 0 else "ld2"
            if hh < 2:
                P.fence("sync", sem_h)
            slot = hh if L == 1 else 16
            P.op("sync", lambda h, d=kaug[0:64, :],
                 s_=k_scr[hh // 2][(hh % 2) * 64:(hh % 2) * 64 + 64, :]:
                 h.dma_start(out=d, in_=s_),
                 wait=[("mm", mm_head.get(hh - 2, 0))], inc=sem_h, n=16)
            P.op("sync", lambda h, d=kaug[64:66, :],
                 s_=inp["c_alibi"][:, T * slot:T * slot + T]:
                 h.dma_start(out=d, in_=s_), inc=sem_h, n=16)
            P.op("sync", lambda h, d=qaug[0:64, :],
                 s_=q_scr[hh // 2][(hh % 2) * 64:(hh % 2) * 64 + 64, :]:
                 h.dma_start(out=d, in_=s_), inc=sem_h, n=16)
            lV = P.op("sync", lambda h, d=vh, s_=v_scr[hh][:]:
                      h.dma_start(out=d, in_=s_), inc=sem_h, n=16)
            lV2[hh] = lV
            vv_attn = P.cnt["vv"]
            exp_hist = {"e": {}, "a": {}}
            for half in range(2):
                qc = 1024 * half
                expc = {}
                avc = {}
                expc_prev = exp_hist["e"]
                avc_prev = exp_hist["a"]

                def emit_qk(j, expc_prev=expc_prev, vv_attn=vv_attn):
                    w0 = []
                    if j <= 1:
                        if half == 0:
                            w0.append((sem_h, lV))
                        if expc_prev.get(14 + j):
                            w0.append(("aa", expc_prev[14 + j]))
                        else:
                            w0.append(("vv", vv_attn))
                    if j >= 2:
                        w0 = w0 + [("aa", expc[j - 2])]
                    for u in range(2):
                        P.mm(pall[:, 1024 * (j % 2) + 512 * u:
                                  1024 * (j % 2) + 512 * u + 512],
                             kaug[:, 128 * j:128 * j + 128],
                             qaug[:, qc + 512 * u:qc + 512 * u + 512],
                             True, True, wait=w0 if u == 0 else (),
                             inc="mm" if u == 1 else None)
                    mv = P.cnt["mm"]
                    we = [("mm", mv)]
                    if j >= 2:
                        we.append(("mm", avc[j - 2]))
                    elif avc_prev.get(14 + j):
                        we.append(("mm", avc_prev[14 + j]))
                    P.act(probs[j % 2], pall[:, 1024 * (j % 2):1024 * (j % 2) + 1024],
                          AF.Exp, scale=0.125, wait=we, inc="aa")
                    expc[j] = P.cnt["aa"]

                def emit_av(j, vv_attn=vv_attn):
                    for u in range(2):
                        w1 = [("aa", expc[j])]
                        if j == 0 and vv_attn:
                            w1.append(("vv", vv_attn))
                        P.mm(pall[0:66, 2048 + 512 * u:2048 + 512 * u + 512],
                             vh[:, 66 * j:66 * j + 66],
                             probs[j % 2][:, 512 * u:512 * u + 512],
                             j == 0, j == 15,
                             wait=w1 if u == 0 else (),
                             inc="mm")
                    avc[j] = P.cnt["mm"]

                emit_qk(0)
                for j in range(1, 16):
                    emit_qk(j)
                    emit_av(j - 1)
                emit_av(15)
                P.v(lambda h: h.reciprocal(rsum[:], pall[64:65, 2048:3072]),
                    wait=[("mm", avc[15])], inc="vv")
                rv = P.cnt["vv"]
                P.mm(pall[0:64, 3072:3584], ones[0:1, 0:64], rsum[0:1, 0:512],
                     True, True, wait=[("vv", rv)])
                mv = P.mm(pall[0:64, 3584:4096], ones[0:1, 0:64],
                          rsum[0:1, 512:1024], True, True, inc="mm")
                P.v(lambda h: h.tensor_copy(rep_sb, pall[0:64, 3072:4096]),
                    wait=[("mm", mv), ("vv", P.cnt["vv"])], inc="vv")
                P.v(lambda h: h.tensor_mul(attn_sb, pall[0:64, 2048:3072], rep_sb),
                    wait=[("st", P.cnt["st"])], inc="vv")
                vv_attn = P.cnt["vv"]
                P.store(attn_scr[hh // 2][(hh % 2) * 64:(hh % 2) * 64 + 64,
                                          qc:qc + 1024],
                        attn_sb, wait=[("vv", vv_attn)])
                exp_hist["e"] = expc
                exp_hist["a"] = avc
            mm_head[hh] = P.cnt["mm"]
        P.bar()

    def phaseP3(L, x_prev, x_new, ln_next=None):
        WL = f"L{L}_attn_"
        wo = AR_W[:, 0:8192]
        if L == 0:
            P.load(wo.rearrange("p (i d) -> p i d", d=1024),
                   inp[WL + "Wo"].rearrange("(i p) d -> p i d", p=128))
            wo_vv = 0
        else:
            wo_vv = load_w_transposed(WL + "Wo", wo)
        wo_ld = P.cnt["ld"]
        attn_t = AR_A[:, 0:8192]
        y_t = AR_A[:, 8192:9216]
        xo2 = [AR_A[:, 9216:10240], AR_B[:, 2048:3072]]
        xn_t = AR_B[:, 0:1024]
        xq_t = AR_B[:, 1024:2048]
        add_done = {}
        for th in range(2):
            tc = slice(1024 * th, 1024 * th + 1024)
            mm_now = P.cnt["mm"]
            s2prev = 0
            P.fence("sync", "ld")
            P.fence("sync", "ld4")
            P.fence("sync", "ld5")
            lva = 0
            for c in range(NCH):
                lva = P.load(attn_t[:, 1024 * c:1024 * c + 1024],
                             attn_scr[c][:, tc],
                             wait=[("mm", mm_now)] if c == 0 else ())
            lxo2 = {}

            def emit_lxo(o, th=th):
                xch = "ld4" if o % 2 == 0 else "ld5"
                gate = (add_done.get((th, o - 2), 0) if o >= 2
                        else add_done.get((th - 1, 6 + o), 0))
                lxo2[o] = P.op("sync", lambda h, d=xo2[o % 2],
                               s_=x_prev[o][:, tc]: h.dma_start(out=d, in_=s_),
                               wait=[("vv", gate)], inc=xch, n=16)

            emit_lxo(0)
            emit_lxo(1)
            adds = []
            for o in range(NCH):
                pb = (o % 2) * 2
                xch = "ld4" if o % 2 == 0 else "ld5"
                for c in range(NCH):
                    w8 = [("ld", max(lva, wo_ld)), ("vv", wo_vv),
                          ("aa", max(0, P.cnt["aa"] - 1))] if c == 0 else []
                    for u in range(2):
                        P.mm(pall[:, 512 * pb + 512 * u:512 * pb + 512 * u + 512],
                             wo[:, 1024 * c + 128 * o:1024 * c + 128 * o + 128],
                             attn_t[:, 1024 * c + 512 * u:1024 * c + 512 * u + 512],
                             c == 0, c == NCH - 1,
                             wait=w8 if u == 0 else (),
                             inc="mm" if (c == NCH - 1 and u == 1) else None)
                mv = P.cnt["mm"]
                av = P.act(y_t, pall[:, 512 * pb:512 * pb + 1024], AF.Identity,
                           bias=bias_sb[:, BIAS_IDX[WL + "bo"] + o:
                                        BIAS_IDX[WL + "bo"] + o + 1],
                           wait=[("mm", mv), ("vv", P.cnt["vv"])], inc="aa")
                P.v(lambda h, o=o: h.tensor_add(xn_t, y_t, xo2[o % 2]),
                    wait=[("aa", av), (xch, lxo2[o]), ("st", P.cnt["st"])],
                    inc="vv")
                add_done[(th, o)] = P.cnt["vv"]
                if o + 2 < NCH:
                    emit_lxo(o + 2)
                xnv = P.cnt["vv"]
                P.fence("gpsimd", "st")
                P.store(x_new[o][:, tc], xn_t, wait=[("vv", xnv)])
                sv = P.act(xq_t, xn_t, AF.Square,
                           wait=[("vv", xnv), ("mm", s2prev)], inc="aa")
                for u in range(2):
                    P.mm(pall[0:1, 2048 + 512 * u:2048 + 512 * u + 512],
                         ones128[:], xn_t[:, 512 * u:512 * u + 512],
                         o == 0, o == NCH - 1, wait=[("vv", xnv)] if u == 0 else ())
                    P.mm(pall[0:1, 3072 + 512 * u:3072 + 512 * u + 512],
                         ones128[:], xq_t[:, 512 * u:512 * u + 512],
                         o == 0, o == NCH - 1,
                         wait=[("aa", sv)] if u == 0 else (),
                         inc="mm" if u == 1 else None)
                s2prev = P.cnt["mm"]
            mvs = P.cnt["mm"]
            P.v(lambda h, th=th: h.tensor_copy(
                Svec[0:1, 1024 * th:1024 * th + 1024],
                pall[0:1, 2048:3072]), wait=[("mm", mvs)])
            P.v(lambda h, th=th: h.tensor_copy(
                S2vec[0:1, 1024 * th:1024 * th + 1024],
                pall[0:1, 3072:4096]), inc="vv")
        P.bar()

    def phaseP4(x_prev, x_new, ln_id=None, ln_next=None):
        ln_affine(ln_id)
        rm4 = AR_B[:, 0:4096]
        rup4 = AR_B[:, 4096:8192]
        P.fence("sync", "ld")
        l1 = P.load(rm4, rails_scr[:, 0:4096])
        l2 = P.load(rup4, rails_scr[:, 4096:8192])
        xs = AR_A[:, 2048:4096]
        xln = AR_A[:, 4096:6144]
        arep = AR_A[:, 6144:6400]
        brep = AR_A[:, 6400:6656]
        sq_t = AR_A[:, 6656:6912]
        part0 = AR_A[:, 8192:10240]
        wct = [AR_A[:, 0:1024], AR_A[:, 1024:2048], AR_A[:, 6912:7936]]
        wct_sems = ("ld", "ld2", "ld4")
        y_sb = AR_W[:, 16384:24576]
        h_half = AR_W[:, 0:16384]
        for s in range(8):
            cols = slice(256 * s, 256 * s + 256)
            P.fence("sync", "ld")
            lvs = 0
            for ch in range(NCH):
                lvs = P.load(xs[:, 256 * ch:256 * ch + 256], x_prev[ch][:, cols],
                             wait=[("vv", P.cnt["vv"])] if ch == 0 else ())
            replicate(Avec[0:1, cols], 6, arep, 256)
            ev = replicate(Bvec[0:1, cols], 6, brep, 256)
            mm_prev = P.cnt["mm"]
            for ch in range(NCH):
                P.v(lambda h, ch=ch: h.tensor_mul(
                    xln[:, 256 * ch:256 * ch + 256],
                    xs[:, 256 * ch:256 * ch + 256], arep),
                    wait=[("ld", lvs), ("vv", ev), ("mm", mm_prev)]
                    if ch == 0 else ())
                P.v(lambda h, ch=ch: h.tensor_sub(
                    xln[:, 256 * ch:256 * ch + 256],
                    xln[:, 256 * ch:256 * ch + 256], brep),
                    inc="vv" if ch == NCH - 1 else None)
            xlv = P.cnt["vv"]
            for pk in range(32):
                w0 = [("ld", max(l1, l2)), ("vv", xlv)] if pk == 0 else \
                     [("vv", P.cnt["vv"] - 1)]
                P.mm(bank(4 + (pk % 2), 256), rm4[:, 128 * pk:128 * pk + 128],
                     xln[:, 256 * (pk % 8):256 * (pk % 8) + 256], True, True,
                     wait=w0, inc="mm")
                mv = P.cnt["mm"]
                P.v(lambda h, pk=pk: h.tensor_copy(
                    y_sb[:, 256 * pk:256 * pk + 256], bank(4 + (pk % 2), 256)),
                    wait=[("mm", mv)], inc="vv")
            yv = P.cnt["vv"]
            prev_done = yv
            for half in range(2):
                for ci in range(64):
                    c = 64 * half + ci
                    hh, m = c // 32, c % 32
                    g = m // 4
                    rowb = 32 * (m % 4)
                    w0 = [("vv", prev_done)] if ci == 0 else \
                         [("aa", P.cnt["aa"] - 1)]
                    P.mm(bank(4 + (ci % 2), 256, rowb if False else 0, 128),
                         rup4[rowb:rowb + 32, 128 * (c // 4):128 * (c // 4) + 128],
                         y_sb[rowb:rowb + 32,
                              256 * (8 * hh + g):256 * (8 * hh + g) + 256],
                         True, True, wait=w0, inc="mm",
                         tile_position=(rowb, 0))
                    mv = P.cnt["mm"]
                    P.act(h_half[:, 256 * ci:256 * ci + 256],
                          bank(4 + (ci % 2), 256), AF.Gelu,
                          wait=[("mm", mv)], inc="aa")
                hv = P.cnt["aa"]
                ldw = {}
                mmc = {}
                for ci in range(64):
                    c = 64 * half + ci
                    sem_c = wct_sems[ci % 3]
                    wl = [("aa", hv), ("vv", P.cnt["vv"]),
                          ("mm", P.cnt["mm"])] if ci == 0 else \
                         ([("mm", mmc[ci - 3])] if ci >= 3 else
                          [("mm", P.cnt["mm"] - NCH)])
                    if ci < 3:
                        P.fence("sync", sem_c)
                    ldw[ci] = P.op(
                        "sync",
                        lambda h, d=wct[ci % 3],
                        s_=wcT_scr[c]:
                        h.dma_start(out=d, in_=s_),
                        wait=wl, inc=sem_c, n=16)
                    for o in range(NCH):
                        P.mm(pall[:, 512 * o:512 * o + 256],
                             wct[ci % 3][:, 128 * o:128 * o + 128],
                             h_half[:, 256 * ci:256 * ci + 256],
                             ci == 0, ci == 63,
                             wait=[(sem_c, ldw[ci]), ("aa", hv)] if o == 0 else (),
                             inc="mm" if o == NCH - 1 else None)
                    mmc[ci] = P.cnt["mm"]
                mv = mmc[63]
                if half == 0:
                    for o in range(NCH):
                        P.v(lambda h, o=o: h.tensor_copy(
                            part0[:, 256 * o:256 * o + 256],
                            pall[:, 512 * o:512 * o + 256]),
                            wait=[("mm", mv), ("st", P.cnt["st"])] if o == 0 else (),
                            inc="vv" if o == NCH - 1 else None)
                    prev_done = P.cnt["vv"]
                else:
                    st_prev = P.cnt["st"]
                    for o in range(NCH):
                        P.v(lambda h, o=o: h.tensor_add(
                            part0[:, 256 * o:256 * o + 256],
                            pall[:, 512 * o:512 * o + 256],
                            part0[:, 256 * o:256 * o + 256]),
                            wait=[("mm", mv)] if o == 0 else ())
                        P.v(lambda h, o=o: h.tensor_add(
                            part0[:, 256 * o:256 * o + 256],
                            part0[:, 256 * o:256 * o + 256],
                            xs[:, 256 * o:256 * o + 256]),
                            inc="vv")
                    addv = P.cnt["vv"]
                    P.fence("gpsimd", "st")
                    for o in range(NCH):
                        P.store(x_new[o][:, cols], part0[:, 256 * o:256 * o + 256],
                                wait=[("vv", addv)] if o == 0 else ())
                    s2p4 = 0
                    for o in range(NCH):
                        sv = P.act(sq_t, part0[:, 256 * o:256 * o + 256],
                                   AF.Square, wait=[("vv", addv), ("mm", s2p4)],
                                   inc="aa")
                        P.mm(pall[0:1, 2048:2304], ones128[:],
                             part0[:, 256 * o:256 * o + 256],
                             o == 0, o == NCH - 1,
                             wait=[("vv", addv)] if o == 0 else ())
                        P.mm(pall[0:1, 3072:3328], ones128[:], sq_t,
                             o == 0, o == NCH - 1, wait=[("aa", sv)],
                             inc="mm")
                        s2p4 = P.cnt["mm"]
                    mvs = P.cnt["mm"]
                    P.v(lambda h, s=s: h.tensor_copy(
                        Svec[0:1, 256 * s:256 * s + 256],
                        pall[0:1, 2048:2304]), wait=[("mm", mvs)])
                    P.v(lambda h, s=s: h.tensor_copy(
                        S2vec[0:1, 256 * s:256 * s + 256],
                        pall[0:1, 3072:3328]), inc="vv")
        P.bar()

    def phaseP6(x_prev, ln_id=None):
        wT = AR_W[:, 0:8192]
        wt_vv = load_w_transposed("L1_ffn_weight_fp8", wT)
        ln_affine(ln_id)
        xs2 = [AR_W[:, 8192:12288], AR_W[:, 12288:16384]]
        xq2 = [AR_B[:, 0:4096], AR_B[:, 4096:8192]]
        xf = AR_W[:, 16384:20480]
        arep = AR_A[:, 2048:2560]
        brep = AR_A[:, 2560:3072]
        t_a = AR_A[:, 4096:5120]
        t_g = AR_A[:, 5120:6144]
        t_c = AR_A[:, 6144:7168]
        t_f = AR_A[:, 7168:8192]
        t_r = AR_A[:, 8192:9216]
        t_s = AR_A[:, 9216:10240]
        yo2 = [AR_A[:, 0:512], AR_A[:, 512:1024]]
        tm2 = [AR_A[:, 1024:2048], AR_A[:, 3072:4096]]
        mm_strip = {}
        qv_strip = {}
        avx_pb = {-1: {0: 0, 1: 0}}
        add_pb = {-1: {0: 0, 1: 0}}
        trans_last = {-1: P.cnt["mm"]}
        tmev_bank = {2: 0, 3: 0}
        st_ch = {"st": {-1: P.cnt["st"]}, "st2": {-1: P.cnt["st2"]}}

        def front(k):
            cols = slice(512 * k, 512 * k + 512)
            xs = xs2[k % 2]
            xq = xq2[k % 2]
            sem_k = "ld" if k % 2 == 0 else "ld2"
            if k < 2:
                P.fence("sync", sem_k)
            lvs = 0
            for ch in range(NCH):
                lvs = P.op("sync", lambda h, d=xs[:, 512 * ch:512 * ch + 512],
                           s_=x_prev[ch][:, cols]: h.dma_start(out=d, in_=s_),
                           wait=[("mm", mm_strip.get(k - 2, 0))]
                           if ch == 0 else (), inc=sem_k, n=16)
            ev = replicate(Avec[0:1, cols], 6, arep, 512)
            ev = replicate(Bvec[0:1, cols], 6, brep, 512)
            for ch in range(NCH):
                x1 = xq[:, 512 * ch:512 * ch + 512]
                xsch = xs[:, 512 * ch:512 * ch + 512]
                P.v(lambda h, a=x1, b=xsch: h.tensor_mul(a, b, arep),
                    wait=[(sem_k, lvs), ("vv", max(ev, wt_vv)),
                          ("mm", mm_strip.get(k - 2, 0))] if ch == 0 else ())
                P.v(lambda h, a=x1: h.tensor_sub(a, a, brep),
                    inc="vv" if ch == NCH - 1 else None)
            for t4 in range(4):
                x1 = xq[:, 1024 * t4:1024 * t4 + 1024]
                P.v(lambda h, a=x1: h.tensor_scalar(a, a, invis[:], 448.0,
                                                    ALU.mult, ALU.min))
                P.v(lambda h, a=x1: h.tensor_single_scalar(a, a, -448.0, ALU.max),
                    inc="vv")
                cv = P.cnt["vv"]
                P.act(t_a, x1, AF.Abs, wait=[("vv", cv)])
                sg = P.act(t_g, x1, AF.Sign, inc="aa")
                P.v(lambda h: h.tensor_single_scalar(t_f, t_a, 2.0 ** -9, ALU.max),
                    wait=[("aa", sg)], inc="vv")
                av2 = P.cnt["vv"]
                lnv = P.act(t_f, t_f, AF.Ln, wait=[("vv", av2)], inc="aa")
                P.v(lambda h: h.tensor_scalar(t_f, t_f, INV_LN2, 9.0, ALU.mult,
                                              ALU.add), wait=[("aa", lnv)])
                P.v(lambda h: h.tensor_copy(t_c, t_f))
                for b in (8.0, 4.0, 2.0, 1.0):
                    P.v(lambda h, b=b: h.tensor_single_scalar(t_r, t_f, b,
                                                              ALU.is_ge))
                    P.v(lambda h, b=b: h.scalar_tensor_tensor(t_f, t_r, -b, t_f,
                                                              ALU.mult, ALU.add))
                P.v(lambda h: h.tensor_sub(t_c, t_c, t_f))
                P.v(lambda h: h.tensor_scalar(t_c, t_c, 3.0, -12.0,
                                              ALU.max, ALU.add), inc="vv")
                ecv = P.cnt["vv"]
                P.act(t_r, t_c, AF.Exp, scale=-LN2, wait=[("vv", ecv)])
                sv2 = P.act(t_s, t_c, AF.Exp, scale=LN2, inc="aa")
                P.v(lambda h: h.tensor_mul(t_f, t_a, t_r), wait=[("aa", sv2)])
                P.v(lambda h: h.tensor_single_scalar(t_f, t_f, 0.5, ALU.add))
                P.v(lambda h: h.tensor_copy(t_c, t_f))
                for b in (16.0, 8.0, 4.0, 2.0, 1.0):
                    P.v(lambda h, b=b: h.tensor_single_scalar(t_r, t_f, b,
                                                              ALU.is_ge))
                    P.v(lambda h, b=b: h.scalar_tensor_tensor(t_f, t_r, -b, t_f,
                                                              ALU.mult, ALU.add))
                P.v(lambda h: h.tensor_sub(t_c, t_c, t_f))
                P.v(lambda h: h.scalar_tensor_tensor(t_c, t_c, iscol[:], t_s,
                                                     ALU.mult, ALU.mult))
                P.v(lambda h, a=x1: h.tensor_mul(a, t_c, t_g), inc="vv")
            qv_strip[k] = P.cnt["vv"]

        def back(k):
            xs = xs2[k % 2]
            xq = xq2[k % 2]
            avx_pb[k] = dict(avx_pb[k - 1])
            add_pb[k] = dict(add_pb[k - 1])
            for o in range(NCH):
                pb = o % 2
                for i in range(NCH):
                    P.mm(pall[:, 512 * pb:512 * pb + 512],
                         wT[:, 1024 * i + 128 * o:1024 * i + 128 * o + 128],
                         xq[:, 512 * i:512 * i + 512],
                         i == 0, i == NCH - 1,
                         wait=[("vv", qv_strip[k]),
                               ("aa", avx_pb[k][pb])] if i == 0 else (),
                         inc="mm" if i == NCH - 1 else None)
                mv = P.cnt["mm"]
                yo = yo2[pb]
                avx = P.act(yo, pall[:, 512 * pb:512 * pb + 512], AF.Identity,
                            scale=wscol[:],
                            bias=bias_sb[:, BIAS_IDX["L1_ffn_bias"] + o:
                                         BIAS_IDX["L1_ffn_bias"] + o + 1],
                            wait=[("mm", mv), ("gg", add_pb[k][pb])], inc="aa")
                avx_pb[k][pb] = avx
                P.op("gpsimd", lambda h, o=o, xs=xs, yo=yo: h.tensor_add(
                    xf[:, 512 * o:512 * o + 512], yo,
                    xs[:, 512 * o:512 * o + 512]),
                    wait=[("aa", avx)] + ([("mm", trans_last[k - 1])]
                                          if o == 0 else []))
                add_pb[k][pb] = P.cnt["gg"]
            adds_done = P.cnt["gg"]
            for tb in range(4):
                tmb = tm2[tb % 2]
                sch = "st" if tb % 2 == 0 else "st2"
                stw = st_ch[sch].get((k, tb - 2), st_ch[sch][-1])
                evs = []
                for ch in range(NCH):
                    bnk = 2 + ch % 2
                    mvt = P.trans(bank(bnk, 128),
                                  xf[:, 512 * ch + 128 * tb:
                                     512 * ch + 128 * tb + 128],
                                  ident[:],
                                  wait=[("gg", adds_done), (sch, stw),
                                        ("aa", tmev_bank[bnk])],
                                  inc="mm")
                    evv = P.act(tmb[:, 128 * ch:128 * ch + 128],
                                bank(bnk, 128), AF.Identity,
                                wait=[("mm", mvt), (sch, stw)], inc="aa")
                    evs.append(evv)
                    tmev_bank[bnk] = evv
                st_ch[sch][(k, tb)] = P.store(
                    out[512 * k + 128 * tb:512 * k + 128 * tb + 128, :],
                    tmb[:], wait=[("aa", evs[-1])], sem=sch)
            st_ch["st"][-1] = st_ch["st"][(k, 2)]
            st_ch["st2"][-1] = st_ch["st2"][(k, 3)]
            trans_last[k] = P.cnt["mm"]
            mm_strip[k] = P.cnt["mm"]

        for k in range(5):
            if k < 4:
                front(k)
            if k >= 1:
                back(k - 1)
        P.bar()

    chain = [
        lambda: phaseP1(0, xs_scr[0], 0),
        lambda: phaseP2_L0(),
        lambda: phaseP3(0, xs_scr[0], xs_scr[1], 1),
        lambda: phaseP4(xs_scr[1], xs_scr[2], 1, 2),
        lambda: phaseP1(1, xs_scr[2], 2),
        lambda: phaseP2(1),
        lambda: phaseP3(1, xs_scr[2], xs_scr[3], 3),
        lambda: phaseP6(xs_scr[3], 3),
    ]
    for i, ph in enumerate(chain):
        if i < upto:
            ph()

    P.raw("gpsimd", lambda h, v=P.cnt["st"]: h.wait_ge(sems["st"], v))
    P.raw("sync", lambda h, v=P.cnt["ld"]: h.wait_ge(sems["ld"], v))

    with es:
        block = es.enter_context(nc.Block())

        @block.sync
        def _(h):
            for f in P.q["sync"]:
                f(h)

        @block.gpsimd
        def _(h):
            for f in P.q["gpsimd"]:
                f(h)

        @block.vector
        def _(h):
            for f in P.q["vector"]:
                f(h)

        @block.scalar
        def _(h):
            for f in P.q["scalar"]:
                f(h)

        @block.tensor
        def _(h):
            for f in P.q["tensor"]:
                f(h)
    return nc


def make_consts():
    ident = np.eye(128, dtype=np.float32)
    alibi = np.zeros((2, 17 * T), dtype=np.float32)
    slopes = np.power(2.0, np.linspace(-4.0, -0.3, 16).astype(np.float64))
    k = np.arange(T, dtype=np.float64)
    for h in range(16):
        t = -8.0 * slopes[h] * k - 160.0
        hi = t.astype(np.float32)
        lo = (t - hi.astype(np.float64)).astype(np.float32)
        alibi[0, h * T:(h + 1) * T] = hi
        alibi[1, h * T:(h + 1) * T] = lo
    alibi[0, 16 * T:17 * T] = -160.0
    cmisc = np.zeros((128, 8), dtype=np.float32)
    cmisc[:, 0] = EPS
    cmisc[:, 1] = 12.0 * LN2
    cmisc[:, 2] = -12.0 * LN2
    cmisc[:, 3] = -20.0
    return {"c_ident": ident, "c_alibi": alibi, "c_misc": cmisc}


def kernel(**inputs):
    consts = make_consts()
    nc = build()
    x = np.ascontiguousarray(np.asarray(inputs["L0_attn_x"], dtype=np.float32))
    in_maps = []
    for c in range(8):
        m = {k: np.ascontiguousarray(np.asarray(v, dtype=np.float32))
             for k, v in inputs.items() if k != "L0_attn_x"}
        m["L0_attn_x"] = x[c]
        m.update(consts)
        in_maps.append(m)
    res = run_bass_kernel_spmd(nc, in_maps, core_ids=list(range(8)))
    return np.stack([np.asarray(r["out"]) for r in res.results], axis=0)


# version 23
# speedup vs baseline: 1.0993x; 1.0268x over previous
import contextlib
import numpy as np
import concourse.bass as bass
import concourse.mybir as mybir
from concourse.bass_utils import run_bass_kernel_spmd

F32 = mybir.dt.float32
AF = mybir.ActivationFunctionType
ALU = mybir.AluOpType

LN2 = float(np.log(2.0))
INV_LN2 = float(1.0 / np.log(2.0))

T = 2048
D = 1024
NCH = 8
H = 16
HD = 64
DZ = 16384
EPS = 1e-5

ENGS = ("sync", "gpsimd", "vector", "scalar", "tensor")


class Plan:
    def __init__(self, nc, sems):
        self.nc = nc
        self.sems = sems
        self.cnt = {k: 0 for k in sems}
        self.q = {e: [] for e in ENGS}
        self.nbar = 0

    AUTO_SER = {"vector": "vv", "scalar": "aa", "gpsimd": "gg"}

    def op(self, eng, fn, wait=(), inc=None, n=1):
        ser = self.AUTO_SER.get(eng)
        if ser is not None:
            wait = tuple(wait) + ((ser, self.cnt[ser]),)
            if inc is None:
                inc = ser
                n = 1
        wait = tuple((s, v) for (s, v) in wait if v and v > 0)
        if inc is not None:
            self.cnt[inc] += n
        sems = self.sems

        def run(h, fn=fn, wait=wait, inc=inc, n=n):
            for s, v in wait:
                h.wait_ge(sems[s], v)
            inst = fn(h)
            if inc is not None:
                inst.then_inc(sems[inc], n)
        self.q[eng].append(run)
        return self.cnt[inc] if inc is not None else None

    def raw(self, eng, fn):
        self.q[eng].append(fn)

    def fence(self, eng, sem):
        v = self.cnt[sem]
        if v:
            sems = self.sems
            self.raw(eng, lambda h, v=v, s=sem: h.wait_ge(sems[s], v))

    def bar(self):
        self.nbar += 1
        tgt = self.nbar * 5
        ld_t, st_t = self.cnt["ld"], self.cnt["st"]
        ld3_t, ld4_t = self.cnt["ld3"], self.cnt["ld4"]
        ld2_t = self.cnt["ld2"]
        ld5_t = self.cnt["ld5"]
        st2_t = self.cnt["st2"]
        st3_t = self.cnt["st3"]
        st4_t = self.cnt["st4"]
        sems = self.sems
        for e in ENGS:
            def run(h, e=e, tgt=tgt, ld_t=ld_t, st_t=st_t,
                    ld3_t=ld3_t, ld4_t=ld4_t, ld2_t=ld2_t, st2_t=st2_t,
                    st3_t=st3_t, st4_t=st4_t, ld5_t=ld5_t):
                if e == "sync":
                    if ld_t:
                        h.wait_ge(sems["ld"], ld_t)
                    if ld2_t:
                        h.wait_ge(sems["ld2"], ld2_t)
                    if ld5_t:
                        h.wait_ge(sems["ld5"], ld5_t)
                if e == "gpsimd":
                    if ld3_t:
                        h.wait_ge(sems["ld3"], ld3_t)
                    if ld4_t:
                        h.wait_ge(sems["ld4"], ld4_t)
                if e == "gpsimd":
                    if st_t:
                        h.wait_ge(sems["st"], st_t)
                    if st2_t:
                        h.wait_ge(sems["st2"], st2_t)
                    if st3_t:
                        h.wait_ge(sems["st3"], st3_t)
                    if st4_t:
                        h.wait_ge(sems["st4"], st4_t)
                h.nop().then_inc(sems["bar"], 1)
                h.wait_ge(sems["bar"], tgt)
            self.q[e].append(run)

    def load(self, dst, src, wait=()):
        return self.op("sync", lambda h, d=dst, s=src: h.dma_start(out=d, in_=s),
                       wait=wait, inc="ld", n=16)

    def store(self, dst, src, wait=(), sem="st"):
        return self.op("gpsimd", lambda h, d=dst, s=src: h.dma_start(out=d, in_=s),
                       wait=wait, inc=sem, n=16)

    def mm(self, out, lhsT, rhs, start, stop, wait=(), inc=None, tile_position=None):
        return self.op(
            "tensor",
            lambda h, o=out, l=lhsT, r=rhs, a=start, b=stop, tp=tile_position:
                h.matmul(o, l, r, start=a, stop=b, tile_position=tp),
            wait=wait, inc=inc)

    def trans(self, out, in_, ident, wait=(), inc=None):
        return self.op(
            "tensor",
            lambda h, o=out, i=in_, d=ident: h.transpose(o, i, d),
            wait=wait, inc=inc)

    def act(self, out, in_, func, bias=0.0, scale=1.0, wait=(), inc=None):
        return self.op(
            "scalar",
            lambda h, o=out, i=in_, f=func, b=bias, s=scale:
                h.activation(o, i, f, bias=b, scale=s),
            wait=wait, inc=inc)

    def v(self, fn, wait=(), inc=None):
        return self.op("vector", fn, wait=wait, inc=inc)


def build(nc=None, debug=(), upto=8):
    if nc is None:
        nc = bass.Bass("TRN2", target_bir_lowering=False,
                       dynamic_dma_scratch_size=16384)

    def dk(name):
        return "ExternalOutput" if name in debug else "Internal"

    inp = {}
    names = [("L0_attn_x", [T, D])]
    for L in (0, 1):
        for w in ("Wq", "Wk", "Wv", "Wo"):
            names.append((f"L{L}_attn_{w}", [D, D]))
        for b in ("bq", "bk", "bv", "bo"):
            names.append((f"L{L}_attn_{b}", [D]))
    names += [("L0_ffn_rail_maker", [4, 64, 16, 16]),
              ("L0_ffn_rail_up_proj", [4, 64, 64, 16]),
              ("L0_ffn_w_cproj", [D, DZ]),
              ("L1_ffn_weight_fp8", [D, D]),
              ("L1_ffn_bias", [D]),
              ("L1_ffn_input_scale", [1]),
              ("L1_ffn_weight_scale", [1]),
              ("c_ident", [128, 128]),
              ("c_alibi", [2, 17 * T]),
              ("c_misc", [128, 8])]
    for nm, shp in names:
        inp[nm] = nc.dram_tensor(nm, shp, F32, kind="ExternalInput").ap()

    out = nc.dram_tensor("out", [T, D], F32, kind="ExternalOutput").ap()

    xs_scr = [nc.dram_tensor(f"x{i}_scr", [NCH, 128, T], F32,
                             kind=dk(f"x{i}_scr")).ap() for i in range(4)]
    q_scr = nc.dram_tensor("q_scr", [NCH, 128, T], F32, kind=dk("q_scr")).ap()
    k_scr = nc.dram_tensor("k_scr", [NCH, 128, T], F32, kind=dk("k_scr")).ap()
    v_scr = nc.dram_tensor("v_scr", [H, 128, 16 * 66], F32, kind=dk("v_scr")).ap()
    v0_scr = nc.dram_tensor("v0_scr", [16, 128, D], F32, kind=dk("v0_scr")).ap()
    attn_scr = nc.dram_tensor("attn_scr", [NCH, 128, T], F32,
                              kind=dk("attn_scr")).ap()
    wcT_scr = nc.dram_tensor("wcT_scr", [128, 128, NCH * 128], F32,
                             kind=dk("wcT_scr")).ap()
    rails_scr = nc.dram_tensor("rails_scr", [128, 8192], F32,
                               kind=dk("rails_scr")).ap()
    wT_scr = nc.dram_tensor("wT_scr", [5, 128, 8192], F32, kind=dk("wT_scr")).ap()

    def sbuf(name, shape):
        return nc.alloc_sbuf_tensor(name, shape, F32).ap()

    AR_W = sbuf("AR_W", [128, 24576])
    AR_A = sbuf("AR_A", [128, 10240])
    AR_B = sbuf("AR_B", [128, 8192])
    ident = sbuf("ident", [128, 128])
    ones = sbuf("ones", [1, 128])
    ones128 = sbuf("ones128", [128, 1])
    VEC = sbuf("VEC", [128, 8256])
    Avec = VEC[0:1, 0:2048]
    Bvec = VEC[0:1, 2048:4096]
    Svec = VEC[0:1, 4096:6144]
    S2vec = VEC[0:1, 6144:8192]
    rsum = VEC[0:1, 4096:5120]
    scal1 = VEC[0:1, 8192:8196]
    bias_sb = sbuf("bias_sb", [128, 9 * 8])
    cmisc = sbuf("cmisc", [128, 8])
    invis = sbuf("invis", [128, 1])
    iscol = sbuf("iscol", [128, 1])
    wscol = sbuf("wscol", [128, 1])

    BIAS_IDX = {f"L{L}_attn_{b}": 8 * (i + 4 * L)
                for L in (0, 1) for i, b in enumerate(("bq", "bk", "bv", "bo"))}
    BIAS_IDX["L1_ffn_bias"] = 64

    es = contextlib.ExitStack()
    pall = es.enter_context(nc.psum_tensor("pall", [128, 4096], F32)).ap()
    sems = {s: es.enter_context(nc.semaphore(s))
            for s in ("ld", "ld2", "ld3", "ld4", "ld5", "st", "st2", "st3",
                      "st4", "mm", "vv", "aa", "gg", "bar")}
    P = Plan(nc, sems)

    def bank(i, ncols=512, row0=0, nrow=128):
        return pall[row0:row0 + nrow, 512 * i: 512 * i + ncols]

    P.load(ident[:], inp["c_ident"][:])
    P.load(cmisc[:], inp["c_misc"][:])
    P.load(scal1[0:1, 0:1], inp["L1_ffn_input_scale"][None, :])
    P.load(scal1[0:1, 1:2], inp["L1_ffn_weight_scale"][None, :])
    for nm, off in BIAS_IDX.items():
        P.op("sync", lambda h, d=bias_sb[:, off:off + 8],
             s=inp[nm].rearrange("(c p) -> p c", p=128):
             h.dma_start(out=d, in_=s, allow_slow_non_contiguous=True),
             inc="ld", n=16)
    ld0 = P.cnt["ld"]
    P.v(lambda h: h.memset(ones[:], 1.0))
    P.v(lambda h: h.memset(ones128[:], 1.0), inc="vv")
    P.v(lambda h: h.reciprocal(scal1[0:1, 2:3], scal1[0:1, 0:1]),
        wait=[("ld", ld0)], inc="vv")
    vv0 = P.cnt["vv"]
    P.mm(pall[0:128, 0:1], ones[:], scal1[0:1, 2:3], True, True,
         wait=[("vv", vv0)])
    P.mm(pall[0:128, 512:513], ones[:], scal1[0:1, 1:2], True, True)
    P.mm(pall[0:128, 1024:1025], ones[:], scal1[0:1, 0:1], True, True, inc="mm")
    mm0 = P.cnt["mm"]
    P.v(lambda h: h.tensor_copy(invis[:], pall[0:128, 0:1]), wait=[("mm", mm0)])
    P.v(lambda h: h.tensor_copy(wscol[:], pall[0:128, 512:513]))
    P.v(lambda h: h.tensor_copy(iscol[:], pall[0:128, 1024:1025]), inc="vv")

    rm_pack = AR_B[:, 0:4096]
    rup_packT = AR_B[:, 4096:8192]
    P.v(lambda h: h.memset(rm_pack, 0.0))
    P.v(lambda h: h.memset(rup_packT, 0.0), inc="vv")
    vz = P.cnt["vv"]
    rmf = inp["L0_ffn_rail_maker"].rearrange("h r i j -> (h r i) j")
    rupf = inp["L0_ffn_rail_up_proj"].rearrange("h r k j -> (h r) k j")
    rup_in = AR_A[:, 4096:4224]
    rup_stage = AR_A[:, 4224:4288]

    def emit_rails_group(g8):
        src = rupf[8 * g8:8 * g8 + 8]
        ldv = P.op("gpsimd", lambda h, s=src, d=rup_in:
                   h.dma_start(out=d[0:64, :].rearrange("k (u j) -> k u j", j=16),
                               in_=s.transpose([1, 0, 2])),
                   wait=[("mm", P.cnt["mm"]), ("vv", vz)], inc="ld3", n=16)
        mv = P.trans(pall[0:128, 2048:2112], rup_in[0:64, :], ident[0:64, 0:64],
                     wait=[("ld3", ldv), ("st", P.cnt["st"])], inc="mm")
        ev = P.v(lambda h: h.tensor_copy(rup_stage, pall[0:128, 2048:2112]),
                 wait=[("mm", mv), ("st", P.cnt["st"])], inc="vv")
        for u in range(8):
            cdst = 128 * g8 + 64 * (u % 2)
            P.store(rup_packT[16 * u:16 * u + 16, cdst:cdst + 64],
                    rup_stage[16 * u:16 * u + 16, :],
                    wait=[("vv", ev), ("ld3", P.cnt["ld3"])] if u == 0 else ())

    xin_t = AR_A[:, 4352:5376]
    fm_t = AR_A[:, 5376:6400]
    sq_t0 = AR_A[:, 6400:7424]
    p0_state = {"mm_sq_prev": 0, "st2_prev": 0, "scp": 0}

    def emit_p0_tt(tt):
        lv = P.op("sync", lambda h, d=xin_t,
                  s_=inp["L0_attn_x"][128 * tt:128 * tt + 128, :]:
                  h.dma_start(out=d, in_=s_),
                  wait=[("mm", P.cnt["mm"])], inc="ld4", n=16)
        for ch in range(NCH):
            mv = P.trans(pall[:, 2560:2688],
                         xin_t[:, 128 * ch:128 * ch + 128],
                         ident[:], wait=[("ld4", lv), ("vv", P.cnt["vv"])],
                         inc="mm")
            ev = P.v(lambda h, ch=ch: h.tensor_copy(
                fm_t[:, 128 * ch:128 * ch + 128], pall[:, 2560:2688]),
                wait=[("mm", mv), ("st3", p0_state["st2_prev"])]
                if ch == 0 else [("mm", mv)], inc="vv")
            sv = P.act(sq_t0[:, 128 * ch:128 * ch + 128],
                       fm_t[:, 128 * ch:128 * ch + 128], AF.Square,
                       wait=[("vv", ev), ("mm", p0_state["mm_sq_prev"])],
                       inc="aa")
            c7a = 3072 + 128 * (tt % 2)
            c7b = 3584 + 128 * (tt % 2)
            P.mm(pall[0:1, c7a:c7a + 128], ones128[:],
                 fm_t[:, 128 * ch:128 * ch + 128], ch == 0, ch == NCH - 1,
                 wait=[("vv", ev), ("vv", p0_state["scp"])] if ch == 0
                 else [("vv", ev)])
            P.mm(pall[0:1, c7b:c7b + 128], ones128[:],
                 sq_t0[:, 128 * ch:128 * ch + 128], ch == 0, ch == NCH - 1,
                 wait=[("aa", sv)], inc="mm" if ch == NCH - 1 else None)
        p0_state["st2_prev"] = P.store(
            xs_scr[0].transpose([1, 0, 2])[:, :, 128 * tt:128 * tt + 128],
            fm_t.rearrange("p (c t) -> p c t", t=128),
            wait=[("vv", P.cnt["vv"])], sem="st3")
        mvs = P.cnt["mm"]
        P.v(lambda h, tt=tt, c7a=3072 + 128 * (tt % 2): h.tensor_copy(
            Svec[0:1, 128 * tt:128 * tt + 128], pall[0:1, c7a:c7a + 128]),
            wait=[("mm", mvs)])
        P.v(lambda h, tt=tt, c7b=3584 + 128 * (tt % 2): h.tensor_copy(
            S2vec[0:1, 128 * tt:128 * tt + 128], pall[0:1, c7b:c7b + 128]),
            inc="vv")
        p0_state["mm_sq_prev"] = P.cnt["mm"]
        p0_state["scp"] = P.cnt["vv"]

    wc = inp["L0_ffn_w_cproj"]
    wc_stages = [AR_W[:, 0:8192], AR_W[:, 8192:16384], AR_W[:, 16384:24576]]
    wcT_tiles = [AR_A[:, 0:2048], AR_A[:, 2048:4096], AR_A[:, 7424:9472]]
    wcT_sch = ("st", "st2", "st4")
    st_cgrp = {}
    lv_h = {}
    mm_h = {}
    for oh in range(16):
        o, hf = oh // 2, oh % 2
        sem_o = ("ld", "ld2", "ld5")[oh % 3]
        if oh < 3:
            P.fence("sync", sem_o)
        stg = wc_stages[oh % 3]
        lv = 0
        for c4 in range(4):
            lv = P.op("sync", lambda h, d=stg[:, 2048 * c4:2048 * c4 + 2048],
                      s_=wc[128 * o:128 * o + 128,
                            8192 * hf + 2048 * c4:8192 * hf + 2048 * c4 + 2048]:
                      h.dma_start(out=d, in_=s_),
                      wait=[("mm", mm_h.get(oh - 3, 0))] if c4 == 0 else (),
                      inc=sem_o, n=16)
        lv_h[oh] = lv
        for cgrp in range(4):
            g = 4 * oh + cgrp
            wt = wcT_tiles[g % 3]
            sch = wcT_sch[g % 3]
            for q4 in range(4):
                mv = None
                for t4 in range(4):
                    t16 = 4 * q4 + t4
                    pb = (2 * (g % 2)) + (q4 % 2)
                    mv = P.trans(pall[:, 512 * pb + 128 * t4:
                                      512 * pb + 128 * t4 + 128],
                                 stg[:, 128 * (16 * cgrp + t16):
                                     128 * (16 * cgrp + t16) + 128], ident[:],
                                 wait=[(sem_o, lv), ("vv", max(0, P.cnt["vv"] - 1))]
                                 if t4 == 0 else (), inc="mm" if t4 == 3 else None)
                mv = P.cnt["mm"]
                P.v(lambda h, q4=q4, wt=wt, pb=(2 * (g % 2)) + (q4 % 2):
                    h.tensor_copy(wt[:, 512 * q4:512 * q4 + 512],
                                  pall[:, 512 * pb:512 * pb + 512]),
                    wait=[("mm", mv), (sch, st_cgrp.get(g - 3, 0))], inc="vv")
            sv = 0
            for e2 in range(2):
                sv = P.store(
                    wcT_scr[64 * hf + 16 * cgrp + 8 * e2:
                            64 * hf + 16 * cgrp + 8 * e2 + 8, :,
                            128 * o:128 * o + 128]
                    .transpose([1, 0, 2]),
                    wt[:, 1024 * e2:1024 * e2 + 1024]
                    .rearrange("p (x d) -> p x d", d=128),
                    wait=[("vv", P.cnt["vv"]),
                          (sch, st_cgrp.get(g - 3, 0))] if e2 == 0 else (),
                    sem=sch)
            st_cgrp[g] = sv
        mm_h[oh] = P.cnt["mm"]
        emit_rails_group(2 * oh)
        emit_rails_group(2 * oh + 1)
        emit_p0_tt(oh)
    for l in range(8):
        P.op("gpsimd", lambda h, l=l,
             d=rm_pack[16 * l:16 * l + 16, :]
             .rearrange("p (pk b) -> p pk b", b=128)[:, :, 16 * l:16 * l + 16],
             s_=rmf.rearrange("(pk l p) j -> l p pk j", pk=32, l=8, p=16)[l]:
             h.dma_start(out=d, in_=s_),
             wait=[("vv", vz)] if l == 0 else (),
             inc="ld3", n=16)
    P.fence("gpsimd", "ld3")
    P.store(rails_scr[:, 0:4096], rm_pack, wait=[("ld3", P.cnt["ld3"])])
    P.store(rails_scr[:, 4096:8192], rup_packT, wait=[("st", P.cnt["st"])])
    P.bar()

    WT_IDX = {"L1_attn_Wq": 0, "L1_attn_Wk": 1, "L1_attn_Wv": 2,
              "L1_attn_Wo": 3, "L1_ffn_weight_fp8": 4}
    wtr_tmp4 = [AR_B[:, 2048 * j:2048 * j + 2048] for j in range(4)]
    wtr_dst4 = [AR_W[:, 16384 + 2048 * j:16384 + 2048 * j + 2048]
                for j in range(4)]
    wtr_lch = ("ld", "ld2", "ld5", "ld4")
    wtr_sch = ("st", "st2", "st4", "st3")
    tr_done = {}
    stw_done = {}
    qn = 0
    for wname, wi in WT_IDX.items():
        for qq in range(4):
            par = qn % 4
            lch = wtr_lch[par]
            sch = wtr_sch[par]
            tmp = wtr_tmp4[par]
            dst = wtr_dst4[par]
            ss = inp[wname].rearrange("(r p) d -> p r d", p=128)[
                :, 2 * qq:2 * qq + 2]
            lv = P.op("sync",
                      lambda h, d=tmp.rearrange("p (r d) -> p r d", d=1024),
                      s_=ss: h.dma_start(out=d, in_=s_),
                      wait=[("mm", tr_done.get(qn - 4, 0))], inc=lch, n=16)
            ev = 0
            for rr in range(2):
                for i in range(8):
                    pb = 4 + ((rr * 8 + i) % 4)
                    mv = P.trans(pall[:, 512 * pb:512 * pb + 128],
                                 tmp[:, 1024 * rr + 128 * i:
                                     1024 * rr + 128 * i + 128],
                                 ident[:],
                                 wait=[(lch, lv),
                                       ("vv", max(0, P.cnt["vv"] - 3))] +
                                 ([(sch, stw_done.get(qn - 4, 0))]
                                  if (rr == 0 and i == 0) else []),
                                 inc="mm")
                    ev = P.v(lambda h, rr=rr, i=i, pb=pb, dst=dst:
                             h.tensor_copy(
                                 dst[:, 256 * i + 128 * rr:
                                     256 * i + 128 * rr + 128],
                                 pall[:, 512 * pb:512 * pb + 128]),
                             wait=[("mm", mv)], inc="vv")
            tr_done[qn] = P.cnt["mm"]
            stw_done[qn] = P.store(
                wT_scr[wi].rearrange("p (i b) -> p i b", b=1024)
                [:, :, 256 * qq:256 * qq + 256],
                dst.rearrange("p (i c) -> p i c", c=256),
                wait=[("vv", ev)], sem=sch)
            qn += 1
    P.bar()

    def ln_affine(ln_id=None):
        P.v(lambda h: h.tensor_single_scalar(Svec, Svec, 1.0 / D, ALU.mult))
        P.v(lambda h: h.tensor_single_scalar(S2vec, S2vec, 1.0 / D, ALU.mult))
        P.v(lambda h: h.tensor_mul(Bvec, Svec, Svec))
        P.v(lambda h: h.tensor_sub(S2vec, S2vec, Bvec))
        P.v(lambda h: h.tensor_single_scalar(S2vec, S2vec, EPS, ALU.add),
            inc="vv")
        va = P.cnt["vv"]
        sq = P.act(Avec, S2vec, AF.Sqrt, wait=[("vv", va)], inc="aa")
        P.v(lambda h: h.reciprocal(Avec, Avec), wait=[("aa", sq)])
        P.v(lambda h: h.tensor_mul(Bvec, Svec, Avec), inc="vv")
        return P.cnt["vv"]

    def replicate(vec_slice, pbank, dst, n):
        mv = None
        for u0 in range(0, n, 512):
            w = min(512, n - u0)
            mv = P.mm(pall[0:128, 512 * pbank + u0:512 * pbank + u0 + w],
                      ones[:], vec_slice[:, u0:u0 + w], True, True,
                      wait=[("vv", P.cnt["vv"])] if u0 == 0 else (), inc="mm")
        return P.v(lambda h, d=dst, b=pbank, n=n: h.tensor_copy(
            d, pall[0:128, 512 * b:512 * b + n]),
            wait=[("mm", mv)], inc="vv")

    def load_w_transposed(wname, w_dst):
        P.fence("sync", "ld")
        P.load(w_dst, wT_scr[WT_IDX[wname]],
               wait=[("mm", P.cnt["mm"]), ("vv", P.cnt["vv"])])
        return 0

    def phaseP1(L, x_in_scr, ln_id=None):
        WL = f"L{L}_attn_"
        w_q, w_k, w_v = AR_W[:, 0:8192], AR_W[:, 8192:16384], AR_W[:, 16384:24576]
        if L == 0:
            P.fence("sync", "ld")
            for w, n in ((w_q, "Wq"), (w_k, "Wk"), (w_v, "Wv")):
                P.load(w.rearrange("p (i d) -> p i d", d=1024),
                       inp[WL + n].rearrange("(i p) d -> p i d", p=128))
            w_ld, w_vv = P.cnt["ld"], 0
        else:
            for w, n in ((w_q, "Wq"), (w_k, "Wk"), (w_v, "Wv")):
                load_w_transposed(WL + n, w)
            w_ld, w_vv = P.cnt["ld"], P.cnt["vv"]
        ln_affine(ln_id)
        xs2 = [AR_A[:, 0:2048], AR_A[:, 2048:4096]]
        xln2 = [AR_A[:, 4096:6144], AR_B[:, 5184:7232]]
        qs = AR_A[:, 6144:8192]
        ks = AR_A[:, 8192:10240]
        vs_ = AR_B[:, 1024:3072]
        arep = AR_B[:, 0:256]
        brep = AR_B[:, 256:512]
        v_all = AR_B[:, 3072:5184]
        vt0 = AR_B[:, 3072:4096]
        lvs_s = {}
        xln_done = {}
        mm_done = {}
        st2_v = 0
        vprev = {"v": 0}

        def emit_loads(si):
            sem_s = "ld" if si % 2 == 0 else "ld2"
            if si < 2:
                P.fence("sync", sem_s)
            cols_i = slice(256 * si, 256 * si + 256)
            lv = 0
            for ch in range(NCH):
                lv = P.op("sync", lambda h,
                          d=xs2[si % 2][:, 256 * ch:256 * ch + 256],
                          s_=x_in_scr[ch][:, cols_i]:
                          h.dma_start(out=d, in_=s_),
                          wait=[("vv", xln_done.get(si - 2, 0))]
                          if ch == 0 else (), inc=sem_s, n=16)
            lvs_s[si] = lv

        def front(s_):
            cols = slice(256 * s_, 256 * s_ + 256)
            xs = xs2[s_ % 2]
            xln = xln2[s_ % 2]
            sem_s = "ld" if s_ % 2 == 0 else "ld2"
            emit_loads(s_)
            replicate(Avec[0:1, cols], 4, arep, 256)
            ev = replicate(Bvec[0:1, cols], 4, brep, 256)
            for ch in range(NCH):
                P.v(lambda h, ch=ch, xs=xs, xln=xln: h.tensor_mul(
                    xln[:, 256 * ch:256 * ch + 256],
                    xs[:, 256 * ch:256 * ch + 256], arep),
                    wait=[(sem_s, lvs_s[s_]), ("vv", ev),
                          ("mm", mm_done.get(s_ - 2, 0))]
                    if ch == 0 else ())
                P.v(lambda h, ch=ch, xs=xs, xln=xln: h.tensor_sub(
                    xln[:, 256 * ch:256 * ch + 256],
                    xln[:, 256 * ch:256 * ch + 256], brep),
                    inc="vv" if ch == NCH - 1 else None)
            xln_done[s_] = P.cnt["vv"]

        front(0)
        front(1)
        for s_ in range(8):
            cols = slice(256 * s_, 256 * s_ + 256)
            xs = xs2[s_ % 2]
            xln = xln2[s_ % 2]
            xlv = xln_done[s_]
            st_prev = P.cnt["st"]
            if L == 1:
                P.v(lambda h: h.memset(v_all[:], 0.0), wait=[("st2", st2_v)])
                P.v(lambda h: h.memset(
                    v_all.rearrange("p (x c) -> p x c", c=66)[:, :, 64:65], 1.0),
                    inc="vv")
            evs = {}
            for w_sb, dst_t, bname in ((w_q, qs, "bq"), (w_k, ks, "bk"),
                                       (w_v, vs_, "bv")):
                for o in range(NCH):
                    pb = o % 2
                    for i in range(NCH):
                        w8 = []
                        if i == 0:
                            w8 = [("vv", max(xlv, w_vv)), ("ld", w_ld),
                                  ("aa", max(0, P.cnt["aa"] - 1))]
                        P.mm(bank(pb, 256),
                             w_sb[:, 1024 * i + 128 * o:1024 * i + 128 * o + 128],
                             xln[:, 256 * i:256 * i + 256],
                             i == 0, i == NCH - 1, wait=w8,
                             inc="mm" if i == NCH - 1 else None)
                    mv = P.cnt["mm"]
                    P.act(dst_t[:, 256 * o:256 * o + 256], bank(pb, 256),
                          AF.Identity,
                          bias=bias_sb[:, BIAS_IDX[WL + bname] + o:
                                       BIAS_IDX[WL + bname] + o + 1],
                          wait=[("mm", mv), ("st", st_prev),
                                ("vv", xlv)], inc="aa")
                    if L == 1 and bname == "bv":
                        evv = P.cnt["aa"]
                        for hh in (2 * o, 2 * o + 1):
                            r0 = (hh % 2) * 64
                            for jj in range(2):
                                mv = P.trans(bank(2 + jj, 64),
                                             vs_[r0:r0 + 64,
                                                 256 * o + 128 * jj:
                                                 256 * o + 128 * jj + 128],
                                             ident[r0:r0 + 64, r0:r0 + 64],
                                             wait=[("aa", evv),
                                                   ("vv", evs.get(2 * hh + jj - 2,
                                                                  vprev["v"]))],
                                             inc="mm")
                                P.v(lambda h, hh=hh, jj=jj: h.tensor_copy(
                                    v_all[:, 132 * hh + 66 * jj:
                                          132 * hh + 66 * jj + 64],
                                    bank(2 + jj, 64)),
                                    wait=[("mm", mv)], inc="vv")
                                evs[2 * hh + jj] = P.cnt["vv"]
            mm_done[s_] = P.cnt["mm"]
            qkv_done = P.cnt["aa"]
            P.fence("gpsimd", "st")
            for ch in range(NCH):
                P.store(q_scr[ch][:, cols], qs[:, 256 * ch:256 * ch + 256],
                        wait=[("aa", qkv_done)] if ch == 0 else ())
                P.store(k_scr[ch][:, cols], ks[:, 256 * ch:256 * ch + 256])
            if L == 1:
                vprev["v"] = P.cnt["vv"]
                st2_v = P.store(
                    v_scr.transpose([1, 0, 2])[:, :, 132 * s_:132 * s_ + 132],
                    v_all.rearrange("p (x c) -> p x c", c=132),
                    wait=[("vv", evs[2 * H - 1])], sem="st2")
            else:
                cops = {}
                for jj in range(2):
                    for dv in range(NCH):
                        mv = P.trans(bank(2 + (dv % 2), 128),
                                     vs_[:, 256 * dv + 128 * jj:
                                         256 * dv + 128 * jj + 128],
                                     ident[:],
                                     wait=[("aa", qkv_done),
                                           ("vv", cops.get(8 * jj + dv - 2,
                                                           vprev["v"]))],
                                     inc="mm")
                        P.v(lambda h, dv=dv: h.tensor_copy(
                            vt0[:, 128 * dv:128 * dv + 128], bank(2 + (dv % 2), 128)),
                            wait=[("mm", mv), ("st", P.cnt["st"])], inc="vv")
                        cops[8 * jj + dv] = P.cnt["vv"]
                    P.store(v0_scr[2 * s_ + jj], vt0[:],
                            wait=[("vv", cops[8 * jj + NCH - 1])])
                vprev["v"] = P.cnt["vv"]
            if s_ + 2 < 8:
                front(s_ + 2)
        P.bar()

    def phaseP2_L0():
        kfm = AR_W[:, 0:16384]
        qh_t = AR_W[:, 16384:24576]
        probs = AR_A[:, 0:8192]
        rep_sb = AR_A[:, 8192:8704]
        attn_sb2 = [AR_A[:, 8704:9216], AR_A[:, 9216:9728]]
        vAh = [AR_B[:, 0:4096], AR_B[:, 4096:8192]]
        vch = ("ld2", "ld4")
        P.fence("sync", "ld")
        P.fence("sync", "ld2")
        P.fence("sync", "ld4")
        kld = 0
        for c in range(NCH):
            kld = P.load(kfm[:, 2048 * c:2048 * c + 2048], k_scr[c][:])
        av_g = {}
        avj3_prev = {}
        ev_q = {}
        qk_last_prev = 0
        prev_store = {0: ("st", 0), 1: ("st2", 0)}
        vld = {}

        def emit_vload(qq, g):
            if g >= 2:
                gate = av_g.get((qq, g - 2), 0)
            else:
                gate = av_g.get((qq - 1, 2 + g), 0)
            vld[(qq, g)] = P.op(
                "sync",
                lambda h, d=vAh[g % 2].rearrange("p (j e) -> p j e", e=256),
                s_=v0_scr[:, :, 256 * g:256 * g + 256].transpose([1, 0, 2]):
                h.dma_start(out=d, in_=s_),
                wait=[("mm", gate)] if gate else (), inc=vch[g % 2], n=16)

        for qq in range(4):
            qcols = slice(512 * qq, 512 * qq + 512)
            qld = 0
            for c in range(NCH):
                qld = P.load(qh_t[:, 1024 * c:1024 * c + 512],
                             q_scr[c][:, qcols],
                             wait=[("mm", qk_last_prev)] if c == 0 else ())
            emit_vload(qq, 0)
            emit_vload(qq, 1)
            expc = {}

            def emit_qk0(j, qq=qq):
                w0 = [("aa", expc.get(j - 2, 0))]
                if j <= 1:
                    w0.append(("ld", max(kld, qld)))
                for c in range(NCH):
                    P.mm(pall[:, 512 * (j % 2):512 * (j % 2) + 512],
                         kfm[:, 2048 * c + 128 * j:2048 * c + 128 * j + 128],
                         qh_t[:, 1024 * c:1024 * c + 512],
                         c == 0, c == NCH - 1, wait=w0 if c == 0 else (),
                         inc="mm" if c == NCH - 1 else None)
                mv = P.cnt["mm"]
                we = [("mm", mv)]
                if avj3_prev.get(j):
                    we.append(("mm", avj3_prev[j]))
                P.act(probs[:, 512 * j:512 * j + 512],
                      pall[:, 512 * (j % 2):512 * (j % 2) + 512],
                      AF.Exp, scale=0.125, bias=cmisc[:, 3:4],
                      wait=we, inc="aa")
                expc[j] = P.cnt["aa"]

            def emit_sum_avA(j, qq=qq):
                P.mm(pall[0:1, 1024:1536], ones128[:],
                     probs[:, 512 * j:512 * j + 512], j == 0, j == 15,
                     wait=[("aa", expc[j])], inc="mm" if j == 15 else None)
                for g in range(2):
                    for d2 in range(2):
                        w1 = []
                        if j == 0 and d2 == 0:
                            w1 = [(vch[g % 2], vld[(qq, g)]),
                                  ("vv", ev_q.get((qq - 1, 4 + 2 * g + 1), 0))]
                        P.mm(pall[:, 2048 + 512 * (2 * g + d2):
                                  2048 + 512 * (2 * g + d2) + 512],
                             vAh[g % 2][:, 256 * j + 128 * d2:
                                        256 * j + 128 * d2 + 128],
                             probs[:, 512 * j:512 * j + 512],
                             j == 0, j == 15, wait=w1,
                             inc="mm" if (j == 15 and d2 == 1) else None)
                    if j == 15:
                        av_g[(qq, g)] = P.cnt["mm"]

            emit_qk0(0)
            for j in range(1, 16):
                emit_qk0(j)
                emit_sum_avA(j - 1)
            emit_sum_avA(15)
            qk_last_prev = P.cnt["mm"]
            sum_done = P.cnt["mm"]
            P.v(lambda h: h.reciprocal(rsum[0:1, 0:512], pall[0:1, 1024:1536]),
                wait=[("mm", sum_done)])
            rv = P.cnt["vv"]
            mv = P.mm(pall[0:128, 1536:2048], ones[:], rsum[0:1, 0:512],
                      True, True, wait=[("vv", rv)], inc="mm")
            P.v(lambda h: h.tensor_copy(rep_sb, pall[0:128, 1536:2048]),
                wait=[("mm", mv)], inc="vv")
            for dv in range(4):
                sem_s = "st" if dv % 2 == 0 else "st2"
                P.v(lambda h, dv=dv: h.tensor_mul(
                    attn_sb2[dv % 2],
                    pall[:, 2048 + 512 * dv:2048 + 512 * dv + 512], rep_sb),
                    wait=[("mm", av_g[(qq, dv // 2)]), prev_store[dv % 2]],
                    inc="vv")
                ev_q[(qq, dv)] = P.cnt["vv"]
                prev_store[dv % 2] = (sem_s, P.store(
                    attn_scr[dv][:, qcols], attn_sb2[dv % 2],
                    wait=[("vv", ev_q[(qq, dv)])], sem=sem_s))
            emit_vload(qq, 2)
            emit_vload(qq, 3)
            avj3 = {}
            for g in (2, 3):
                for j in range(16):
                    for d2 in range(2):
                        w1 = []
                        if j == 0 and d2 == 0:
                            w1 = [(vch[g % 2], vld[(qq, g)]),
                                  ("vv", ev_q[(qq, 2 * (g - 2) + 1)])]
                        P.mm(pall[:, 2048 + 512 * (2 * (g - 2) + d2):
                                  2048 + 512 * (2 * (g - 2) + d2) + 512],
                             vAh[g % 2][:, 256 * j + 128 * d2:
                                        256 * j + 128 * d2 + 128],
                             probs[:, 512 * j:512 * j + 512],
                             j == 0, j == 15,
                             wait=w1, inc="mm" if d2 == 1 else None)
                    if g == 3:
                        avj3[j] = P.cnt["mm"]
                av_g[(qq, g)] = P.cnt["mm"]
            avj3_prev = avj3
            for dv in range(4, 8):
                sem_s = "st" if dv % 2 == 0 else "st2"
                g = 2 + (dv - 4) // 2
                P.v(lambda h, dv=dv: h.tensor_mul(
                    attn_sb2[dv % 2],
                    pall[:, 2048 + 512 * (dv - 4):2048 + 512 * (dv - 4) + 512],
                    rep_sb),
                    wait=[("mm", av_g[(qq, g)]), prev_store[dv % 2]],
                    inc="vv")
                ev_q[(qq, dv)] = P.cnt["vv"]
                prev_store[dv % 2] = (sem_s, P.store(
                    attn_scr[dv][:, qcols], attn_sb2[dv % 2],
                    wait=[("vv", ev_q[(qq, dv)])], sem=sem_s))
        P.bar()

    def phaseP2(L):
        kaug2 = [AR_A[0:66, 0:2048], AR_A[0:66, 8192:10240]]
        qaug2 = [AR_A[0:66, 2048:4096], AR_B[0:66, 2080:4128]]
        vh2 = [AR_B[0:128, 0:1056], AR_B[0:128, 4128:5184]]
        probs = [AR_A[:, 4096:5120], AR_A[:, 5120:6144]]
        rep_sb = AR_A[0:64, 6144:7168]
        attn_sb = AR_B[0:64, 1056:2080]
        P.v(lambda h: h.memset(qaug2[0][64:66, :], 1.0))
        P.v(lambda h: h.memset(qaug2[1][64:66, :], 1.0), inc="vv")
        lV2 = {}
        mm_head = {}
        for hh in range(H):
            kaug, qaug, vh = kaug2[hh % 2], qaug2[hh % 2], vh2[hh % 2]
            sem_h = "ld" if hh % 2 == 0 else "ld2"
            if hh < 2:
                P.fence("sync", sem_h)
            slot = hh if L == 1 else 16
            P.op("sync", lambda h, d=kaug[0:64, :],
                 s_=k_scr[hh // 2][(hh % 2) * 64:(hh % 2) * 64 + 64, :]:
                 h.dma_start(out=d, in_=s_),
                 wait=[("mm", mm_head.get(hh - 2, 0))], inc=sem_h, n=16)
            P.op("sync", lambda h, d=kaug[64:66, :],
                 s_=inp["c_alibi"][:, T * slot:T * slot + T]:
                 h.dma_start(out=d, in_=s_), inc=sem_h, n=16)
            P.op("sync", lambda h, d=qaug[0:64, :],
                 s_=q_scr[hh // 2][(hh % 2) * 64:(hh % 2) * 64 + 64, :]:
                 h.dma_start(out=d, in_=s_), inc=sem_h, n=16)
            lV = P.op("sync", lambda h, d=vh, s_=v_scr[hh][:]:
                      h.dma_start(out=d, in_=s_), inc=sem_h, n=16)
            lV2[hh] = lV
            vv_attn = P.cnt["vv"]
            exp_hist = {"e": {}, "a": {}}
            for half in range(2):
                qc = 1024 * half
                expc = {}
                avc = {}
                expc_prev = exp_hist["e"]
                avc_prev = exp_hist["a"]

                def emit_qk(j, expc_prev=expc_prev, vv_attn=vv_attn):
                    w0 = []
                    if j <= 1:
                        if half == 0:
                            w0.append((sem_h, lV))
                        if expc_prev.get(14 + j):
                            w0.append(("aa", expc_prev[14 + j]))
                        else:
                            w0.append(("vv", vv_attn))
                    if j >= 2:
                        w0 = w0 + [("aa", expc[j - 2])]
                    for u in range(2):
                        P.mm(pall[:, 1024 * (j % 2) + 512 * u:
                                  1024 * (j % 2) + 512 * u + 512],
                             kaug[:, 128 * j:128 * j + 128],
                             qaug[:, qc + 512 * u:qc + 512 * u + 512],
                             True, True, wait=w0 if u == 0 else (),
                             inc="mm" if u == 1 else None)
                    mv = P.cnt["mm"]
                    we = [("mm", mv)]
                    if j >= 2:
                        we.append(("mm", avc[j - 2]))
                    elif avc_prev.get(14 + j):
                        we.append(("mm", avc_prev[14 + j]))
                    P.act(probs[j % 2], pall[:, 1024 * (j % 2):1024 * (j % 2) + 1024],
                          AF.Exp, scale=0.125, wait=we, inc="aa")
                    expc[j] = P.cnt["aa"]

                def emit_av(j, vv_attn=vv_attn):
                    for u in range(2):
                        w1 = [("aa", expc[j])]
                        if j == 0 and vv_attn:
                            w1.append(("vv", vv_attn))
                        P.mm(pall[0:66, 2048 + 512 * u:2048 + 512 * u + 512],
                             vh[:, 66 * j:66 * j + 66],
                             probs[j % 2][:, 512 * u:512 * u + 512],
                             j == 0, j == 15,
                             wait=w1 if u == 0 else (),
                             inc="mm")
                    avc[j] = P.cnt["mm"]

                emit_qk(0)
                for j in range(1, 16):
                    emit_qk(j)
                    emit_av(j - 1)
                emit_av(15)
                P.v(lambda h: h.reciprocal(rsum[:], pall[64:65, 2048:3072]),
                    wait=[("mm", avc[15])], inc="vv")
                rv = P.cnt["vv"]
                P.mm(pall[0:64, 3072:3584], ones[0:1, 0:64], rsum[0:1, 0:512],
                     True, True, wait=[("vv", rv)])
                mv = P.mm(pall[0:64, 3584:4096], ones[0:1, 0:64],
                          rsum[0:1, 512:1024], True, True, inc="mm")
                P.v(lambda h: h.tensor_copy(rep_sb, pall[0:64, 3072:4096]),
                    wait=[("mm", mv), ("vv", P.cnt["vv"])], inc="vv")
                P.v(lambda h: h.tensor_mul(attn_sb, pall[0:64, 2048:3072], rep_sb),
                    wait=[("st", P.cnt["st"])], inc="vv")
                vv_attn = P.cnt["vv"]
                P.store(attn_scr[hh // 2][(hh % 2) * 64:(hh % 2) * 64 + 64,
                                          qc:qc + 1024],
                        attn_sb, wait=[("vv", vv_attn)])
                exp_hist["e"] = expc
                exp_hist["a"] = avc
            mm_head[hh] = P.cnt["mm"]
        P.bar()

    def phaseP3(L, x_prev, x_new, ln_next=None):
        WL = f"L{L}_attn_"
        wo = AR_W[:, 0:8192]
        if L == 0:
            P.load(wo.rearrange("p (i d) -> p i d", d=1024),
                   inp[WL + "Wo"].rearrange("(i p) d -> p i d", p=128))
            wo_vv = 0
        else:
            wo_vv = load_w_transposed(WL + "Wo", wo)
        wo_ld = P.cnt["ld"]
        attn_t = AR_A[:, 0:8192]
        y_t = AR_A[:, 8192:9216]
        xo2 = [AR_A[:, 9216:10240], AR_B[:, 2048:3072]]
        xn_t = AR_B[:, 0:1024]
        xq_t = AR_B[:, 1024:2048]
        add_done = {}
        for th in range(2):
            tc = slice(1024 * th, 1024 * th + 1024)
            mm_now = P.cnt["mm"]
            s2prev = 0
            P.fence("sync", "ld")
            P.fence("sync", "ld4")
            P.fence("sync", "ld5")
            lva = 0
            for c in range(NCH):
                lva = P.load(attn_t[:, 1024 * c:1024 * c + 1024],
                             attn_scr[c][:, tc],
                             wait=[("mm", mm_now)] if c == 0 else ())
            lxo2 = {}

            def emit_lxo(o, th=th):
                xch = "ld4" if o % 2 == 0 else "ld5"
                gate = (add_done.get((th, o - 2), 0) if o >= 2
                        else add_done.get((th - 1, 6 + o), 0))
                lxo2[o] = P.op("sync", lambda h, d=xo2[o % 2],
                               s_=x_prev[o][:, tc]: h.dma_start(out=d, in_=s_),
                               wait=[("vv", gate)], inc=xch, n=16)

            emit_lxo(0)
            emit_lxo(1)
            adds = []
            for o in range(NCH):
                pb = (o % 2) * 2
                xch = "ld4" if o % 2 == 0 else "ld5"
                for c in range(NCH):
                    w8 = [("ld", max(lva, wo_ld)), ("vv", wo_vv),
                          ("aa", max(0, P.cnt["aa"] - 1))] if c == 0 else []
                    for u in range(2):
                        P.mm(pall[:, 512 * pb + 512 * u:512 * pb + 512 * u + 512],
                             wo[:, 1024 * c + 128 * o:1024 * c + 128 * o + 128],
                             attn_t[:, 1024 * c + 512 * u:1024 * c + 512 * u + 512],
                             c == 0, c == NCH - 1,
                             wait=w8 if u == 0 else (),
                             inc="mm" if (c == NCH - 1 and u == 1) else None)
                mv = P.cnt["mm"]
                av = P.act(y_t, pall[:, 512 * pb:512 * pb + 1024], AF.Identity,
                           bias=bias_sb[:, BIAS_IDX[WL + "bo"] + o:
                                        BIAS_IDX[WL + "bo"] + o + 1],
                           wait=[("mm", mv), ("vv", P.cnt["vv"])], inc="aa")
                P.v(lambda h, o=o: h.tensor_add(xn_t, y_t, xo2[o % 2]),
                    wait=[("aa", av), (xch, lxo2[o]), ("st", P.cnt["st"])],
                    inc="vv")
                add_done[(th, o)] = P.cnt["vv"]
                if o + 2 < NCH:
                    emit_lxo(o + 2)
                xnv = P.cnt["vv"]
                P.fence("gpsimd", "st")
                P.store(x_new[o][:, tc], xn_t, wait=[("vv", xnv)])
                sv = P.act(xq_t, xn_t, AF.Square,
                           wait=[("vv", xnv), ("mm", s2prev)], inc="aa")
                for u in range(2):
                    P.mm(pall[0:1, 2048 + 512 * u:2048 + 512 * u + 512],
                         ones128[:], xn_t[:, 512 * u:512 * u + 512],
                         o == 0, o == NCH - 1, wait=[("vv", xnv)] if u == 0 else ())
                    P.mm(pall[0:1, 3072 + 512 * u:3072 + 512 * u + 512],
                         ones128[:], xq_t[:, 512 * u:512 * u + 512],
                         o == 0, o == NCH - 1,
                         wait=[("aa", sv)] if u == 0 else (),
                         inc="mm" if u == 1 else None)
                s2prev = P.cnt["mm"]
            mvs = P.cnt["mm"]
            P.v(lambda h, th=th: h.tensor_copy(
                Svec[0:1, 1024 * th:1024 * th + 1024],
                pall[0:1, 2048:3072]), wait=[("mm", mvs)])
            P.v(lambda h, th=th: h.tensor_copy(
                S2vec[0:1, 1024 * th:1024 * th + 1024],
                pall[0:1, 3072:4096]), inc="vv")
        P.bar()

    def phaseP4(x_prev, x_new, ln_id=None, ln_next=None):
        ln_affine(ln_id)
        rm4 = AR_B[:, 0:4096]
        rup4 = AR_B[:, 4096:8192]
        P.fence("sync", "ld")
        l1 = P.load(rm4, rails_scr[:, 0:4096])
        l2 = P.load(rup4, rails_scr[:, 4096:8192])
        xs = AR_A[:, 2048:4096]
        xln = AR_A[:, 4096:6144]
        arep = AR_A[:, 6144:6400]
        brep = AR_A[:, 6400:6656]
        sq_t = AR_A[:, 6656:6912]
        part0 = AR_A[:, 8192:10240]
        wct = [AR_A[:, 0:1024], AR_A[:, 1024:2048], AR_A[:, 6912:7936]]
        wct_sems = ("ld", "ld2", "ld3")
        wct_engs = ("sync", "sync", "gpsimd")
        y_sb = AR_W[:, 16384:24576]
        h_half = AR_W[:, 0:16384]
        for s in range(8):
            cols = slice(256 * s, 256 * s + 256)
            P.fence("sync", "ld")
            lvs = 0
            for ch in range(NCH):
                lvs = P.load(xs[:, 256 * ch:256 * ch + 256], x_prev[ch][:, cols],
                             wait=[("vv", P.cnt["vv"])] if ch == 0 else ())
            replicate(Avec[0:1, cols], 6, arep, 256)
            ev = replicate(Bvec[0:1, cols], 6, brep, 256)
            mm_prev = P.cnt["mm"]
            for ch in range(NCH):
                P.v(lambda h, ch=ch: h.tensor_mul(
                    xln[:, 256 * ch:256 * ch + 256],
                    xs[:, 256 * ch:256 * ch + 256], arep),
                    wait=[("ld", lvs), ("vv", ev), ("mm", mm_prev)]
                    if ch == 0 else ())
                P.v(lambda h, ch=ch: h.tensor_sub(
                    xln[:, 256 * ch:256 * ch + 256],
                    xln[:, 256 * ch:256 * ch + 256], brep),
                    inc="vv" if ch == NCH - 1 else None)
            xlv = P.cnt["vv"]
            for pk in range(32):
                w0 = [("ld", max(l1, l2)), ("vv", xlv)] if pk == 0 else \
                     [("vv", P.cnt["vv"] - 1)]
                P.mm(bank(4 + (pk % 2), 256), rm4[:, 128 * pk:128 * pk + 128],
                     xln[:, 256 * (pk % 8):256 * (pk % 8) + 256], True, True,
                     wait=w0, inc="mm")
                mv = P.cnt["mm"]
                P.v(lambda h, pk=pk: h.tensor_copy(
                    y_sb[:, 256 * pk:256 * pk + 256], bank(4 + (pk % 2), 256)),
                    wait=[("mm", mv)], inc="vv")
            yv = P.cnt["vv"]
            prev_done = yv
            for half in range(2):
                for ci in range(64):
                    c = 64 * half + ci
                    hh, m = c // 32, c % 32
                    g = m // 4
                    rowb = 32 * (m % 4)
                    w0 = [("vv", prev_done)] if ci == 0 else \
                         [("aa", P.cnt["aa"] - 1)]
                    P.mm(bank(4 + (ci % 2), 256, rowb if False else 0, 128),
                         rup4[rowb:rowb + 32, 128 * (c // 4):128 * (c // 4) + 128],
                         y_sb[rowb:rowb + 32,
                              256 * (8 * hh + g):256 * (8 * hh + g) + 256],
                         True, True, wait=w0, inc="mm",
                         tile_position=(rowb, 0))
                    mv = P.cnt["mm"]
                    P.act(h_half[:, 256 * ci:256 * ci + 256],
                          bank(4 + (ci % 2), 256), AF.Gelu,
                          wait=[("mm", mv)], inc="aa")
                hv = P.cnt["aa"]
                ldw = {}
                mmc = {}
                for ci in range(64):
                    c = 64 * half + ci
                    sem_c = wct_sems[ci % 3]
                    wl = [("aa", hv), ("vv", P.cnt["vv"]),
                          ("mm", P.cnt["mm"])] if ci == 0 else \
                         ([("mm", mmc[ci - 3])] if ci >= 3 else
                          [("mm", P.cnt["mm"] - NCH)])
                    if ci < 3:
                        P.fence(wct_engs[ci % 3], sem_c)
                    ldw[ci] = P.op(
                        wct_engs[ci % 3],
                        lambda h, d=wct[ci % 3],
                        s_=wcT_scr[c]:
                        h.dma_start(out=d, in_=s_),
                        wait=wl, inc=sem_c, n=16)
                    for o in range(NCH):
                        P.mm(pall[:, 512 * o:512 * o + 256],
                             wct[ci % 3][:, 128 * o:128 * o + 128],
                             h_half[:, 256 * ci:256 * ci + 256],
                             ci == 0, ci == 63,
                             wait=[(sem_c, ldw[ci]), ("aa", hv)] if o == 0 else (),
                             inc="mm" if o == NCH - 1 else None)
                    mmc[ci] = P.cnt["mm"]
                mv = mmc[63]
                if half == 0:
                    for o in range(NCH):
                        P.v(lambda h, o=o: h.tensor_copy(
                            part0[:, 256 * o:256 * o + 256],
                            pall[:, 512 * o:512 * o + 256]),
                            wait=[("mm", mv), ("st", P.cnt["st"])] if o == 0 else (),
                            inc="vv" if o == NCH - 1 else None)
                    prev_done = P.cnt["vv"]
                else:
                    st_prev = P.cnt["st"]
                    for o in range(NCH):
                        P.v(lambda h, o=o: h.tensor_add(
                            part0[:, 256 * o:256 * o + 256],
                            pall[:, 512 * o:512 * o + 256],
                            part0[:, 256 * o:256 * o + 256]),
                            wait=[("mm", mv)] if o == 0 else ())
                        P.v(lambda h, o=o: h.tensor_add(
                            part0[:, 256 * o:256 * o + 256],
                            part0[:, 256 * o:256 * o + 256],
                            xs[:, 256 * o:256 * o + 256]),
                            inc="vv")
                    addv = P.cnt["vv"]
                    P.fence("gpsimd", "st")
                    for o in range(NCH):
                        P.store(x_new[o][:, cols], part0[:, 256 * o:256 * o + 256],
                                wait=[("vv", addv)] if o == 0 else ())
                    s2p4 = 0
                    for o in range(NCH):
                        sv = P.act(sq_t, part0[:, 256 * o:256 * o + 256],
                                   AF.Square, wait=[("vv", addv), ("mm", s2p4)],
                                   inc="aa")
                        P.mm(pall[0:1, 2048:2304], ones128[:],
                             part0[:, 256 * o:256 * o + 256],
                             o == 0, o == NCH - 1,
                             wait=[("vv", addv)] if o == 0 else ())
                        P.mm(pall[0:1, 3072:3328], ones128[:], sq_t,
                             o == 0, o == NCH - 1, wait=[("aa", sv)],
                             inc="mm")
                        s2p4 = P.cnt["mm"]
                    mvs = P.cnt["mm"]
                    P.v(lambda h, s=s: h.tensor_copy(
                        Svec[0:1, 256 * s:256 * s + 256],
                        pall[0:1, 2048:2304]), wait=[("mm", mvs)])
                    P.v(lambda h, s=s: h.tensor_copy(
                        S2vec[0:1, 256 * s:256 * s + 256],
                        pall[0:1, 3072:3328]), inc="vv")
        P.bar()

    def phaseP6(x_prev, ln_id=None):
        wT = AR_W[:, 0:8192]
        wt_vv = load_w_transposed("L1_ffn_weight_fp8", wT)
        ln_affine(ln_id)
        xs2 = [AR_W[:, 8192:12288], AR_W[:, 12288:16384]]
        xq2 = [AR_B[:, 0:4096], AR_B[:, 4096:8192]]
        xf = AR_W[:, 16384:20480]
        arep = AR_A[:, 2048:2560]
        brep = AR_A[:, 2560:3072]
        t_a = AR_A[:, 4096:5120]
        t_g = AR_A[:, 5120:6144]
        t_c = AR_A[:, 6144:7168]
        t_f = AR_A[:, 7168:8192]
        t_r = AR_A[:, 8192:9216]
        t_s = AR_A[:, 9216:10240]
        yo2 = [AR_A[:, 0:512], AR_A[:, 512:1024]]
        tm2 = [AR_A[:, 1024:2048], AR_A[:, 3072:4096]]
        mm_strip = {}
        qv_strip = {}
        avx_pb = {-1: {0: 0, 1: 0}}
        add_pb = {-1: {0: 0, 1: 0}}
        trans_last = {-1: P.cnt["mm"]}
        tmev_bank = {2: 0, 3: 0}
        st_ch = {"st": {-1: P.cnt["st"]}, "st2": {-1: P.cnt["st2"]}}

        def front(k):
            cols = slice(512 * k, 512 * k + 512)
            xs = xs2[k % 2]
            xq = xq2[k % 2]
            sem_k = "ld" if k % 2 == 0 else "ld2"
            if k < 2:
                P.fence("sync", sem_k)
            lvs = 0
            for ch in range(NCH):
                lvs = P.op("sync", lambda h, d=xs[:, 512 * ch:512 * ch + 512],
                           s_=x_prev[ch][:, cols]: h.dma_start(out=d, in_=s_),
                           wait=[("mm", mm_strip.get(k - 2, 0))]
                           if ch == 0 else (), inc=sem_k, n=16)
            ev = replicate(Avec[0:1, cols], 6, arep, 512)
            ev = replicate(Bvec[0:1, cols], 6, brep, 512)
            for ch in range(NCH):
                x1 = xq[:, 512 * ch:512 * ch + 512]
                xsch = xs[:, 512 * ch:512 * ch + 512]
                P.v(lambda h, a=x1, b=xsch: h.tensor_mul(a, b, arep),
                    wait=[(sem_k, lvs), ("vv", max(ev, wt_vv)),
                          ("mm", mm_strip.get(k - 2, 0))] if ch == 0 else ())
                P.v(lambda h, a=x1: h.tensor_sub(a, a, brep),
                    inc="vv" if ch == NCH - 1 else None)
            for t4 in range(4):
                x1 = xq[:, 1024 * t4:1024 * t4 + 1024]
                P.v(lambda h, a=x1: h.tensor_scalar(a, a, invis[:], 448.0,
                                                    ALU.mult, ALU.min))
                P.v(lambda h, a=x1: h.tensor_single_scalar(a, a, -448.0, ALU.max),
                    inc="vv")
                cv = P.cnt["vv"]
                P.act(t_a, x1, AF.Abs, wait=[("vv", cv)])
                sg = P.act(t_g, x1, AF.Sign, inc="aa")
                P.v(lambda h: h.tensor_single_scalar(t_f, t_a, 2.0 ** -9, ALU.max),
                    wait=[("aa", sg)], inc="vv")
                av2 = P.cnt["vv"]
                lnv = P.act(t_f, t_f, AF.Ln, wait=[("vv", av2)], inc="aa")
                P.v(lambda h: h.tensor_scalar(t_f, t_f, INV_LN2, 9.0, ALU.mult,
                                              ALU.add), wait=[("aa", lnv)])
                P.v(lambda h: h.tensor_copy(t_c, t_f))
                for b in (8.0, 4.0, 2.0, 1.0):
                    P.v(lambda h, b=b: h.tensor_single_scalar(t_r, t_f, b,
                                                              ALU.is_ge))
                    P.v(lambda h, b=b: h.scalar_tensor_tensor(t_f, t_r, -b, t_f,
                                                              ALU.mult, ALU.add))
                P.v(lambda h: h.tensor_sub(t_c, t_c, t_f))
                P.v(lambda h: h.tensor_scalar(t_c, t_c, 3.0, -12.0,
                                              ALU.max, ALU.add), inc="vv")
                ecv = P.cnt["vv"]
                P.act(t_r, t_c, AF.Exp, scale=-LN2, wait=[("vv", ecv)])
                sv2 = P.act(t_s, t_c, AF.Exp, scale=LN2, inc="aa")
                P.v(lambda h: h.tensor_mul(t_f, t_a, t_r), wait=[("aa", sv2)])
                P.v(lambda h: h.tensor_single_scalar(t_f, t_f, 0.5, ALU.add))
                P.v(lambda h: h.tensor_copy(t_c, t_f))
                for b in (16.0, 8.0, 4.0, 2.0, 1.0):
                    P.v(lambda h, b=b: h.tensor_single_scalar(t_r, t_f, b,
                                                              ALU.is_ge))
                    P.v(lambda h, b=b: h.scalar_tensor_tensor(t_f, t_r, -b, t_f,
                                                              ALU.mult, ALU.add))
                P.v(lambda h: h.tensor_sub(t_c, t_c, t_f))
                P.v(lambda h: h.scalar_tensor_tensor(t_c, t_c, iscol[:], t_s,
                                                     ALU.mult, ALU.mult))
                P.v(lambda h, a=x1: h.tensor_mul(a, t_c, t_g), inc="vv")
            qv_strip[k] = P.cnt["vv"]

        def back(k):
            xs = xs2[k % 2]
            xq = xq2[k % 2]
            avx_pb[k] = dict(avx_pb[k - 1])
            add_pb[k] = dict(add_pb[k - 1])
            for o in range(NCH):
                pb = o % 2
                for i in range(NCH):
                    P.mm(pall[:, 512 * pb:512 * pb + 512],
                         wT[:, 1024 * i + 128 * o:1024 * i + 128 * o + 128],
                         xq[:, 512 * i:512 * i + 512],
                         i == 0, i == NCH - 1,
                         wait=[("vv", qv_strip[k]),
                               ("aa", avx_pb[k][pb])] if i == 0 else (),
                         inc="mm" if i == NCH - 1 else None)
                mv = P.cnt["mm"]
                yo = yo2[pb]
                avx = P.act(yo, pall[:, 512 * pb:512 * pb + 512], AF.Identity,
                            scale=wscol[:],
                            bias=bias_sb[:, BIAS_IDX["L1_ffn_bias"] + o:
                                         BIAS_IDX["L1_ffn_bias"] + o + 1],
                            wait=[("mm", mv), ("gg", add_pb[k][pb])], inc="aa")
                avx_pb[k][pb] = avx
                P.op("gpsimd", lambda h, o=o, xs=xs, yo=yo: h.tensor_add(
                    xf[:, 512 * o:512 * o + 512], yo,
                    xs[:, 512 * o:512 * o + 512]),
                    wait=[("aa", avx)] + ([("mm", trans_last[k - 1])]
                                          if o == 0 else []))
                add_pb[k][pb] = P.cnt["gg"]
            adds_done = P.cnt["gg"]
            for tb in range(4):
                tmb = tm2[tb % 2]
                sch = "st" if tb % 2 == 0 else "st2"
                stw = st_ch[sch].get((k, tb - 2), st_ch[sch][-1])
                evs = []
                for ch in range(NCH):
                    bnk = 2 + ch % 2
                    mvt = P.trans(bank(bnk, 128),
                                  xf[:, 512 * ch + 128 * tb:
                                     512 * ch + 128 * tb + 128],
                                  ident[:],
                                  wait=[("gg", adds_done), (sch, stw),
                                        ("aa", tmev_bank[bnk])],
                                  inc="mm")
                    evv = P.act(tmb[:, 128 * ch:128 * ch + 128],
                                bank(bnk, 128), AF.Identity,
                                wait=[("mm", mvt), (sch, stw)], inc="aa")
                    evs.append(evv)
                    tmev_bank[bnk] = evv
                st_ch[sch][(k, tb)] = P.store(
                    out[512 * k + 128 * tb:512 * k + 128 * tb + 128, :],
                    tmb[:], wait=[("aa", evs[-1])], sem=sch)
            st_ch["st"][-1] = st_ch["st"][(k, 2)]
            st_ch["st2"][-1] = st_ch["st2"][(k, 3)]
            trans_last[k] = P.cnt["mm"]
            mm_strip[k] = P.cnt["mm"]

        for k in range(5):
            if k < 4:
                front(k)
            if k >= 1:
                back(k - 1)
        P.bar()

    chain = [
        lambda: phaseP1(0, xs_scr[0], 0),
        lambda: phaseP2_L0(),
        lambda: phaseP3(0, xs_scr[0], xs_scr[1], 1),
        lambda: phaseP4(xs_scr[1], xs_scr[2], 1, 2),
        lambda: phaseP1(1, xs_scr[2], 2),
        lambda: phaseP2(1),
        lambda: phaseP3(1, xs_scr[2], xs_scr[3], 3),
        lambda: phaseP6(xs_scr[3], 3),
    ]
    for i, ph in enumerate(chain):
        if i < upto:
            ph()

    P.raw("gpsimd", lambda h, v=P.cnt["st"]: h.wait_ge(sems["st"], v))
    P.raw("sync", lambda h, v=P.cnt["ld"]: h.wait_ge(sems["ld"], v))

    with es:
        block = es.enter_context(nc.Block())

        @block.sync
        def _(h):
            for f in P.q["sync"]:
                f(h)

        @block.gpsimd
        def _(h):
            for f in P.q["gpsimd"]:
                f(h)

        @block.vector
        def _(h):
            for f in P.q["vector"]:
                f(h)

        @block.scalar
        def _(h):
            for f in P.q["scalar"]:
                f(h)

        @block.tensor
        def _(h):
            for f in P.q["tensor"]:
                f(h)
    return nc


def make_consts():
    ident = np.eye(128, dtype=np.float32)
    alibi = np.zeros((2, 17 * T), dtype=np.float32)
    slopes = np.power(2.0, np.linspace(-4.0, -0.3, 16).astype(np.float64))
    k = np.arange(T, dtype=np.float64)
    for h in range(16):
        t = -8.0 * slopes[h] * k - 160.0
        hi = t.astype(np.float32)
        lo = (t - hi.astype(np.float64)).astype(np.float32)
        alibi[0, h * T:(h + 1) * T] = hi
        alibi[1, h * T:(h + 1) * T] = lo
    alibi[0, 16 * T:17 * T] = -160.0
    cmisc = np.zeros((128, 8), dtype=np.float32)
    cmisc[:, 0] = EPS
    cmisc[:, 1] = 12.0 * LN2
    cmisc[:, 2] = -12.0 * LN2
    cmisc[:, 3] = -20.0
    return {"c_ident": ident, "c_alibi": alibi, "c_misc": cmisc}


def kernel(**inputs):
    consts = make_consts()
    nc = build()
    x = np.ascontiguousarray(np.asarray(inputs["L0_attn_x"], dtype=np.float32))
    in_maps = []
    for c in range(8):
        m = {k: np.ascontiguousarray(np.asarray(v, dtype=np.float32))
             for k, v in inputs.items() if k != "L0_attn_x"}
        m["L0_attn_x"] = x[c]
        m.update(consts)
        in_maps.append(m)
    res = run_bass_kernel_spmd(nc, in_maps, core_ids=list(range(8)))
    return np.stack([np.asarray(r["out"]) for r in res.results], axis=0)
